# version 02
# baseline (speedup 1.0000x reference)
import numpy as np
from contextlib import ExitStack

import concourse.bass as bass
import concourse.mybir as mybir

F32 = mybir.dt.float32
BF16 = mybir.dt.bfloat16
AF = mybir.ActivationFunctionType
ALU = mybir.AluOpType
AX = mybir.AxisListType

MAGIC = 1.5 * 2.0 ** 23
ENGINES = ("sync", "gpsimd", "vector", "scalar", "tensor")


class SemT:
    __slots__ = ("name", "n", "kind")

    def __init__(self, name, kind):
        self.name = name
        self.n = 0
        self.kind = kind


class Op:
    __slots__ = ("eng", "fn", "incs", "waits")

    def __init__(self, eng, fn, incs, waits):
        self.eng = eng
        self.fn = fn
        self.incs = incs
        self.waits = waits


class Prog:
    def __init__(self, nc):
        self.nc = nc
        self.ops = []
        self.sems = {}

    def sem(self, name, kind="c"):
        s = self.sems.get(name)
        if s is None:
            s = self.sems[name] = SemT(name, kind)
        assert s.kind == kind, (name, s.kind, kind)
        return s

    def op(self, eng, fn, inc=(), wait=()):
        assert eng in ENGINES
        self.ops.append(Op(eng, fn, tuple(inc), tuple(wait)))
        for s, k in inc:
            s.n += k

    def raw(self, eng, fn, inc=(), wait=()):
        self.ops.append(Op(eng, fn, (), tuple(wait)))
        for s, k in inc:
            s.n += k

    def now(self, *sems):
        return tuple((s, s.n) for s in sems)

    def dma(self, queue, out_ap, in_ap, wait=(), accum=False):
        kind = "sw" if queue == "gpsimd" else "hw"
        s = self.sem("dma_" + queue, kind)

        def fn(e, out_ap=out_ap, in_ap=in_ap):
            if accum:
                return e.dma_start(out=out_ap, in_=in_ap, accum_op=mybir.AluOpType.add)
            return e.dma_start(out=out_ap, in_=in_ap)
        self.op(queue, fn, inc=[(s, 16)], wait=wait)
        return (s, s.n)

    def emit(self, ctx):
        nc = self.nc
        real = {}
        for name, s in self.sems.items():
            assert s.n < 60000, (name, s.n)
            real[name] = ctx.enter_context(nc.semaphore(name))
        self.real = real
        block = ctx.enter_context(nc.Block())
        prog = self

        def mk(eng_name):
            def run(e):
                for o in prog.ops:
                    if o.eng != eng_name:
                        continue
                    for s, tgt in o.waits:
                        if tgt > 0:
                            e.wait_ge(real[s.name], tgt)
                    inst = o.fn(e)
                    for s, k in o.incs:
                        inst.then_inc(real[s.name], k)
            return run

        for eng_name in ENGINES:
            getattr(block, eng_name)(mk(eng_name))
        return real



def T_(t):
    return t[:, :]


class Ctx:

    def __init__(self, nc, prog):
        self.nc = nc
        self.p = prog
        self.t = {}

    def sb(self, ctx, name, shape, dt=F32):
        if getattr(self, "ovl_on", False) and name not in getattr(self, "ovl_skip", ()):
            epe = {F32: 4, BF16: 2}.get(dt, 4)
            off = (self.ovl_cursor + 63) // 64 * 64
            nbytes = shape[1] * epe
            base = self.t["_ovl"]
            ap = base[:, :].bitcast(dt)
            h = ap[0:shape[0], off // epe:off // epe + shape[1]]
            self.ovl_cursor = off + nbytes
            assert self.ovl_cursor <= self.ovl_cap, (
                f"overlay overflow at {name}: {self.ovl_cursor}")
            self.t[name] = h
            return h
        h = ctx.enter_context(self.nc.sbuf_tensor(name, shape, dt))
        self.t[name] = h
        return h

    def ps(self, ctx, name, shape, dt=F32):
        h = ctx.enter_context(self.nc.psum_tensor(name, shape, dt))
        self.t[name] = h
        return h



def alloc_common(C, ctx, T):
    C.T = T
    C.sb(ctx, "x_res", [128, 16 * T], F32)
    C.sb(ctx, "xbf", [128, 16 * T], BF16)
    C.sb(ctx, "ybf", [128, 16 * T], BF16)
    C.sb(ctx, "sqA", [128, T], BF16)
    C.sb(ctx, "sqB", [128, T], BF16)
    C.sb(ctx, "maxsq", [128, T], BF16)
    C.sb(ctx, "tmp32e", [128, 512], F32)
    C.sb(ctx, "e1", [128, 512], F32)
    C.sb(ctx, "rows32a", [128, T], F32)
    C.sb(ctx, "rows32b", [128, T], F32)
    C.sb(ctx, "m_B", [128, T], BF16)
    C.sb(ctx, "rstd_B", [128, T], F32)
    C.sb(ctx, "m2_B", [128, T], BF16)
    C.sb(ctx, "r2_B", [128, T], F32)
    C.sb(ctx, "F_B", [128, T], F32)
    C.sb(ctx, "AS_B", [128, T], F32)
    C.sb(ctx, "tok8", [128, 64], F32)
    C.sb(ctx, "colmisc", [128, 8], F32)
    C.sb(ctx, "constcol", [128, 4], F32)
    C.t["tmp32b"] = C.t["wabs"] if "wabs" in C.t else None
    C.sb(ctx, "ones128_bf", [128, 128], BF16)
    C.sb(ctx, "ones128_f", [128, 128], F32)
    C.sb(ctx, "onesr_f", [1, 128], F32)
    C.sb(ctx, "ident_bf", [128, 128], BF16)
    C.sb(ctx, "ident_f", [128, 128], F32)
    for i in range(8):
        C.ps(ctx, f"p{i}", [128, 512], F32)


def barrier(C, tag=""):
    p = C.p
    b = p.sem("bar")
    for eng in ENGINES:
        if eng != "sync":
            p.op(eng, lambda e: e.drain())
        p.op(eng, lambda e: e.nop(), inc=[(b, 1)])
    tgt = b.n
    for eng in ENGINES:
        p.op(eng, lambda e: e.nop(), wait=[(b, tgt)])


def chunk(tile, c, T):
    return tile[:, c * T: (c + 1) * T]


ESEM = {"vector": "cv", "scalar": "cs", "tensor": "ct", "gpsimd": "cp"}


def flush(C, eng):
    p = C.p
    s = p.sem(ESEM[eng])
    p.op(eng, lambda e: e.drain(), wait=[(s, s.n)])
    p.op(eng, lambda e: e.nop(), inc=[(s, 1)])
    return s.n



def emit_ln_stats(C, src, nch, out_m, out_r, cast_src=None):
    p, t, T = C.p, C.t, C.T
    cv, cs, ct = p.sem("cv"), p.sem("cs"), p.sem("ct")
    ones = t["ones128_bf"]
    banks = [t["p0"], t["p1"], t["p2"], t["p3"]]
    NG = T // 512
    sq = [t["sqA"], t["sqB"]]
    mm_done = {}
    for c in range(nch):
        if cast_src is not None:
            p.op("vector", (lambda e, c=c: e.tensor_copy(src(c), cast_src(c))))
        w = [(ct, mm_done[c - 2])] if c >= 2 else []
        p.op("vector", (lambda e, c=c: e.tensor_mul(sq[c % 2][:, :], src(c), src(c))),
             inc=[(cv, 1)], wait=w)
        cvt = cv.n
        for g in range(NG):
            p.op("tensor", (lambda e, c=c, g=g: e.matmul(
                banks[g][:, :], ones[:, :], src(c)[:, 512 * g:512 * (g + 1)],
                start=(c == 0), stop=(c == nch - 1))),
                 wait=[(cv, cvt)] if g == 0 else ())
        for g in range(NG):
            p.op("tensor", (lambda e, c=c, g=g: e.matmul(
                banks[NG + g][:, :], ones[:, :], sq[c % 2][:, 512 * g:512 * (g + 1)],
                start=(c == 0), stop=(c == nch - 1))),
                 inc=[(ct, 1)] if g == NG - 1 else ())
        mm_done[c] = ct.n
    stats_done = ct.n
    for g in range(NG):
        sl = slice(512 * g, 512 * (g + 1))
        p.op("vector", (lambda e, g=g, sl=sl: e.tensor_scalar(
            out_m[:, sl], banks[g][:, :], 1.0 / (128 * nch), None, op0=ALU.mult)),
             wait=[(ct, stats_done)] if g == 0 else ())
        p.op("vector", (lambda e, g=g, sl=sl: e.tensor_scalar(
            t["rows32a"][:, sl], banks[NG + g][:, :], 1.0 / (128 * nch), None, op0=ALU.mult)))
        p.op("vector", (lambda e, sl=sl: e.tensor_mul(
            t["rows32b"][:, sl], out_m[:, sl], out_m[:, sl])))
        p.op("vector", (lambda e, sl=sl: e.tensor_sub(
            t["rows32a"][:, sl], t["rows32a"][:, sl], t["rows32b"][:, sl])),
             inc=[(cv, 1)])
        cvt = cv.n
        p.op("scalar", (lambda e, sl=sl: e.activation(
            t["rows32b"][:, sl], t["rows32a"][:, sl], AF.Sqrt, bias=t["constcol"][:, 0:1])),
             inc=[(cs, 1)], wait=[(cv, cvt)])
        cst = cs.n
        p.op("vector", (lambda e, sl=sl: e.reciprocal(out_r[:, sl], t["rows32b"][:, sl])),
             inc=[(cv, 1)], wait=[(cs, cst)])
    return cv.n



def alloc_l0ffn(C, ctx):
    C.sb(ctx, "wraw0", [128, 512], F32)
    C.sb(ctx, "wraw1", [128, 512], F32)
    C.sb(ctx, "wabs", [128, 512], F32)
    C.sb(ctx, "tw", [128, 16 * 1024], BF16)
    C.sb(ctx, "bias_cols", [128, 16], F32)


def emit_l0ffn(C, get_x, W0t_dram, nch=16):
    p, t, T = C.p, C.t, C.T
    cv, cs, ct, cp = p.sem("cv"), p.sem("cs"), p.sem("ct"), p.sem("cp")
    xbf, ybf = t["xbf"], t["ybf"]
    NG = T // 512
    JT = T // 128
    wraw = [t["wraw0"], t["wraw1"]]
    wabs, tw = t["wabs"], t["tw"]

    abs_done, mm_done = {}, {}
    for hc in range(4 * nch):
        c, q = hc // 4, hc % 4
        w = [(cs, abs_done[hc - 2])] if hc >= 2 else []
        ldt = p.dma("sync", wraw[hc % 2][:, :],
                    W0t_dram[128 * c:128 * (c + 1), 512 * q:512 * (q + 1)], wait=w)
        w2 = [ldt] + ([(ct, mm_done[hc - 1])] if hc >= 1 else [])
        p.op("scalar", (lambda e, hc=hc: e.activation(wabs[:, :], wraw[hc % 2][:, :], AF.Abs)),
             inc=[(cs, 1)], wait=w2)
        abs_done[hc] = cs.n
        cst = cs.n
        p.op("tensor", (lambda e, hc=hc: e.matmul(
            t["p5"][:, :], t["ones128_f"][:, :], wabs[:, :],
            start=(hc == 0), stop=(hc == 4 * nch - 1))),
             inc=[(ct, 1)], wait=[(cs, cst)])
        mm_done[hc] = ct.n
    alpha_mm = ct.n
    p.op("vector", (lambda e: e.tensor_copy(t["rows32a"][0:1, 0:512], t["p5"][0:1, :])),
         inc=[(cv, 1)], wait=[(ct, alpha_mm)])
    p.op("vector", (lambda e: e.tensor_reduce(
        t["rows32a"][0:1, 520:521], t["rows32a"][0:1, 0:512], axis=AX.X, op=ALU.add)),
         inc=[(cv, 1)], wait=[(cv, cv.n)])
    p.op("vector", (lambda e: e.tensor_scalar(
        t["rows32a"][0:1, 1:2], t["rows32a"][0:1, 520:521], 1.0 / (2048.0 * 2048.0), 1e-5,
        op0=ALU.mult, op1=ALU.max)), inc=[(cv, 1)], wait=[(cv, cv.n)])
    p.op("vector", (lambda e: e.reciprocal(t["rows32a"][0:1, 2:3], t["rows32a"][0:1, 1:2])),
         inc=[(cv, 1)], wait=[(cv, cv.n)])
    cvt = flush(C, "vector")
    p.op("tensor", (lambda e: e.matmul(t["p4"][:, 0:2], t["onesr_f"][:, :],
                                       t["rows32a"][0:1, 1:3], start=True, stop=True)),
         inc=[(ct, 1)], wait=[(cv, cvt)])
    ctt = ct.n
    p.op("vector", (lambda e: e.tensor_copy(t["colmisc"][:, 0:2], t["p4"][:, 0:2])),
         inc=[(cv, 1)], wait=[(ct, ctt)])

    barrier(C, "alpha")

    emit_ln_stats(C, lambda c: chunk(xbf, c, T), nch, t["m_B"], t["rstd_B"],
                  cast_src=get_x)
    for c in range(nch):
        p.op("vector", (lambda e, c=c: e.tensor_sub(chunk(xbf, c, T), chunk(xbf, c, T), t["m_B"][:, :])))
        p.op("vector", (lambda e, c=c: e.tensor_mul(t["sqA"][:, :], chunk(xbf, c, T), chunk(xbf, c, T))))
        if c == 0:
            p.op("vector", (lambda e: e.tensor_copy(t["maxsq"][:, :], t["sqA"][:, :])), inc=[(cv, 1)])
        else:
            p.op("vector", (lambda e: e.tensor_max(t["maxsq"][:, :], t["maxsq"][:, :], t["sqA"][:, :])),
                 inc=[(cv, 1)])
    ln_done = cv.n

    red_done = {}
    for j in range(JT):
        pb = t[f"p{6 + j % 2}"][:, :].bitcast(BF16)
        w = [(cv, ln_done)] if j < 2 else [(cv, red_done[j - 2])]
        p.op("tensor", (lambda e, j=j, pb=pb: e.transpose(
            pb[:, 0:128], t["maxsq"][:, 128 * j:128 * (j + 1)], t["ident_bf"][:, :])),
             inc=[(ct, 1)], wait=w)
        ctt = ct.n
        p.op("vector", (lambda e, j=j, pb=pb: e.tensor_reduce(
            t["tok8"][:, j:j + 1], pb[:, 0:128], axis=AX.X, op=ALU.max)),
             inc=[(cv, 1)], wait=[(ct, ctt)])
        red_done[j] = cv.n
    cp_done = {}
    for j in range(JT):
        pb = [t["p4"], t["p5"]][j % 2]
        w = [] if j < 2 else [(cv, cp_done[j - 2])]
        p.op("tensor", (lambda e, j=j, pb=pb: e.transpose(
            pb[:, 0:128], t["rstd_B"][:, 128 * j:128 * (j + 1)], t["ident_f"][:, :])),
             inc=[(ct, 1)], wait=w)
        ctt = ct.n
        p.op("vector", (lambda e, j=j, pb=pb: e.tensor_copy(
            t["tok8"][:, 8 + j:9 + j], pb[:, 0:1])),
             inc=[(cv, 1)], wait=[(ct, ctt)])
        cp_done[j] = cv.n
    tok_ready = flush(C, "vector")
    p.op("scalar", (lambda e: e.activation(t["tok8"][:, 16:16 + JT], t["tok8"][:, 0:JT], AF.Sqrt)),
         inc=[(cs, 1)], wait=[(cv, tok_ready)])
    cst = flush(C, "scalar")
    p.op("vector", (lambda e: e.tensor_mul(t["tok8"][:, 24:24 + JT], t["tok8"][:, 8:8 + JT],
                                           t["tok8"][:, 16:16 + JT])), inc=[(cv, 1)],
         wait=[(cs, cst), (cv, tok_ready)])
    p.op("vector", (lambda e: e.tensor_scalar(t["tok8"][:, 32:32 + JT], t["tok8"][:, 24:24 + JT],
                                              1e-5, 1.0 / 127.0, op0=ALU.max, op1=ALU.mult)),
         inc=[(cv, 1)], wait=[(cv, cv.n)])
    p.op("vector", (lambda e: e.reciprocal(t["tok8"][:, 40:40 + JT], t["tok8"][:, 32:32 + JT])),
         inc=[(cv, 1)], wait=[(cv, cv.n)])
    p.op("vector", (lambda e: e.tensor_mul(t["tok8"][:, 48:48 + JT], t["tok8"][:, 8:8 + JT],
                                           t["tok8"][:, 40:40 + JT])), inc=[(cv, 1)], wait=[(cv, cv.n)])
    p.op("vector", (lambda e: e.tensor_scalar(t["tok8"][:, 56:56 + JT], t["tok8"][:, 32:32 + JT],
                                              t["colmisc"][:, 0:1], None, op0=ALU.mult)),
         inc=[(cv, 1)], wait=[(cv, cv.n)])
    ftok_ready = flush(C, "vector")
    ev_done = {}
    for j in range(JT):
        pb = [t["p4"], t["p5"]][j % 2]
        w = [(cv, ftok_ready)] if j < 2 else [(cv, ev_done[j - 2])]
        p.op("tensor", (lambda e, j=j, pb=pb: e.matmul(
            pb[0:1, 0:128], t["tok8"][:, 48 + j:49 + j], t["ident_f"][:, :],
            start=True, stop=True)), wait=w)
        p.op("tensor", (lambda e, j=j, pb=pb: e.matmul(
            pb[0:1, 128:256], t["tok8"][:, 56 + j:57 + j], t["ident_f"][:, :],
            start=True, stop=True)), inc=[(ct, 1)])
        ctt = ct.n
        p.op("vector", (lambda e, j=j, pb=pb: e.tensor_copy(
            t["rows32a"][0:1, 128 * j:128 * (j + 1)], pb[0:1, 0:128])), wait=[(ct, ctt)])
        p.op("vector", (lambda e, j=j, pb=pb: e.tensor_copy(
            t["rows32b"][0:1, 128 * j:128 * (j + 1)], pb[0:1, 128:256])),
             inc=[(cv, 1)])
        ev_done[j] = cv.n
    rows_ready = flush(C, "vector")
    bc_done = {}
    for g in range(NG):
        pb = [t["p4"], t["p5"]][g % 2]
        w = [(cv, rows_ready)] if g < 2 else [(cv, bc_done[g - 2])]
        p.op("tensor", (lambda e, g=g, pb=pb: e.matmul(
            pb[:, :], t["onesr_f"][:, :], t["rows32a"][0:1, 512 * g:512 * (g + 1)],
            start=True, stop=True)), inc=[(ct, 1)], wait=w)
        ctt = ct.n
        p.op("vector", (lambda e, g=g, pb=pb: e.tensor_copy(
            t["F_B"][:, 512 * g:512 * (g + 1)], pb[:, :])), inc=[(cv, 1)], wait=[(ct, ctt)])
        bc_done[g] = cv.n
    for g in range(NG):
        pb = [t["p4"], t["p5"]][g % 2]
        w = [(cv, bc_done[g])] if g >= NG - 2 else [(cv, bc_done[g])]
        p.op("tensor", (lambda e, g=g, pb=pb: e.matmul(
            pb[:, :], t["onesr_f"][:, :], t["rows32b"][0:1, 512 * g:512 * (g + 1)],
            start=True, stop=True)), inc=[(ct, 1)], wait=w)
        ctt = ct.n
        p.op("vector", (lambda e, g=g, pb=pb: e.tensor_copy(
            t["AS_B"][:, 512 * g:512 * (g + 1)], pb[:, :])), inc=[(cv, 1)], wait=[(ct, ctt)])

    barrier(C, "scales")

    for c in range(nch):
        for g in range(NG):
            sl = slice(T * c + 512 * g, T * c + 512 * (g + 1))
            p.op("vector", (lambda e, sl=sl, g=g: e.tensor_mul(
                t["tmp32e"][:, :], xbf[:, sl], t["F_B"][:, 512 * g:512 * (g + 1)])))
            p.op("vector", (lambda e, sl=sl: e.tensor_scalar(
                xbf[:, sl], t["tmp32e"][:, :], MAGIC, MAGIC, op0=ALU.add, op1=ALU.subtract)),
                 inc=[(cv, 1)])
    qx_done = cv.n

    barrier(C, "qx")

    pbanks = [t["p0"], t["p1"], t["p2"], t["p3"]]
    evac_done = {}
    i = 0
    for po in range(2):
        q_done = {}
        for hc in range(2 * nch):
            c, q = hc // 2, hc % 2
            w = [(cv, q_done[hc - 2])] if hc >= 2 else []
            if po == 1 and hc == 0:
                w = w + [(ct, p.sem("ct").n)]
            ldt = p.dma("sync", wraw[hc % 2][:, :],
                        W0t_dram[128 * c:128 * (c + 1), 1024 * po + 512 * q:1024 * po + 512 * (q + 1)],
                        wait=w)
            p.op("vector", (lambda e, hc=hc: e.tensor_scalar(
                wabs[:, :], wraw[hc % 2][:, :], t["colmisc"][:, 1:2], MAGIC,
                op0=ALU.mult, op1=ALU.add)), wait=[ldt])
            p.op("vector", (lambda e: e.tensor_scalar(
                wabs[:, :], wabs[:, :], MAGIC, 1.0, op0=ALU.subtract, op1=ALU.min)))
            p.op("vector", (lambda e, c=c, q=q: e.tensor_scalar(
                tw[:, 1024 * c + 512 * q:1024 * c + 512 * (q + 1)], wabs[:, :], -1.0, None, op0=ALU.max)),
                 inc=[(cv, 1)])
            q_done[hc] = cv.n
        qw_done = cv.n
        for oo in range(8):
            o = 8 * po + oo
            for g in range(NG):
                pb = pbanks[i % 4]
                w = [(cv, max(qx_done, qw_done))] if i % 4 == i or i < 4 else []
                w = [(cv, max(qx_done, qw_done))]
                if i >= 4:
                    w.append((cv, evac_done[i - 4]))
                for c in range(nch):
                    p.op("tensor", (lambda e, c=c, oo=oo, g=g, pb=pb: e.matmul(
                        pb[:, :], tw[:, 1024 * c + 128 * oo:1024 * c + 128 * oo + 128],
                        xbf[:, T * c + 512 * g:T * c + 512 * (g + 1)],
                        start=(c == 0), stop=(c == nch - 1))),
                         inc=[(ct, 1)] if c == nch - 1 else (), wait=w if c == 0 else ())
                mmt = ct.n
                p.op("vector", (lambda e, g=g, pb=pb: e.tensor_mul(
                    t["tmp32e"][:, :], pb[:, :], t["AS_B"][:, 512 * g:512 * (g + 1)])),
                     wait=[(ct, mmt)])
                p.op("vector", (lambda e, o=o, g=g: e.tensor_scalar(
                    ybf[:, T * o + 512 * g:T * o + 512 * (g + 1)], t["tmp32e"][:, :],
                    t["bias_cols"][:, o:o + 1], None, op0=ALU.add)), inc=[(cv, 1)])
                evac_done[i] = cv.n
                i += 1

    barrier(C, "l0ffn_mm")

    stats_ready = emit_ln_stats(C, lambda c: chunk(ybf, c, T), nch, t["m2_B"], t["r2_B"])
    for c in range(nch):
        for g in range(NG):
            slx = slice(T * c + 512 * g, T * c + 512 * (g + 1))
            slg = slice(512 * g, 512 * (g + 1))
            if c % 2 == 0:
                p.op("vector", (lambda e, slx=slx, slg=slg: e.tensor_sub(
                    t["sqA"][:, 0:512], ybf[:, slx], t["m2_B"][:, slg])))
                p.op("vector", (lambda e, slg=slg: e.tensor_mul(
                    t["tmp32e"][:, :], t["sqA"][:, 0:512], t["r2_B"][:, slg])))
                p.op("vector", (lambda e, slx=slx: e.tensor_add(
                    t["x_res"][:, slx], t["x_res"][:, slx], t["tmp32e"][:, :])),
                     inc=[(cv, 1)])
            else:
                p.op("gpsimd", (lambda e, slx=slx, slg=slg: e.tensor_sub(
                    t["sqB"][:, 0:512], ybf[:, slx], t["m2_B"][:, slg])),
                     wait=[(cv, stats_ready)] if (c == 1 and g == 0) else ())
                p.op("gpsimd", (lambda e, slg=slg: e.tensor_mul(
                    t["wabs"][:, :], t["sqB"][:, 0:512], t["r2_B"][:, slg])))
                p.op("gpsimd", (lambda e, slx=slx: e.tensor_add(
                    t["x_res"][:, slx], t["x_res"][:, slx], t["wabs"][:, :])),
                     inc=[(cp, 1)])
    barrier(C, "l0ffn_done")



GELU_A = 1.5957691216057308
GELU_B = GELU_A * 0.044715


def alloc_l1ffn(C, ctx):
    C.sb(ctx, "w1buf0", [128, 2048], BF16)
    C.sb(ctx, "w1buf1", [128, 2048], BF16)
    C.sb(ctx, "w2buf0", [128, 4096], BF16)
    C.sb(ctx, "w2buf1", [128, 4096], BF16)
    for i in range(2):
        C.sb(ctx, f"zbf{i}", [128, 512], BF16)
        C.sb(ctx, f"sqz{i}", [128, 512], BF16)
        C.sb(ctx, f"tbf{i}", [128, 512], BF16)
        C.sb(ctx, f"ubf{i}", [128, 512], BF16)
        C.sb(ctx, f"gbf{i}", [128, 512], BF16)


def emit_l1ffn(C, W1s_dram, W2s_dram, x_out_dram, nch=16):
    p, t, T = C.p, C.t, C.T
    cv, cs, ct, cp = p.sem("cv"), p.sem("cs"), p.sem("ct"), p.sem("cp")
    xbf, ybf = t["xbf"], t["ybf"]
    a_ap = t["x_res"][:, :].bitcast(BF16)

    emit_ln_stats(C, lambda c: chunk(xbf, c, T), nch, t["m_B"], t["rstd_B"],
                  cast_src=lambda c: chunk(t["x_res"], c, T))
    for c in range(nch):
        p.op("vector", (lambda e, c=c: e.tensor_sub(chunk(xbf, c, T), chunk(xbf, c, T), t["m_B"][:, :])))
        p.op("vector", (lambda e, c=c: e.tensor_mul(chunk(xbf, c, T), chunk(xbf, c, T), t["rstd_B"][:, :])),
             inc=[(cv, 1)] if c == nch - 1 else ())
    xln_done = cv.n
    sp = None
    for c in range(nch):
        sp = p.dma("sync", x_out_dram[128 * c:128 * (c + 1), :], chunk(t["x_res"], c, T))
    spill_done = sp

    barrier(C, "l1ffn_ln")

    hb = [t["p0"], t["p1"]]
    yb = [t["p2"], t["p3"]]
    NHO = 64
    a_done = 0
    for th in range(2):
        if th == 0:
            h_mm_done = {}
            chain_done = {}
            pool_done = {}
        for ho in range(NHO):
            wb = [t["w1buf0"], t["w1buf1"]][ho % 2]
            w = [(ct, h_mm_done[(th, ho - 2)])] if (th, ho - 2) in h_mm_done else \
                ([(ct, h_mm_done[(0, NHO - 1)])] if th == 1 and ho < 2 else [])
            ldt = p.dma("gpsimd", wb[:, :], W1s_dram[ho], wait=w)
            pb = hb[ho % 2]
            w2 = [ldt]
            if (th, ho - 2) in chain_done:
                w2.append((cv, chain_done[(th, ho - 2)]))
            elif th == 1 and ho < 2:
                w2.append((cv, chain_done[(0, NHO - 2 + ho)]))
            if th == 0 and ho == 0:
                w2.append((cv, xln_done))
            for c in range(nch):
                xsl = slice(T * c + 512 * th, T * c + 512 * th + 512)
                p.op("tensor", (lambda e, c=c, pb=pb, wb=wb, xsl=xsl: e.matmul(
                    pb[:, :], wb[:, 128 * c:128 * (c + 1)], xbf[:, xsl],
                    start=(c == 0), stop=(c == nch - 1))),
                     inc=[(ct, 1)] if c == nch - 1 else (), wait=w2 if c == 0 else ())
            h_mm_done[(th, ho)] = ct.n
            mmt = ct.n
            i2 = ho % 2
            zb, sqz, tb, ub, gb = t[f"zbf{i2}"], t[f"sqz{i2}"], t[f"tbf{i2}"], t[f"ubf{i2}"], t[f"gbf{i2}"]
            wA = [(ct, mmt)]
            if th == 0 and ho == 0:
                wA.append(spill_done)
            p.op("vector", (lambda e, pb=pb: e.tensor_scalar(
                t["e1"][:, :], pb[:, :], 10.0, MAGIC, op0=ALU.mult, op1=ALU.add)), wait=wA)
            p.op("vector", (lambda e, zb=zb: e.tensor_scalar(
                zb[:, :], t["e1"][:, :], MAGIC, 0.1, op0=ALU.subtract, op1=ALU.mult)),
                 inc=[(cv, 1)])
            cvt = cv.n
            wsq = [(cv, cvt)]
            if (th, ho - 2) in pool_done:
                wsq.append((cp, pool_done[(th, ho - 2)]))
            elif th == 1 and ho < 2 and (0, NHO - 2 + ho) in pool_done:
                wsq.append((cp, pool_done[(0, NHO - 2 + ho)]))
            p.op("scalar", (lambda e, zb=zb, sqz=sqz: e.activation(sqz[:, :], zb[:, :], AF.Square)),
                 inc=[(cs, 1)], wait=wsq)
            cst = cs.n
            p.op("gpsimd", (lambda e, sqz=sqz, tb=tb: e.tensor_scalar(
                tb[:, :], sqz[:, :], GELU_B, GELU_A, op0=ALU.mult, op1=ALU.add)),
                 inc=[(cp, 1)], wait=[(cs, cst)])
            cpt = cp.n
            pool_done[(th, ho)] = cp.n
            p.op("vector", (lambda e, tb=tb, ub=ub, zb=zb: e.tensor_mul(ub[:, :], tb[:, :], zb[:, :])),
                 inc=[(cv, 1)], wait=[(cp, cpt)])
            cvt2 = cv.n
            p.op("scalar", (lambda e, ub=ub, gb=gb: e.activation(gb[:, :], ub[:, :], AF.Sigmoid)),
                 inc=[(cs, 1)], wait=[(cv, cvt2)])
            cst2 = cs.n
            p.op("vector", (lambda e, gb=gb, zb=zb, ho=ho, th=th: e.tensor_mul(
                a_ap[:, 512 * ho:512 * (ho + 1)], gb[:, :], zb[:, :])),
                 inc=[(cv, 1)], wait=[(cs, cst2)])
            chain_done[(th, ho)] = cv.n
        a_done = cv.n

        ev_done = {}
        for oc in range(nch):
            pb = yb[oc % 2]
            for kb in range(2):
                wbuf = [t["w2buf0"], t["w2buf1"]][kb]
                wld = []
                if oc >= 1:
                    wld.append((ct, p.sem("ct").n))
                ldt = p.dma("gpsimd", wbuf[:, :], W2s_dram[oc, kb], wait=wld)
                wmm = [ldt, (cv, a_done)]
                if (oc - 2) in ev_done:
                    wmm.append((cv, ev_done[oc - 2]))
                for hh in range(32):
                    ho = 32 * kb + hh
                    p.op("tensor", (lambda e, hh=hh, ho=ho, pb=pb, wbuf=wbuf: e.matmul(
                        pb[:, :], wbuf[:, 128 * hh:128 * (hh + 1)],
                        a_ap[:, 512 * ho:512 * (ho + 1)],
                        start=(ho == 0), stop=(ho == NHO - 1))),
                         inc=[(ct, 1)] if ho == NHO - 1 else (), wait=wmm if hh == 0 else ())
            mmt = ct.n
            p.op("vector", (lambda e, oc=oc, pb=pb, th=th: e.tensor_copy(
                ybf[:, T * oc + 512 * th:T * oc + 512 * th + 512], pb[:, :])),
                 inc=[(cv, 1)], wait=[(ct, mmt)])
            ev_done[oc] = cv.n

    barrier(C, "l1ffn_mm")

    stats_ready = emit_ln_stats(C, lambda c: chunk(ybf, c, T), nch, t["m2_B"], t["r2_B"])
    acc_tgt = {}
    spill_sync = spill_done
    for c in range(nch):
        for g in range(T // 512):
            slx = slice(T * c + 512 * g, T * c + 512 * (g + 1))
            slg = slice(512 * g, 512 * (g + 1))
            i2 = (2 * c + g) % 2
            stg = t["tmp32e"] if i2 == 0 else t["e1"]
            sname = p.sem("dma_gpsimd", "sw")
            wv = []
            if (2 * c + g) >= 2 and "acc_tgt" in locals() and acc_tgt.get((2 * c + g) - 2):
                wv.append((sname, acc_tgt[(2 * c + g) - 2]))
            p.op("vector", (lambda e, slx=slx, slg=slg: e.tensor_sub(
                t["sqA"][:, 0:512], ybf[:, slx], t["m2_B"][:, slg])), wait=wv)
            p.op("vector", (lambda e, slg=slg, stg=stg: e.tensor_mul(
                stg[:, :], t["sqA"][:, 0:512], t["r2_B"][:, slg])),
                 inc=[(cv, 1)])
            cvt = cv.n
            tgt = p.dma("gpsimd", x_out_dram[128 * c:128 * (c + 1), 512 * g:512 * (g + 1)],
                        stg[:, :], wait=[(cv, cvt)] + ([spill_sync] if (c == 0 and g == 0) else []),
                        accum=True)
            acc_tgt[(2 * c + g)] = tgt[1]
    barrier(C, "l1ffn_done")



def alloc_l1attn(C, ctx):
    C.sb(ctx, "kfull", [128, 2048], BF16)
    C.sb(ctx, "vfull", [128, 2048], BF16)
    C.sb(ctx, "PT0", [128, 1024], BF16)
    C.sb(ctx, "PT1", [128, 1024], BF16)
    C.sb(ctx, "wbq0", [128, 2048], BF16)
    C.sb(ctx, "wbq1", [128, 2048], BF16)
    C.sb(ctx, "wkv", [128, 4096], BF16)
    C.sb(ctx, "biasq", [128, 16], F32)
    C.sb(ctx, "biaskv", [128, 2], F32)
    C.sb(ctx, "biaso", [128, 16], F32)
    C.sb(ctx, "negb", [128, 16], F32)
    C.sb(ctx, "sroleA", [1, 2], mybir.dt.int32)
    C.sb(ctx, "sflagA", [1, 8], mybir.dt.int32)
    C.sb(ctx, "spollA", [1, 8], mybir.dt.int32)


def emit_l1attn(C, wqs_dram, wos_dram, role_dram, sh_kv, sh_flag, nch=16, tag=0, dbg=None):
    p, t, T = C.p, C.t, C.T
    cv, cs, ct, cp = p.sem("cv"), p.sem("cs"), p.sem("ct"), p.sem("cp")
    xbf, ybf = t["xbf"], t["ybf"]
    qbf = ybf
    obf = xbf
    vT = t["AS_B"][:, :].bitcast(BF16)
    linvB = t["F_B"]
    MAGICF = 1234567.0

    emit_ln_stats(C, lambda c: chunk(xbf, c, T), nch, t["m_B"], t["rstd_B"],
                  cast_src=lambda c: chunk(t["x_res"], c, T))
    for c in range(nch):
        p.op("vector", (lambda e, c=c: e.tensor_sub(chunk(xbf, c, T), chunk(xbf, c, T), t["m_B"][:, :])))
        p.op("vector", (lambda e, c=c: e.tensor_mul(chunk(xbf, c, T), chunk(xbf, c, T), t["rstd_B"][:, :])),
             inc=[(cv, 1)] if c == nch - 1 else ())
    xln_done = cv.n

    barrier(C, f"l1a_ln{tag}")

    pb4 = [t["p0"], t["p1"], t["p2"], t["p3"]]
    ev_done = {}
    mm_done = {}
    for o in range(nch):
        wb = [t["wbq0"], t["wbq1"]][o % 2]
        w = [(ct, mm_done[o - 2])] if o >= 2 else []
        ldt = p.dma("gpsimd", wb[:, :], wqs_dram[o], wait=w)
        for g in range(2):
            pb = pb4[(2 * o + g) % 4]
            wmm = [ldt]
            if (2 * o + g) >= 4:
                wmm.append((cv, ev_done[2 * o + g - 4]))
            for c in range(nch):
                p.op("tensor", (lambda e, c=c, o=o, g=g, pb=pb, wb=wb: e.matmul(
                    pb[:, :], wb[:, 128 * c:128 * (c + 1)],
                    xbf[:, T * c + 512 * g:T * c + 512 * (g + 1)],
                    start=(c == 0), stop=(c == nch - 1))),
                     inc=[(ct, 1)] if c == nch - 1 else (), wait=wmm if c == 0 else ())
            mmt = ct.n
            p.op("vector", (lambda e, o=o, g=g, pb=pb: e.tensor_scalar(
                qbf[:, 1024 * o + 512 * g:1024 * o + 512 * (g + 1)], pb[:, :],
                t["biasq"][:, o:o + 1], None, op0=ALU.add)),
                 inc=[(cv, 1)], wait=[(ct, mmt)])
            ev_done[2 * o + g] = cv.n
        mm_done[o] = ct.n
    kv_ev = {}
    for kv in range(2):
        dst = [t["kfull"], t["vfull"]][kv]
        for g in range(2):
            pb = pb4[(kv * 2 + g) % 4]
            wmm = [(cv, ev_done[28 + kv * 2 + g])] if False else []
            wmm = [(cv, list(ev_done.values())[-1])]
            for c in range(nch):
                p.op("tensor", (lambda e, c=c, kv=kv, g=g, pb=pb: e.matmul(
                    pb[:, :], t["wkv"][:, 2048 * kv + 128 * c:2048 * kv + 128 * (c + 1)],
                    xbf[:, T * c + 512 * g:T * c + 512 * (g + 1)],
                    start=(c == 0), stop=(c == nch - 1))),
                     inc=[(ct, 1)] if c == nch - 1 else (), wait=wmm if c == 0 else ())
            mmt = ct.n
            dstt = [t["PT0"], t["PT1"]][kv]
            p.op("vector", (lambda e, kv=kv, g=g, pb=pb, dstt=dstt: e.tensor_scalar(
                dstt[:, 512 * g:512 * (g + 1)], pb[:, :],
                t["biaskv"][:, kv:kv + 1], None, op0=ALU.add)),
                 inc=[(cv, 1)], wait=[(ct, mmt)])
            kv_ev[(kv, g)] = cv.n
    kv_done = cv.n

    ex = p.sem("exch", "hw")
    exd = p.sem("exdone")

    def exchange(e):
        e.wait_ge(p.real["cv"], kv_done)
        with e.register(f"cnt{tag}") as cnt, e.register(f"rrole{tag}") as rrole, \
             e.register(f"rflag{tag}") as rflag, e.register(f"rcond{tag}") as rcond:
            e.reg_mov(cnt, 0)
            e.reg_load(rrole, t["sroleA"][0:1, 0:1])
            for myrow in (0, 1):
                other = 1 - myrow
                with e.If_eq(rrole, myrow):
                    e.dma_start(out=sh_kv[myrow, 0:128, :], in_=t["PT0"][:, :]).then_inc(p.real["exch"], 16)
                    e.dma_start(out=sh_kv[myrow, 128:256, :], in_=t["PT1"][:, :]).then_inc(p.real["exch"], 16)
                    e.dma_start(out=t["kfull"][:, 1024 * myrow:1024 * (myrow + 1)], in_=t["PT0"][:, :]).then_inc(p.real["exch"], 16)
                    e.dma_start(out=t["vfull"][:, 1024 * myrow:1024 * (myrow + 1)], in_=t["PT1"][:, :]).then_inc(p.real["exch"], 16)
                    e.reg_add(cnt, cnt, 64)
                    e.wait_ge(p.real["exch"], cnt)
                    e.dma_start(out=sh_flag[myrow:myrow + 1, :], in_=t["sflagA"][:, :]).then_inc(p.real["exch"], 16)
                    e.reg_add(cnt, cnt, 16)
                    e.wait_ge(p.real["exch"], cnt)

                    def cond(other=other, cnt=cnt, rflag=rflag, rcond=rcond, e=e):
                        e.dma_start(out=t["spollA"][:, :], in_=sh_flag[other:other + 1, :]).then_inc(p.real["exch"], 16)
                        e.reg_add(cnt, cnt, 16)
                        e.wait_ge(p.real["exch"], cnt)
                        e.reg_load(rflag, t["spollA"][0:1, 0:1])
                        e.reg_alu(rcond, rflag, tag + 7, mybir.AluOpType.not_equal)
                        return rcond
                    with e.While(cond):
                        e.nop()
                    e.dma_start(out=t["kfull"][:, 1024 * other:1024 * (other + 1)],
                                in_=sh_kv[other, 0:128, :]).then_inc(p.real["exch"], 16)
                    e.dma_start(out=t["vfull"][:, 1024 * other:1024 * (other + 1)],
                                in_=sh_kv[other, 128:256, :]).then_inc(p.real["exch"], 16)
                    e.reg_add(cnt, cnt, 32)
                    e.wait_ge(p.real["exch"], cnt)
        e.nop().then_inc(p.real["exdone"], 1)

    p.raw("sync", exchange, inc=[(ex, 0), (exd, 1)])
    exd_t = exd.n

    barrier(C, f"l1a_ex{tag}")

    vt_done = {}
    for kt in range(16):
        pbt = [t["p4"], t["p5"]][kt % 2]
        pbb = pbt[:, :].bitcast(BF16)
        w = [(p.sem("exdone"), exd_t)] if kt < 2 else [(cv, vt_done[kt - 2])]
        p.op("tensor", (lambda e, kt=kt, pbb=pbb: e.transpose(
            pbb[:, 0:128], t["vfull"][:, 128 * kt:128 * (kt + 1)], t["ident_bf"][:, :])),
             inc=[(ct, 1)], wait=w)
        ctt = ct.n
        p.op("vector", (lambda e, kt=kt, pbb=pbb: e.tensor_copy(
            vT[:, 128 * kt:128 * (kt + 1)], pbb[:, 0:128])),
             inc=[(cv, 1)], wait=[(ct, ctt)])
        vt_done[kt] = cv.n
    vt_all = cv.n

    R = t["rows32a"]
    km_done = {}
    for g in range(4):
        w = [(p.sem("exdone"), exd_t)] if g == 0 else [(cv, km_done[g - 1])]
        p.op("scalar", (lambda e, g=g: e.activation(
            t["sqA"][:, 0:512], t["kfull"][:, 512 * g:512 * (g + 1)], AF.Square)),
             inc=[(cs, 1)], wait=w + [(cp, 0)])
        cst = cs.n
        p.op("tensor", (lambda e: e.matmul(
            t["p6"][0:1, :], t["ones128_bf"][:, 0:1], t["sqA"][:, 0:512],
            start=True, stop=True)), inc=[(ct, 1)], wait=[(cs, cst)])
        ctt = ct.n
        p.op("vector", (lambda e, g=g: e.tensor_reduce(
            R[0:1, 700 + g:701 + g], t["p6"][0:1, :], axis=AX.X, op=ALU.max)),
             inc=[(cv, 1)], wait=[(ct, ctt)])
        km_done[g] = cv.n
    p.op("vector", (lambda e: e.tensor_reduce(
        R[0:1, 710:711], R[0:1, 700:704], axis=AX.X, op=ALU.max)),
         inc=[(cv, 1)], wait=[(cv, cv.n)])
    kmax_t = flush(C, "vector")
    for h in range(16):
        w0 = [(cv, kmax_t)] if h == 0 else [(cv, p.sem("cv").n)]
        p.op("scalar", (lambda e, h=h: e.activation(
            t["sqA"][:, 0:1024], qbf[:, 1024 * h:1024 * (h + 1)], AF.Square)),
             inc=[(cs, 1)], wait=w0)
        cst = cs.n
        p.op("tensor", (lambda e: e.matmul(
            t["p6"][0:1, :], t["ones128_bf"][:, 0:1], t["sqA"][:, 0:512],
            start=True, stop=True)), wait=[(cs, cst)])
        p.op("tensor", (lambda e: e.matmul(
            t["p7"][0:1, :], t["ones128_bf"][:, 0:1], t["sqA"][:, 512:1024],
            start=True, stop=True)), inc=[(ct, 1)])
        ctt = ct.n
        p.op("vector", (lambda e, h=h: e.tensor_reduce(
            R[0:1, 712:713], t["p6"][0:1, :], axis=AX.X, op=ALU.max)),
             inc=[(cv, 1)], wait=[(ct, ctt)])
        p.op("vector", (lambda e, h=h: e.tensor_reduce(
            R[0:1, 713:714], t["p7"][0:1, :], axis=AX.X, op=ALU.max)),
             inc=[(cv, 1)], wait=[(cv, cv.n)])
        p.op("vector", (lambda e: e.tensor_max(
            R[0:1, 714:715], R[0:1, 712:713], R[0:1, 713:714])),
             inc=[(cv, 1)], wait=[(cv, cv.n)])
        p.op("vector", (lambda e: e.tensor_mul(
            R[0:1, 715:716], R[0:1, 714:715], R[0:1, 710:711])),
             inc=[(cv, 1)], wait=[(cv, cv.n)])
        cvt = flush(C, "vector")
        p.op("scalar", (lambda e: e.activation(
            R[0:1, 716:717], R[0:1, 715:716], AF.Sqrt)), inc=[(cs, 1)], wait=[(cv, cvt)])
        cst = flush(C, "scalar")
        p.op("vector", (lambda e: e.tensor_scalar(
            R[0:1, 717:718], R[0:1, 716:717], -1.0 / (128.0 ** 0.5), None, op0=ALU.mult)),
             inc=[(cv, 1)], wait=[(cs, cst)])
        cvt = flush(C, "vector")
        p.op("tensor", (lambda e: e.matmul(
            t["p6"][:, 0:1], t["onesr_f"][:, :], R[0:1, 717:718],
            start=True, stop=True)), inc=[(ct, 1)], wait=[(cv, cvt)])
        ctt = ct.n
        p.op("vector", (lambda e, h=h: e.tensor_copy(t["negb"][:, h:h + 1], t["p6"][:, 0:1])),
             inc=[(cv, 1)], wait=[(ct, ctt)])
    bounds_done = cv.n

    if dbg is not None:
        dbg("post_bounds")
    barrier(C, f"l1a_bounds{tag}")

    spair = [(t["p0"], t["p1"]), (t["p2"], t["p3"])]
    PT = [t["PT0"], t["PT1"]]
    for h in range(16):
        exp_done = {}
        luse_done = {}
        for kt in range(16):
            sp_ = spair[kt % 2]
            for g in range(2):
                w = []
                if kt >= 2:
                    w.append((cs, exp_done[kt - 2]))
                if h > 0 and kt < 2:
                    w.append((cs, h_exp_last))
                if h > 0 and kt == 0 and g == 0:
                    w.append((cv, h_prev_obf))
                p.op("tensor", (lambda e, h=h, kt=kt, g=g, sp_=sp_: e.matmul(
                    sp_[g][:, :], t["kfull"][:, 128 * kt:128 * (kt + 1)],
                    qbf[:, 1024 * h + 512 * g:1024 * h + 512 * (g + 1)],
                    start=True, stop=True)),
                     inc=[(ct, 1)] if g == 1 else (), wait=w if g == 0 else ())
            mmt = ct.n
            wexp = [(ct, mmt)]
            if kt >= 2:
                wexp.append((ct, luse_done[kt - 2]))
            for g in range(2):
                p.op("scalar", (lambda e, h=h, kt=kt, g=g, sp_=sp_: e.activation(
                    PT[kt % 2][:, 512 * g:512 * (g + 1)], sp_[g][:, :], AF.Exp,
                    bias=t["negb"][:, h:h + 1], scale=1.0 / (128.0 ** 0.5))),
                     inc=[(cs, 1)] if g == 1 else (), wait=wexp if g == 0 else ())
            exp_done[kt] = cs.n
            cst = cs.n
            for g in range(2):
                p.op("tensor", (lambda e, kt=kt, g=g: e.matmul(
                    [t["p6"], t["p7"]][g][0:1, :], t["ones128_bf"][:, 0:1],
                    PT[kt % 2][:, 512 * g:512 * (g + 1)],
                    start=(kt == 0), stop=(kt == 15))), wait=[(cs, cst)] if g == 0 else ())
            for g in range(2):
                p.op("tensor", (lambda e, kt=kt, g=g: e.matmul(
                    [t["p4"], t["p5"]][g][:, :], vT[:, 128 * kt:128 * (kt + 1)],
                    PT[kt % 2][:, 512 * g:512 * (g + 1)],
                    start=(kt == 0), stop=(kt == 15))),
                     inc=[(ct, 1)] if g == 1 else ())
            p.op("tensor", (lambda e: e.nop()), inc=[(ct, 1)])
            luse_done[kt] = ct.n
        h_exp_last = exp_done[15]
        avl_done = ct.n
        p.op("vector", (lambda e: e.reciprocal(R[0:1, 0:512], t["p6"][0:1, :])),
             inc=[(cv, 1)], wait=[(ct, avl_done)])
        p.op("vector", (lambda e: e.reciprocal(R[0:1, 512:1024], t["p7"][0:1, :])),
             inc=[(cv, 1)], wait=[(cv, cv.n)])
        cvt = flush(C, "vector")
        for g in range(2):
            p.op("tensor", (lambda e, g=g: e.matmul(
                [t["p0"], t["p1"]][g][:, :], t["onesr_f"][:, :], R[0:1, 512 * g:512 * (g + 1)],
                start=True, stop=True)), inc=[(ct, 1)] if g == 1 else (), wait=[(cv, cvt)] if g == 0 else ())
        ctt = ct.n
        for g in range(2):
            p.op("vector", (lambda e, g=g: e.tensor_copy(
                linvB[:, 512 * g:512 * (g + 1)], [t["p0"], t["p1"]][g][:, :])),
                 inc=[(cv, 1)], wait=[(ct, ctt)] if g == 0 else ())
        cvt = cv.n
        for g in range(2):
            p.op("vector", (lambda e, h=h, g=g: e.tensor_mul(
                obf[:, 1024 * h + 512 * g:1024 * h + 512 * (g + 1)],
                [t["p4"], t["p5"]][g][:, :], linvB[:, 512 * g:512 * (g + 1)])),
                 inc=[(cv, 1)], wait=[(cv, cvt)] if g == 0 else ())
        h_prev_obf = cv.n

    if dbg is not None:
        dbg("post_heads")
    barrier(C, f"l1a_heads{tag}")

    ev_done = {}
    mm_done = {}
    for o in range(nch):
        wb = [t["wbq0"], t["wbq1"]][o % 2]
        w = [(ct, mm_done[o - 2])] if o >= 2 else []
        ldt = p.dma("gpsimd", wb[:, :], wos_dram[o], wait=w)
        for g in range(2):
            pb = pb4[(2 * o + g) % 4]
            wmm = [ldt]
            if (2 * o + g) >= 4:
                wmm.append((cv, ev_done[2 * o + g - 4]))
            for c in range(nch):
                p.op("tensor", (lambda e, c=c, o=o, g=g, pb=pb, wb=wb: e.matmul(
                    pb[:, :], wb[:, 128 * c:128 * (c + 1)],
                    obf[:, T * c + 512 * g:T * c + 512 * (g + 1)],
                    start=(c == 0), stop=(c == nch - 1))),
                     inc=[(ct, 1)] if c == nch - 1 else (), wait=wmm if c == 0 else ())
            mmt = ct.n
            p.op("vector", (lambda e, o=o, g=g, pb=pb: e.tensor_scalar(
                ybf[:, T * o + 512 * g:T * o + 512 * (g + 1)], pb[:, :],
                t["biaso"][:, o:o + 1], None, op0=ALU.add)),
                 inc=[(cv, 1)], wait=[(ct, mmt)])
            ev_done[2 * o + g] = cv.n
        mm_done[o] = ct.n

    barrier(C, f"l1a_wo{tag}")

    stats_ready = emit_ln_stats(C, lambda c: chunk(ybf, c, T), nch, t["m2_B"], t["r2_B"])
    for c in range(nch):
        for g in range(T // 512):
            slx = slice(T * c + 512 * g, T * c + 512 * (g + 1))
            slg = slice(512 * g, 512 * (g + 1))
            if c % 2 == 0:
                p.op("vector", (lambda e, slx=slx, slg=slg: e.tensor_sub(
                    t["sqA"][:, 0:512], ybf[:, slx], t["m2_B"][:, slg])))
                p.op("vector", (lambda e, slg=slg: e.tensor_mul(
                    t["tmp32e"][:, :], t["sqA"][:, 0:512], t["r2_B"][:, slg])))
                p.op("vector", (lambda e, slx=slx: e.tensor_add(
                    t["x_res"][:, slx], t["x_res"][:, slx], t["tmp32e"][:, :])),
                     inc=[(cv, 1)])
            else:
                p.op("gpsimd", (lambda e, slx=slx, slg=slg: e.tensor_sub(
                    t["sqB"][:, 0:512], ybf[:, slx], t["m2_B"][:, slg])),
                     wait=[(cv, stats_ready)] if (c == 1 and g == 0) else ())
                p.op("gpsimd", (lambda e, slg=slg: e.tensor_mul(
                    t["e1"][:, :], t["sqB"][:, 0:512], t["r2_B"][:, slg])))
                p.op("gpsimd", (lambda e, slx=slx: e.tensor_add(
                    t["x_res"][:, slx], t["x_res"][:, slx], t["e1"][:, :])),
                     inc=[(cp, 1)])
    barrier(C, f"l1a_done{tag}")


RT128 = 1.0 / (128.0 ** 0.5)


def alloc_l0attn(C, ctx):
    C.sb(ctx, "wsA", [128, 2048], BF16)
    C.sb(ctx, "wsB", [128, 2048], BF16)
    C.sb(ctx, "wsmall", [128, 512], F32)
    C.sb(ctx, "wsmallb", [128, 256], BF16)
    C.sb(ctx, "convw", [128, 160], F32)
    C.sb(ctx, "convext", [128, 1027], BF16)
    C.sb(ctx, "haloloc", [128, 128], BF16)
    C.sb(ctx, "halorem", [128, 128], BF16)
    C.sb(ctx, "ggtmp", [128, 1024], BF16)
    C.sb(ctx, "scanprep", [128, 1792], F32)
    C.sb(ctx, "acol", [128, 8], F32)
    C.sb(ctx, "onw", [128, 1], F32)
    C.sb(ctx, "cumsL", [128, 64], F32)
    C.sb(ctx, "dupB", [64, 128], F32)
    C.sb(ctx, "dupE", [128, 128], F32)
    C.sb(ctx, "dupO", [128, 128], F32)
    C.sb(ctx, "sel63", [64, 1], F32)
    C.sb(ctx, "m1c4", [128, 512], BF16)
    C.sb(ctx, "m2c", [128, 64], BF16)
    C.sb(ctx, "sroleB", [1, 2], mybir.dt.int32)
    C.sb(ctx, "sflagB", [1, 8], mybir.dt.int32)
    C.sb(ctx, "spollB", [1, 8], mybir.dt.int32)


def emit_l0attn_pre(C, Wqs, Wks, Wvs, Wgs, wsmall_d, convw_d, acol_d,
                    x_work, gg_work, sh_halo, sh_flag, nch=16):
    p, t, T = C.p, C.t, C.T
    cv, cs, ct, cp = p.sem("cv"), p.sem("cs"), p.sem("ct"), p.sem("cp")
    xbf, ybf = t["xbf"], t["ybf"]
    KCVC = t["x_res"][:, :].bitcast(BF16)
    qc = lambda o: ybf[:, 1024 * o:1024 * (o + 1)]
    kc = lambda o: KCVC[:, 1024 * o:1024 * (o + 1)]
    vc = lambda o: KCVC[:, 16384 + 1024 * o:16384 + 1024 * (o + 1)]

    emit_ln_stats(C, lambda c: chunk(xbf, c, T), nch, t["m_B"], t["rstd_B"],
                  cast_src=lambda c: chunk(t["x_res"], c, T))
    for c in range(nch):
        p.op("vector", (lambda e, c=c: e.tensor_sub(chunk(xbf, c, T), chunk(xbf, c, T), t["m_B"][:, :])))
        p.op("vector", (lambda e, c=c: e.tensor_mul(chunk(xbf, c, T), chunk(xbf, c, T), t["rstd_B"][:, :])),
             inc=[(cv, 1)] if c == nch - 1 else ())
    xln_done = cv.n
    sp = None
    for c in range(nch):
        sp = p.dma("sync", x_work[128 * c:128 * (c + 1), :], chunk(t["x_res"], c, T))
    spill_done = sp

    barrier(C, "l0a_ln")

    p.op("vector", lambda e: e.memset(t["haloloc"][:, :], 0))
    p.op("vector", lambda e: e.memset(t["halorem"][:, :], 0))

    for c in range(nch):
        p.op("tensor", (lambda e, c=c: e.matmul(
            t["p6"][0:8, 0:1], t["wsmall"][:, 256 + 8 * c:256 + 8 * (c + 1)],
            t["constcol"][:, 1:2], start=(c == 0), stop=(c == nch - 1))),
             inc=[(ct, 1)] if c == nch - 1 else ())
    ctt = ct.n
    p.op("vector", (lambda e: e.tensor_copy(t["acol"][0:8, 3:4], t["p6"][0:8, 0:1])),
         inc=[(cv, 1)], wait=[(ct, ctt)])
    wsum_t = flush(C, "vector")

    sig_done = 0
    aev_done = 0
    for g in range(2):
        wb0 = [(cv, xln_done)]
        if g == 1:
            wb0.append((cs, sig_done))
            wb0.append((cv, aev_done))
        for c in range(nch):
            p.op("tensor", (lambda e, c=c, g=g: e.matmul(
                t["p4"][0:16, :], t["wsmallb"][:, 16 * c:16 * (c + 1)],
                xbf[:, T * c + 512 * g:T * c + 512 * (g + 1)],
                start=(c == 0), stop=(c == nch - 1))), wait=wb0 if c == 0 else ())
        for c in range(nch):
            p.op("tensor", (lambda e, c=c, g=g: e.matmul(
                t["p5"][0:8, :], t["wsmall"][:, 256 + 8 * c:256 + 8 * (c + 1)],
                chunk(t["x_res"], c, T)[:, 512 * g:512 * (g + 1)],
                start=(c == 0), stop=(c == nch - 1))),
                 inc=[(ct, 1)] if c == nch - 1 else ())
        ctt = ct.n
        p.op("scalar", (lambda e, g=g: e.activation(
            t["rows32b"][0:16, 512 * g:512 * (g + 1)], t["p4"][0:16, :], AF.Sigmoid)),
             inc=[(cs, 1)], wait=[(ct, ctt)])
        sig_done = cs.n
        p.op("vector", (lambda e, g=g: e.tensor_scalar(
            t["rows32b"][0:16, 512 * g:512 * (g + 1)], t["rows32b"][0:16, 512 * g:512 * (g + 1)],
            2.0, None, op0=ALU.mult)), wait=[(cs, cs.n)])
        p.op("vector", (lambda e, g=g: e.tensor_scalar(
            t["tmp32e"][0:8, :], t["m_B"][0:8, 512 * g:512 * (g + 1)],
            t["acol"][0:8, 3:4], None, op0=ALU.mult)), inc=[(cv, 1)], wait=[(cv, wsum_t)])
        p.op("vector", (lambda e, g=g: e.scalar_tensor_tensor(
            t["F_B"][0:8, 512 * g:512 * (g + 1)], t["p5"][0:8, :], 1.0,
            t["tmp32e"][0:8, :], op0=ALU.mult, op1=ALU.subtract)), wait=[(cv, cv.n)])
        p.op("vector", (lambda e, g=g: e.tensor_mul(
            t["F_B"][0:8, 512 * g:512 * (g + 1)], t["F_B"][0:8, 512 * g:512 * (g + 1)],
            t["rstd_B"][0:8, 512 * g:512 * (g + 1)])))
        p.op("vector", (lambda e, g=g: e.tensor_scalar(
            t["F_B"][0:8, 512 * g:512 * (g + 1)], t["F_B"][0:8, 512 * g:512 * (g + 1)],
            t["acol"][0:8, 1:2], None, op0=ALU.add)), inc=[(cv, 1)])
        aev_done = cv.n
    ba_done = cv.n
    if getattr(C, "dbg_pre", None):
        C.dbg_pre("a_done")
    A = t["rows32b"]
    p.op("scalar", (lambda e: e.activation(t["r2_B"][0:8, :], t["F_B"][0:8, :], AF.Relu)),
         inc=[(cs, 1)], wait=[(cv, ba_done)])
    relu_t = cs.n
    p.op("vector", (lambda e: e.tensor_scalar(t["AS_B"][0:8, :], t["F_B"][0:8, :], -1.0, None, op0=ALU.mult)))
    p.op("vector", (lambda e: e.tensor_max(t["AS_B"][0:8, :], t["F_B"][0:8, :], t["AS_B"][0:8, :])),
         inc=[(cv, 1)], wait=[(cv, cv.n)])
    cvt = cv.n
    p.op("scalar", (lambda e: e.activation(t["AS_B"][0:8, :], t["AS_B"][0:8, :], AF.Exp, scale=-1.0)),
         inc=[(cs, 1)], wait=[(cv, cvt)])
    cst = cs.n
    p.op("scalar", (lambda e: e.activation(t["AS_B"][0:8, :], t["AS_B"][0:8, :], AF.Ln,
                                           bias=t["constcol"][0:8, 1:2])), inc=[(cs, 1)],
         wait=[(cs, cst)])
    cst = cs.n
    p.op("vector", (lambda e: e.tensor_add(t["F_B"][0:8, :], t["AS_B"][0:8, :], t["r2_B"][0:8, :])),
         wait=[(cs, cst)])
    p.op("scalar", (lambda e: e.activation(t["acol"][0:8, 2:3], t["acol"][0:8, 0:1], AF.Exp)),
         inc=[(cs, 1)])
    cstA = flush(C, "scalar")
    p.op("vector", (lambda e: e.tensor_scalar(t["acol"][0:8, 2:3], t["acol"][0:8, 2:3],
                                              -1.0, None, op0=ALU.mult)),
         inc=[(cv, 1)], wait=[(cs, cstA)])
    cvA = flush(C, "vector")
    p.op("vector", (lambda e: e.tensor_scalar(t["F_B"][0:8, :], t["F_B"][0:8, :], t["acol"][0:8, 2:3],
                                              None, op0=ALU.mult)), inc=[(cv, 1)], wait=[(cv, cv.n)])
    g_done = flush(C, "vector")
    if getattr(C, "dbg_pre", None):
        C.dbg_pre("g_done")

    mm_done = {}
    for o in range(8):
        wb = [t["wsA"], t["wsB"]][o % 2]
        w = [(ct, mm_done[o - 2])] if o >= 2 else []
        ldt = p.dma("gpsimd", wb[:, :], Wgs[o], wait=w)
        for g in range(2):
            pb = [t["p6"], t["p7"]][g]
            for c in range(nch):
                p.op("tensor", (lambda e, c=c, g=g, pb=pb, wb=wb: e.matmul(
                    pb[:, :], wb[:, 128 * c:128 * (c + 1)],
                    xbf[:, T * c + 512 * g:T * c + 512 * (g + 1)],
                    start=(c == 0), stop=(c == nch - 1))),
                     inc=[(ct, 1)] if c == nch - 1 else (),
                     wait=[ldt] + ([(cs, p.sem("cs").n)] if o > 0 else []) if c == 0 else ())
        mm_done[o] = ct.n
        ctt = ct.n
        for g in range(2):
            pb = [t["p6"], t["p7"]][g]
            wsl = [(ct, ctt)]
            if o >= 1:
                wsl.append(gg_dma_t)
            p.op("scalar", (lambda e, g=g, pb=pb: e.activation(
                t["ggtmp"][:, 512 * g:512 * (g + 1)], pb[:, :], AF.Silu)),
                 inc=[(cs, 1)], wait=wsl if g == 0 else ())
        cst = cs.n
        gg_dma_t = p.dma("gpsimd", gg_work[o], t["ggtmp"][:, :], wait=[(cs, cst)])

    PLAN = [("q", 8, Wqs, qc), ("k", 16, Wks, kc), ("v", 16, Wvs, vc)]
    hcol = 0
    halo_done = {}
    pbr = [t["p0"], t["p1"], t["p2"], t["p3"]]
    i = 0
    raw_done = {}
    for nm, nо, Ws, dst in PLAN:
        for o in range(nо):
            wb = [t["wsA"], t["wsB"]][i % 2]
            w = [(ct, raw_done[i - 2])] if i >= 2 else [(ct, mm_done[7])]
            ldt = p.dma("gpsimd", wb[:, :], Ws[o], wait=w)
            for g in range(2):
                pb = pbr[(2 * i + g) % 4]
                wmm = [ldt]
                if i >= 2:
                    wmm.append((cv, halo_done[i - 2]))
                for c in range(nch):
                    p.op("tensor", (lambda e, c=c, g=g, pb=pb, wb=wb: e.matmul(
                        pb[:, :], wb[:, 128 * c:128 * (c + 1)],
                        xbf[:, T * c + 512 * g:T * c + 512 * (g + 1)],
                        start=(c == 0), stop=(c == nch - 1))),
                         inc=[(ct, 1)] if c == nch - 1 else (), wait=wmm if c == 0 else ())
            raw_done[i] = ct.n
            ctt = ct.n
            for g in range(2):
                pb = pbr[(2 * i + g) % 4]
                wv = [(ct, ctt)]
                if i == 0 and g == 0:
                    wv.append(spill_done)
                p.op("vector", (lambda e, g=g, pb=pb, dst=dst, o=o: e.tensor_copy(
                    dst(o)[:, 512 * g:512 * (g + 1)], pb[:, :])), wait=wv)
            pb1 = pbr[(2 * i + 1) % 4]
            p.op("vector", (lambda e, pb1=pb1, hcol=hcol: e.tensor_copy(
                t["haloloc"][:, 3 * hcol:3 * hcol + 3], pb1[:, 509:512])),
                 inc=[(cv, 1)])
            halo_done[i] = cv.n
            hcol += 1
            i += 1
    allraw = flush(C, "vector")

    exd = p.sem("exdB")
    exs = p.sem("exchB", "hw")

    def exchange(e):
        e.wait_ge(p.real["cv"], allraw)
        with e.register("cntB") as cnt, e.register("rroleB") as rrole, \
             e.register("rflagB") as rflag, e.register("rcondB") as rcond:
            e.reg_mov(cnt, 0)
            e.reg_load(rrole, t["sroleB"][0:1, 0:1])
            with e.If_eq(rrole, 0):
                e.dma_start(out=sh_halo[0], in_=t["haloloc"][:, :]).then_inc(p.real["exchB"], 16)
                e.reg_add(cnt, cnt, 16)
                e.wait_ge(p.real["exchB"], cnt)
                e.dma_start(out=sh_flag[0:1, :], in_=t["sflagB"][:, :]).then_inc(p.real["exchB"], 16)
                e.reg_add(cnt, cnt, 16)
                e.wait_ge(p.real["exchB"], cnt)
            with e.If_eq(rrole, 1):
                def cond():
                    e.dma_start(out=t["spollB"][:, :], in_=sh_flag[0:1, :]).then_inc(p.real["exchB"], 16)
                    e.reg_add(cnt, cnt, 16)
                    e.wait_ge(p.real["exchB"], cnt)
                    e.reg_load(rflag, t["spollB"][0:1, 0:1])
                    e.reg_alu(rcond, rflag, 21, mybir.AluOpType.not_equal)
                    return rcond
                with e.While(cond):
                    e.nop()
                e.dma_start(out=t["halorem"][:, :], in_=sh_halo[0]).then_inc(p.real["exchB"], 16)
                e.reg_add(cnt, cnt, 16)
                e.wait_ge(p.real["exchB"], cnt)
        e.nop().then_inc(p.real["exdB"], 1)

    p.raw("sync", exchange, inc=[(exd, 1)])
    exd_t = exd.n

    barrier(C, "l0a_halo")

    conv_done = {}
    ci = 0
    for nm, nо, Ws, dst in PLAN:
        base = {"q": 0, "k": 32, "v": 96}[nm]
        for o in range(nо):
            hcolо = {"q": 0, "k": 8, "v": 24}[nm] + o
            p.op("vector", (lambda e, dst=dst, o=o: e.tensor_copy(
                t["convext"][:, 3:1027], dst(o)[:, :])))
            p.op("vector", (lambda e, hcolо=hcolо: e.tensor_copy(
                t["convext"][:, 0:3], t["halorem"][:, 3 * hcolо:3 * hcolо + 3])))
            p.op("vector", (lambda e, base=base, o=o: e.tensor_scalar(
                t["sqA"][:, :], t["convext"][:, 3:1027], t["convw"][:, base + 4 * o:base + 4 * o + 1],
                None, op0=ALU.mult)),
                 wait=[(cs, conv_done[ci - 1])] if ci >= 1 else [(p.sem("exdB"), exd_t)])
            for j in range(1, 4):
                p.op("vector", (lambda e, base=base, o=o, j=j: e.scalar_tensor_tensor(
                    t["sqA"][:, :], t["convext"][:, 3 - j:1027 - j],
                    t["convw"][:, base + 4 * o + j:base + 4 * o + j + 1], t["sqA"][:, :],
                    op0=ALU.mult, op1=ALU.add)))
            p.op("vector", (lambda e: e.engine_nop()), inc=[(cv, 1)])
            cvt = cv.n
            p.op("scalar", (lambda e, dst=dst, o=o: e.activation(
                dst(o)[:, :], t["sqA"][:, :], AF.Silu)), inc=[(cs, 1)], wait=[(cv, cvt)])
            conv_done[ci] = cs.n
            ci += 1
    convs_done = cs.n

    barrier(C, "l0a_conv")

    prev_sqrt = 0
    for which, n_, get in (("k", 16, kc), ("q", 8, qc)):
        for o in range(n_):
            p.op("vector", (lambda e, get=get, o=o: e.tensor_mul(
                t["sqA"][:, :], get(o)[:, :], get(o)[:, :])), inc=[(cv, 1)])
            cvt = cv.n
            for g in range(2):
                wmm2 = [(cv, cvt)] + ([(cs, prev_sqrt)] if prev_sqrt else [])
                p.op("tensor", (lambda e, g=g, get=get, o=o: e.matmul(
                    t["p6"][0:1, :] if g == 0 else t["p7"][0:1, :],
                    t["ones128_bf"][:, 0:1], t["sqA"][:, 512 * g:512 * (g + 1)],
                    start=True, stop=True)),
                     inc=[(ct, 1)] if g == 1 else (), wait=wmm2 if g == 0 else ())
            ctt = ct.n
            R = t["rows32a"]
            p.op("scalar", (lambda e: e.activation(R[0:1, 0:512], t["p6"][0:1, :], AF.Sqrt)),
                 wait=[(ct, ctt)])
            p.op("scalar", (lambda e: e.activation(R[0:1, 512:1024], t["p7"][0:1, :], AF.Sqrt)),
                 inc=[(cs, 1)])
            cst = cs.n
            prev_sqrt = cst
            p.op("vector", (lambda e: e.tensor_scalar(R[0:1, :], R[0:1, :], 1e-12, None, op0=ALU.max)),
                 inc=[(cv, 1)], wait=[(cs, cst)])
            p.op("vector", (lambda e: e.reciprocal(R[0:1, :], R[0:1, :])),
                 inc=[(cv, 1)], wait=[(cv, cv.n)])
            if which == "q":
                p.op("vector", (lambda e: e.tensor_scalar(R[0:1, :], R[0:1, :], RT128, None, op0=ALU.mult)),
                     inc=[(cv, 1)], wait=[(cv, cv.n)])
            cvt = flush(C, "vector")
            for g in range(2):
                p.op("tensor", (lambda e, g=g: e.matmul(
                    [t["p6"], t["p7"]][g][:, :], t["onesr_f"][:, :], R[0:1, 512 * g:512 * (g + 1)],
                    start=True, stop=True)), inc=[(ct, 1)] if g == 1 else (),
                     wait=[(cv, cvt)] if g == 0 else ())
            ctt = ct.n
            for g in range(2):
                p.op("vector", (lambda e, g=g, get=get, o=o: e.tensor_mul(
                    get(o)[:, 512 * g:512 * (g + 1)], get(o)[:, 512 * g:512 * (g + 1)],
                    [t["p6"], t["p7"]][g][:, :])),
                     inc=[(cv, 1)] if g == 1 else (), wait=[(ct, ctt)] if g == 0 else ())
    norm_done = cv.n

    SP = t["scanprep"]
    R = t["rows32a"]
    for jj in range(8):
        pb = [t["p4"], t["p5"]][jj % 2]
        p.op("tensor", (lambda e, jj=jj, pb=pb: e.transpose(
            pb[:, 0:8], t["F_B"][0:8, 128 * jj:128 * (jj + 1)], t["ident_f"][0:8, 0:8])),
             inc=[(ct, 1)], wait=[(cv, p.sem("cv").n), (cv, g_done)])
        ctt = ct.n
        p.op("vector", (lambda e, jj=jj, pb=pb: e.tensor_copy(
            SP[:, 1024 + 8 * jj:1024 + 8 * (jj + 1)], pb[:, 0:8])),
             inc=[(cv, 1)], wait=[(ct, ctt)])
    for jj in range(8):
        pb = [t["p4"], t["p5"]][jj % 2]
        p.op("tensor", (lambda e, jj=jj, pb=pb: e.transpose(
            pb[:, 0:16], A[0:16, 128 * jj:128 * (jj + 1)], t["ident_f"][0:16, 0:16])),
             inc=[(ct, 1)], wait=[(cv, p.sem("cv").n)])
        ctt = ct.n
        p.op("vector", (lambda e, jj=jj, pb=pb: e.tensor_copy(
            SP[:, 1152 + 16 * jj:1152 + 16 * (jj + 1)], pb[:, 0:16])),
             inc=[(cv, 1)], wait=[(ct, ctt)])
    gt_done = flush(C, "vector")
    for par in range(2):
        p.op("tensor", (lambda e, par=par: e.matmul(
            t["p4"][0:64, 0:64], t["cumsL"][64 * par:64 * (par + 1), :],
            SP[64 * par:64 * (par + 1), 1024:1088], start=True, stop=True)),
             inc=[(ct, 1)], wait=[(cv, gt_done)] if par == 0 else [(cv, p.sem("cv").n)])
        ctt = ct.n
        p.op("vector", (lambda e, par=par: e.tensor_copy(
            SP[0:64, 384 + 64 * par:448 + 64 * par], t["p4"][0:64, 0:64])),
             inc=[(cv, 1)], wait=[(ct, ctt)])
    cs_done = flush(C, "vector")
    p.op("tensor", (lambda e: e.matmul(
        t["p4"][:, 0:128], t["dupB"][:, :], SP[0:64, 384:512], start=True, stop=True)),
         inc=[(ct, 1)], wait=[(cv, cs_done)])
    ctt = ct.n
    p.op("vector", (lambda e: e.tensor_copy(SP[:, 0:128], t["p4"][:, 0:128])),
         inc=[(cv, 1)], wait=[(ct, ctt)])
    bT3 = SP[:, 1152:1280].rearrange("p (j r) -> p j r", r=16)
    for par in range(2):
        p.op("tensor", (lambda e, par=par: e.matmul(
            t["p5"][:, 0:64], t["dupE"][64 * par:64 * (par + 1), :], bT3[64 * par:64 * (par + 1), :, 0:8],
            start=True, stop=False)), wait=[(cv, p.sem("cv").n)] if par == 1 else ())
        p.op("tensor", (lambda e, par=par: e.matmul(
            t["p5"][:, 0:64], t["dupO"][64 * par:64 * (par + 1), :], bT3[64 * par:64 * (par + 1), :, 8:16],
            start=False, stop=True)), inc=[(ct, 1)])
        ctt = ct.n
        p.op("vector", (lambda e, par=par: e.tensor_copy(
            SP[:, 128 + 64 * par:192 + 64 * par], t["p5"][:, 0:64])),
             inc=[(cv, 1)], wait=[(ct, ctt)])
    ball_done = flush(C, "vector")
    p.op("tensor", (lambda e: e.matmul(
        t["p6"][0:1, 0:128], t["sel63"][:, :], SP[0:64, 384:512], start=True, stop=True)),
         inc=[(ct, 1)], wait=[(cv, ball_done)])
    ctt = ct.n
    p.op("vector", (lambda e: e.tensor_copy(R[0:1, 0:128], t["p6"][0:1, 0:128])),
         inc=[(cv, 1)], wait=[(ct, ctt)])
    cvt = flush(C, "vector")
    p.op("scalar", (lambda e: e.activation(R[0:1, 128:256], R[0:1, 0:128], AF.Exp)),
         inc=[(cs, 1)], wait=[(cv, cvt)])
    cst = flush(C, "scalar")
    p.op("tensor", (lambda e: e.matmul(
        t["p6"][:, 0:128], t["onesr_f"][:, :], R[0:1, 128:256], start=True, stop=True)),
         wait=[(cs, cst)])
    p.op("tensor", (lambda e: e.matmul(
        t["p6"][:, 128:256], t["onesr_f"][:, :], R[0:1, 0:128], start=True, stop=True)),
         inc=[(ct, 1)])
    ctt = ct.n
    p.op("vector", (lambda e: e.tensor_copy(SP[:, 256:384], t["p6"][:, 0:128])),
         wait=[(ct, ctt)])
    p.op("vector", (lambda e: e.tensor_copy(SP[:, 640:768], t["p6"][:, 128:256])),
         inc=[(cv, 1)])
    p.op("scalar", (lambda e: e.activation(SP[0:64, 512:640], SP[0:64, 384:512], AF.Exp)),
         inc=[(cs, 1)], wait=[(cv, cs_done)])
    barrier(C, "l0a_prep")


def emit_l0attn_scan(C, sh_smid, sh_flag, nch=16):
    p, t, T = C.p, C.t, C.T
    cv, cs, ct, cp = p.sem("cv"), p.sem("cs"), p.sem("ct"), p.sem("cp")
    xbf, ybf = t["xbf"], t["ybf"]
    SCR = xbf
    SCRF = xbf[:, :].bitcast(F32)
    SP = t["scanprep"]
    R = t["rows32a"]
    S_ar = t["F_B"]

    G = [512 * k for k in range(7)]
    H = [3584 + 512 * k for k in range(5)]
    Kq, Vq, KTq, KtTq, Yq, Uq, Sbq, Kngq, EgBq, VTq = (
        6144, 6656, 7168, 7680, 8192, 8704, 9216, 9728, 10240, 11264)
    Poq, Qppq = 10752, 11008
    bf = lambda c0, w=512: SCR[:, c0:c0 + w]
    Ef = SCRF[:, 6144:6656]
    Gtf = SCRF[:, 6656:7168]
    edf = SCRF[:, 7168:7176]

    p0, p1, p2, p3, p4, p5 = t["p0"], t["p1"], t["p2"], t["p3"], t["p4"], t["p5"]
    p6b = t["p6"][:, :].bitcast(BF16)
    p7b = t["p7"][:, :].bitcast(BF16)

    p.op("vector", (lambda e: e.memset(S_ar[:, :], 0.0)), inc=[(cv, 1)])
    s_upd = {h: cv.n for h in range(8)}
    smem_t = cv.n

    exdC = p.sem("exdC")
    exC = p.sem("exchC", "hw")

    def exchange_load(e):
        e.wait_ge(p.real["cv"], smem_t)
        with e.register("cntC") as cnt, e.register("rroleC") as rrole, \
             e.register("rflagC") as rflag, e.register("rcondC") as rcond:
            e.reg_mov(cnt, 0)
            e.reg_load(rrole, t["sroleB"][0:1, 0:1])
            with e.If_eq(rrole, 1):
                def cond():
                    e.dma_start(out=t["spollB"][:, :], in_=sh_flag[2:3, :]).then_inc(p.real["exchC"], 16)
                    e.reg_add(cnt, cnt, 16)
                    e.wait_ge(p.real["exchC"], cnt)
                    e.reg_load(rflag, t["spollB"][0:1, 0:1])
                    e.reg_alu(rcond, rflag, 33, mybir.AluOpType.not_equal)
                    return rcond
                with e.While(cond):
                    e.nop()
                e.dma_start(out=S_ar[:, :], in_=sh_smid[0]).then_inc(p.real["exchC"], 16)
                e.reg_add(cnt, cnt, 16)
                e.wait_ge(p.real["exchC"], cnt)
        e.nop().then_inc(p.real["exdC"], 1)

    p.raw("sync", exchange_load, inc=[(exdC, 1)])
    exdC_t = exdC.n

    H5R = 11776

    prev_pe_end = 0
    prev_gbuild = 0
    prev_ets = 0
    prev_expg = 0
    prev_acts = 0
    KCV = t["x_res"][:, :].bitcast(BF16)
    for j in range(nch):
        par, jj = j % 2, j // 2
        for q in range(2):
            heads = [4 * q + i for i in range(4)]
            ncol = [64 * par + 8 * jj + h for h in heads]
            w0 = []
            if prev_pe_end:
                w0.append((ct, prev_pe_end))
            if j == 0 and q == 0:
                w0.append((exdC, exdC_t))
            first = True
            for idx, h in enumerate(heads):
                for nh in range(2):
                    koff = 1024 * (8 * nh + h) + 64 * j
                    voff = 16384 + 1024 * (8 * nh + h) + 64 * j
                    p.op("vector", (lambda e, idx=idx, nh=nh, koff=koff: e.tensor_copy(
                        bf(Kq)[:, 128 * idx + nh:128 * (idx + 1):2], KCV[:, koff:koff + 64])),
                         wait=w0 if first else ())
                    first = False
                    p.op("vector", (lambda e, idx=idx, nh=nh, voff=voff: e.tensor_copy(
                        bf(Vq)[:, 128 * idx + nh:128 * (idx + 1):2], KCV[:, voff:voff + 64])))
                p.op("vector", (lambda e, idx=idx, h=h: e.tensor_copy(
                    bf(Sbq)[:, 128 * idx:128 * (idx + 1)], S_ar[:, 128 * h:128 * (h + 1)])),
                     inc=[(cv, 1)] if idx == 3 else ())
            t0 = cv.n
            w1 = [(cv, t0)]
            if prev_gbuild:
                w1.append((cv, prev_gbuild))
            if prev_ets:
                w1.append((cv, prev_ets))
            if prev_expg:
                w1.append((cs, prev_expg))
            if prev_acts:
                w1.append((cs, prev_acts))
            for idx in range(4):
                p.op("tensor", (lambda e, idx=idx: e.matmul(
                    p0[:, 128 * idx:128 * (idx + 1)], bf(Kq)[:, 128 * idx:128 * (idx + 1)],
                    bf(Kq)[:, 128 * idx:128 * (idx + 1)], start=True, stop=True)),
                     wait=w1 if idx == 0 else ())
            for idx in range(4):
                p.op("tensor", (lambda e, idx=idx, n=ncol[idx]: e.matmul(
                    p4[0:1, 128 * idx:128 * (idx + 1)], SP[:, n:n + 1], t["ident_f"][:, :],
                    start=True, stop=True)))
            for idx in range(4):
                p.op("tensor", (lambda e, idx=idx: e.transpose(
                    p6b[:, 128 * idx:128 * (idx + 1)], bf(Kq)[:, 128 * idx:128 * (idx + 1)],
                    t["ident_bf"][:, :])))
            for idx in range(4):
                p.op("tensor", (lambda e, idx=idx: e.transpose(
                    p7b[:, 128 * idx:128 * (idx + 1)], bf(Vq)[:, 128 * idx:128 * (idx + 1)],
                    t["ident_bf"][:, :])), inc=[(ct, 1)] if idx == 3 else ())
            t1 = ct.n
            p.op("vector", (lambda e: e.tensor_copy(R[0:1, 256:768], p4[0:1, :])),
                 inc=[(cv, 1)], wait=[(ct, t1)])
            t2v = flush(C, "vector")
            p.op("scalar", (lambda e: e.activation(bf(KTq)[:, :], p6b[:, 0:512], AF.Copy)),
                 wait=[(ct, t1)])
            p.op("scalar", (lambda e: e.activation(bf(VTq)[:, :], p7b[:, 0:512], AF.Copy)),
                 inc=[(cs, 1)])
            t2s = cs.n
            for idx in range(4):
                p.op("tensor", (lambda e, idx=idx: e.matmul(
                    p4[:, 128 * idx:128 * (idx + 1)], t["onesr_f"][:, :],
                    R[0:1, 256 + 128 * idx:256 + 128 * (idx + 1)], start=True, stop=True)),
                     inc=[(ct, 1)] if idx == 3 else (), wait=[(cv, t2v)] if idx == 0 else ())
            t3 = ct.n
            for idx in range(4):
                p.op("vector", (lambda e, idx=idx, n=ncol[idx]: e.tensor_scalar(
                    Ef[:, 128 * idx:128 * (idx + 1)], p4[:, 128 * idx:128 * (idx + 1)],
                    SP[:, n:n + 1], 0.0, op0=ALU.subtract, op1=ALU.min)),
                     inc=[(cv, 1)] if idx == 3 else (), wait=[(ct, t3)] if idx == 0 else ())
            t4v = cv.n
            p.op("scalar", (lambda e: e.activation(Ef[:, :], Ef[:, :], AF.Exp)),
                 wait=[(cv, t4v)])
            p.op("scalar", (lambda e: e.activation(bf(EgBq)[:, :], p4[:, :], AF.Exp)),
                 inc=[(cs, 1)])
            t5s = cs.n
            prev_ets = t4v
            prev_expg = t5s
            p.op("vector", (lambda e: e.tensor_mul(Gtf[:, :], Ef[:, :], p0[:, :])),
                 wait=[(cs, t5s)])
            for idx in range(4):
                p.op("vector", (lambda e, idx=idx, n=ncol[idx]: e.tensor_scalar(
                    Gtf[:, 128 * idx:128 * (idx + 1)], Gtf[:, 128 * idx:128 * (idx + 1)],
                    SP[:, 128 + n:129 + n], None, op0=ALU.mult)))
            p.op("vector", (lambda e: e.tensor_mul(bf(G[0])[:, :], Gtf[:, :], t["m1c4"][:, :])))
            p.op("vector", (lambda e: e.scalar_tensor_tensor(
                bf(Kngq)[:, :], bf(Kq)[:, :], -1.0, bf(EgBq)[:, :], op0=ALU.mult, op1=ALU.mult)),
                 inc=[(cv, 1)])
            t6v = cv.n
            prev_gbuild = t6v
            for idx in range(4):
                p.op("tensor", (lambda e, idx=idx: e.transpose(
                    p6b[:, 128 * idx:128 * (idx + 1)], bf(G[0])[:, 128 * idx:128 * (idx + 1)],
                    t["ident_bf"][:, :])), wait=[(cv, t6v), (cs, t2s)] if idx == 0 else ())
            for idx in range(4):
                p.op("tensor", (lambda e, idx=idx: e.matmul(
                    p1[:, 128 * idx:128 * (idx + 1)], bf(Kngq)[:, 128 * idx:128 * (idx + 1)],
                    bf(Sbq)[:, 128 * idx:128 * (idx + 1)], start=True, stop=True)),
                     inc=[(ct, 1)] if idx == 3 else ())
            t7 = ct.n
            p.op("scalar", (lambda e: e.activation(bf(H[0])[:, :], p6b[:, 0:512], AF.Copy)),
                 inc=[(cs, 1)], wait=[(ct, t7)])
            t8s = cs.n
            prev_acts = t8s
            p.op("vector", (lambda e: e.tensor_add(bf(Yq)[:, :], p1[:, :], bf(VTq)[:, :])),
                 inc=[(cv, 1)], wait=[(ct, t7)])
            t8v = cv.n
            for idx in range(4):
                p.op("vector", (lambda e, idx=idx, n=ncol[idx]: e.tensor_sub(
                    edf[:, idx:idx + 1], SP[:, 640 + n:641 + n], SP[:, n:n + 1])),
                     inc=[(cv, 1)] if idx == 3 else ())
            edt = flush(C, "vector")
            p.op("scalar", (lambda e: e.activation(edf[:, 0:4], edf[:, 0:4], AF.Exp)),
                 inc=[(cs, 1)], wait=[(cv, edt)])
            edts = flush(C, "scalar")
            for idx in range(4):
                p.op("vector", (lambda e, idx=idx: e.tensor_scalar(
                    bf(KtTq)[:, 128 * idx:128 * (idx + 1)], bf(KTq)[:, 128 * idx:128 * (idx + 1)],
                    edf[:, idx:idx + 1], None, op0=ALU.mult)),
                     inc=[(cv, 1)] if idx == 3 else (), wait=[(cs, edts)] if idx == 0 else ())
            ktt_done = cv.n
            seq = [(G[1], H[0], G[0]), (H[1], G[0], H[0]),
                   (G[2], H[1], G[1]), (H[2], G[1], H[1]),
                   (G[3], H[2], G[2]), (H[3], G[2], H[2]),
                   (G[4], H[3], G[3]), (H[4], G[3], H[3]),
                   (G[5], H[4], G[4]), (H5R, G[4], H[4]),
                   (G[6], H5R, G[5])]
            ev = {G[0]: (cv, t6v), H[0]: (cs, t8s)}
            bank_ev = {}
            banks2 = [p1, p5]
            rr = 0
            for dst, lh, rh in seq:
                pb = banks2[rr % 2]
                wv = [ev[lh], ev[rh]]
                if rr == 0:
                    wv.append((cv, t8v))
                if rr >= 2:
                    wv.append(bank_ev[rr - 2])
                for idx in range(4):
                    p.op("tensor", (lambda e, idx=idx, lh=lh, rh=rh, pb=pb: e.matmul(
                        pb[:, 128 * idx:128 * (idx + 1)], bf(lh)[:, 128 * idx:128 * (idx + 1)],
                        bf(rh)[:, 128 * idx:128 * (idx + 1)], start=True, stop=True)),
                         inc=[(ct, 1)] if idx == 3 else (), wait=wv if idx == 0 else ())
                mmt = ct.n
                if rr % 2 == 0:
                    p.op("vector", (lambda e, dst=dst, pb=pb: e.tensor_copy(bf(dst)[:, :], pb[:, :])),
                         inc=[(cv, 1)], wait=[(ct, mmt)])
                    ev[dst] = (cv, cv.n)
                else:
                    p.op("scalar", (lambda e, dst=dst, pb=pb: e.activation(
                        bf(dst)[:, :], pb[:, :], AF.Copy)), inc=[(cs, 1)], wait=[(ct, mmt)])
                    ev[dst] = (cs, cs.n)
                bank_ev[rr] = ev[dst]
                rr += 1
            ylast = (cv, t8v)
            for k in range(7):
                pb = banks2[rr % 2]
                wv = [ev[G[k]], ylast]
                if rr >= 2:
                    wv.append(bank_ev[rr - 2])
                for idx in range(4):
                    p.op("tensor", (lambda e, idx=idx, k=k, pb=pb: e.matmul(
                        pb[:, 128 * idx:128 * (idx + 1)], bf(G[k])[:, 128 * idx:128 * (idx + 1)],
                        bf(Yq)[:, 128 * idx:128 * (idx + 1)], start=True, stop=False)),
                         wait=wv if idx == 0 else ())
                    p.op("tensor", (lambda e, idx=idx, pb=pb: e.matmul(
                        pb[:, 128 * idx:128 * (idx + 1)], t["ident_bf"][:, :],
                        bf(Yq)[:, 128 * idx:128 * (idx + 1)], start=False, stop=True)),
                         inc=[(ct, 1)] if idx == 3 else ())
                mmt = ct.n
                if rr % 2 == 0:
                    p.op("vector", (lambda e, pb=pb: e.tensor_copy(bf(Yq)[:, :], pb[:, :])),
                         inc=[(cv, 1)], wait=[(ct, mmt)])
                    ylast = (cv, cv.n)
                else:
                    p.op("scalar", (lambda e, pb=pb: e.activation(bf(Yq)[:, :], pb[:, :], AF.Copy)),
                         inc=[(cs, 1)], wait=[(ct, mmt)])
                    ylast = (cs, cs.n)
                bank_ev[rr] = ylast
                rr += 1
            for idx in range(4):
                p.op("vector", (lambda e, idx=idx, n=ncol[idx]: e.tensor_scalar(
                    bf(Uq)[:, 128 * idx:128 * (idx + 1)], bf(Yq)[:, 128 * idx:128 * (idx + 1)],
                    SP[:, 128 + n:129 + n], None, op0=ALU.mult)),
                     inc=[(cv, 1)] if idx == 3 else (),
                     wait=[ylast] if (idx == 0 and ylast[0] is cs) else ())
            u_done = cv.n
            wpq = [bank_ev[rr - 2], bank_ev[rr - 1]]
            for idx in range(4):
                p.op("tensor", (lambda e, idx=idx, hh=heads[idx], j=j: e.matmul(
                    p2[:, 64 * idx:64 * (idx + 1)], bf(Kq)[:, 128 * idx:128 * (idx + 1)],
                    ybf[:, 1024 * hh + 64 * j:1024 * hh + 64 * j + 64],
                    start=True, stop=True)))
            for idx in range(4):
                p.op("tensor", (lambda e, idx=idx, n=ncol[idx]: e.matmul(
                    p2[0:1, 256 + 64 * idx:256 + 64 * (idx + 1)],
                    SP[0:64, 512 + n:513 + n], t["ident_f"][0:64, 0:64],
                    start=True, stop=True)), inc=[(ct, 1)] if idx == 3 else ())
            tpq = ct.n
            p.op("vector", (lambda e: e.tensor_copy(R[0:1, 768:1024], p2[0:1, 256:512])),
                 inc=[(cv, 1)], wait=[(ct, tpq)])
            tlrow = flush(C, "vector")
            for idx in range(4):
                p.op("tensor", (lambda e, idx=idx: e.matmul(
                    p2[:, 256 + 64 * idx:256 + 64 * (idx + 1)], t["onesr_f"][:, :],
                    R[0:1, 768 + 64 * idx:768 + 64 * (idx + 1)], start=True, stop=True)),
                     inc=[(ct, 1)] if idx == 3 else (), wait=[(cv, tlrow)] if idx == 0 else ())
            tlb = ct.n
            for idx in range(4):
                p.op("vector", (lambda e, idx=idx: e.tensor_mul(
                    Gtf[:, 128 * idx:128 * idx + 64], p2[:, 64 * idx:64 * (idx + 1)],
                    Ef[:, 128 * idx + 1:128 * (idx + 1):2])), wait=[(ct, tlb)] if idx == 0 else ())
            for idx in range(4):
                p.op("vector", (lambda e, idx=idx: e.tensor_mul(
                    bf(Poq)[:, 64 * idx:64 * (idx + 1)], Gtf[:, 128 * idx:128 * idx + 64],
                    t["m2c"][:, :])))
            for idx in range(4):
                p.op("vector", (lambda e, idx=idx, hh=heads[idx], j=j: e.tensor_mul(
                    bf(Qppq)[:, 64 * idx:64 * (idx + 1)],
                    ybf[:, 1024 * hh + 64 * j:1024 * hh + 64 * j + 64],
                    p2[:, 256 + 64 * idx:256 + 64 * (idx + 1)])),
                     inc=[(cv, 1)] if idx == 3 else ())
            tpo = cv.n
            for idx in range(4):
                p.op("tensor", (lambda e, idx=idx: e.matmul(
                    p3[:, 64 * idx:64 * (idx + 1)], bf(Uq)[:, 128 * idx:128 * (idx + 1)],
                    bf(Poq)[:, 64 * idx:64 * (idx + 1)], start=True, stop=False)),
                     wait=[(cv, tpo), (cv, u_done)] if idx == 0 else ())
                p.op("tensor", (lambda e, idx=idx: e.matmul(
                    p3[:, 64 * idx:64 * (idx + 1)], bf(Sbq)[:, 128 * idx:128 * (idx + 1)],
                    bf(Qppq)[:, 64 * idx:64 * (idx + 1)], start=False, stop=True)))
            for idx in range(4):
                p.op("tensor", (lambda e, idx=idx: e.matmul(
                    p5[:, 128 * idx:128 * (idx + 1)], bf(KtTq)[:, 128 * idx:128 * (idx + 1)],
                    bf(Uq)[:, 128 * idx:128 * (idx + 1)], start=True, stop=True)),
                     inc=[(ct, 1)] if idx == 3 else (),
                     wait=[(cv, ktt_done), bank_ev[rr - 2]] if idx == 0 else ())
            tos = ct.n
            prev_pe_end = tos
            for idx, h in enumerate(heads):
                p.op("vector", (lambda e, idx=idx, h=h, j=j: e.tensor_copy(
                    ybf[:, 8192 + 1024 * h + 64 * j:8192 + 1024 * h + 64 * j + 64],
                    p3[:, 64 * idx:64 * (idx + 1)])), wait=[(ct, tos)] if idx == 0 else ())
            for idx, h in enumerate(heads):
                p.op("vector", (lambda e, idx=idx, h=h, n=ncol[idx]: e.tensor_scalar(
                    S_ar[:, 128 * h:128 * (h + 1)], S_ar[:, 128 * h:128 * (h + 1)],
                    SP[:, 256 + n:257 + n], None, op0=ALU.mult)))
                p.op("vector", (lambda e, idx=idx, h=h: e.tensor_add(
                    S_ar[:, 128 * h:128 * (h + 1)], S_ar[:, 128 * h:128 * (h + 1)],
                    p5[:, 128 * idx:128 * (idx + 1)])), inc=[(cv, 1)])
                s_upd[h] = cv.n
            if getattr(C, "dbg_scan", None):
                C.dbg_scan(j, q)

    sfinal = flush(C, "vector")

    def exchange_pub2(e):
        e.wait_ge(p.real["cv"], sfinal)
        with e.register("cntD") as cnt, e.register("rroleD") as rrole:
            e.reg_mov(cnt, 0)
            e.reg_load(rrole, t["sroleB"][0:1, 0:1])
            with e.If_eq(rrole, 0):
                e.dma_start(out=sh_smid[0], in_=S_ar[:, :]).then_inc(p.real["exchC"], 16)
                e.reg_add(cnt, cnt, 16)
                e.wait_ge(p.real["exchC"], cnt)
                e.dma_start(out=sh_flag[2:3, 0:1], in_=t["sroleB"][0:1, 1:2]).then_inc(p.real["exchC"], 16)
                e.reg_add(cnt, cnt, 16)
                e.wait_ge(p.real["exchC"], cnt)
        e.nop().then_inc(p.real["exdC"], 1)

    p.raw("sync", exchange_pub2, inc=[(exdC, 1)])
    barrier(C, "l0a_scan")


def emit_l0attn_post(C, Wos, gg_work, x_work, nch=16):
    p, t, T = C.p, C.t, C.T
    cv, cs, ct, cp = p.sem("cv"), p.sem("cs"), p.sem("ct"), p.sem("cp")
    ybf = t["ybf"]
    KCV = t["x_res"][:, :].bitcast(BF16)
    R = t["rows32a"]
    oar = lambda h: ybf[:, 8192 + 1024 * h:8192 + 1024 * (h + 1)]

    prev_gg = 0
    for h in range(8):
        ld = p.dma("gpsimd", t["ggtmp"][:, :], gg_work[h],
                   wait=[(cv, prev_gg)] if h > 0 else ())
        p.op("vector", (lambda e, h=h: e.tensor_mul(t["sqA"][:, :], oar(h)[:, :], oar(h)[:, :])),
             inc=[(cv, 1)])
        cvt = cv.n
        for g in range(2):
            p.op("tensor", (lambda e, g=g: e.matmul(
                [t["p6"], t["p7"]][g][0:1, :], t["ones128_bf"][:, 0:1],
                t["sqA"][:, 512 * g:512 * (g + 1)], start=True, stop=True)),
                 inc=[(ct, 1)] if g == 1 else (), wait=[(cv, cvt)] if g == 0 else ())
        ctt = ct.n
        p.op("scalar", (lambda e: e.activation(R[0:1, 0:512], t["p6"][0:1, :], AF.Sqrt,
                                               scale=1.0 / 128.0, bias=t["constcol"][0:1, 0:1])),
             wait=[(ct, ctt)])
        p.op("scalar", (lambda e: e.activation(R[0:1, 512:1024], t["p7"][0:1, :], AF.Sqrt,
                                               scale=1.0 / 128.0, bias=t["constcol"][0:1, 0:1])),
             inc=[(cs, 1)])
        cst = cs.n
        p.op("vector", (lambda e: e.reciprocal(R[0:1, :], R[0:1, :])),
             inc=[(cv, 1)], wait=[(cs, cst)])
        cvt = flush(C, "vector")
        for g in range(2):
            p.op("tensor", (lambda e, g=g: e.matmul(
                [t["p6"], t["p7"]][g][:, :], t["onesr_f"][:, :], R[0:1, 512 * g:512 * (g + 1)],
                start=True, stop=True)), inc=[(ct, 1)] if g == 1 else (),
                 wait=[(cv, cvt)] if g == 0 else ())
        ctt = ct.n
        for g in range(2):
            p.op("vector", (lambda e, h=h, g=g: e.tensor_mul(
                t["sqB"][:, 0:512], oar(h)[:, 512 * g:512 * (g + 1)],
                [t["p6"], t["p7"]][g][:, :])), wait=[(ct, ctt), ld] if g == 0 else [ld])
            p.op("vector", (lambda e, g=g: e.tensor_scalar(
                t["sqB"][:, 0:512], t["sqB"][:, 0:512], t["onw"][:, 0:1], None, op0=ALU.mult)))
            p.op("vector", (lambda e, h=h, g=g: e.tensor_mul(
                oar(h)[:, 512 * g:512 * (g + 1)], t["sqB"][:, 0:512],
                t["ggtmp"][:, 512 * g:512 * (g + 1)])),
                 inc=[(cv, 1)] if g == 1 else ())
        prev_gg = cv.n
    opost_done = flush(C, "vector")

    barrier(C, "l0a_opost")

    pb4 = [t["p0"], t["p1"], t["p2"], t["p3"]]
    mm_done = {}
    ev_done = {}
    for o in range(nch):
        wb = [t["wsA"], t["wsB"]][o % 2]
        w = [(ct, mm_done[o - 2])] if o >= 2 else []
        ldt = p.dma("gpsimd", wb[:, 0:1024], Wos[o], wait=w)
        for g in range(2):
            pb = pb4[(2 * o + g) % 4]
            wmm = [ldt, (cv, opost_done)]
            if (2 * o + g) >= 4:
                wmm.append((cv, ev_done[2 * o + g - 4]))
            for c in range(8):
                p.op("tensor", (lambda e, c=c, o=o, g=g, pb=pb, wb=wb: e.matmul(
                    pb[:, :], wb[:, 128 * c:128 * (c + 1)],
                    ybf[:, 8192 + 1024 * c + 512 * g:8192 + 1024 * c + 512 * (g + 1)],
                    start=(c == 0), stop=(c == 7))),
                     inc=[(ct, 1)] if c == 7 else (), wait=wmm if c == 0 else ())
            mmt = ct.n
            p.op("vector", (lambda e, o=o, g=g, pb=pb: e.tensor_copy(
                KCV[:, 1024 * o + 512 * g:1024 * o + 512 * (g + 1)], pb[:, :])),
                 inc=[(cv, 1)], wait=[(ct, mmt)])
            ev_done[2 * o + g] = cv.n
        mm_done[o] = ct.n

    barrier(C, "l0a_wo")

    stats_ready = emit_ln_stats(C, lambda c: KCV[:, 1024 * c:1024 * (c + 1)], nch,
                                t["m2_B"], t["r2_B"])
    acc_tgt = {}
    sname = p.sem("dma_gpsimd", "sw")
    for c in range(nch):
        for g in range(2):
            i2 = (2 * c + g) % 2
            stg = t["tmp32e"] if i2 == 0 else t["e1"]
            wv = []
            if (2 * c + g) >= 2:
                wv.append((sname, acc_tgt[(2 * c + g) - 2]))
            sl = slice(1024 * c + 512 * g, 1024 * c + 512 * (g + 1))
            slg = slice(512 * g, 512 * (g + 1))
            p.op("vector", (lambda e, sl=sl, slg=slg: e.tensor_sub(
                t["sqA"][:, 0:512], KCV[:, sl], t["m2_B"][:, slg])), wait=wv)
            p.op("vector", (lambda e, slg=slg, stg=stg: e.tensor_mul(
                stg[:, :], t["sqA"][:, 0:512], t["r2_B"][:, slg])),
                 inc=[(cv, 1)])
            cvt = cv.n
            tgt = p.dma("gpsimd", x_work[128 * c:128 * (c + 1), 512 * g:512 * (g + 1)],
                        stg[:, :], wait=[(cv, cvt)], accum=True)
            acc_tgt[(2 * c + g)] = tgt[1]
    p.op("gpsimd", lambda e: e.nop(), wait=[(sname, sname.n)])
    rl = None
    for c in range(nch):
        rl = p.dma("sync", chunk(t["x_res"], c, T), x_work[128 * c:128 * (c + 1), :],
                   wait=[(sname, sname.n)] if c == 0 else ())
    p.op("sync", (lambda e: e.nop()), wait=[rl])
    barrier(C, "l0a_done")



def build_fused(T=1024):
    I32 = mybir.dt.int32
    nc = bass.Bass("TRN2", target_bir_lowering=False, debug=False,
                   detect_race_conditions=False)
    dram = {}

    def di(nm, sh, dt=F32):
        dram[nm] = nc.dram_tensor(nm, sh, dt, kind="ExternalInput").ap()
        return dram[nm]

    x_in = di("x_in", [2048, T])
    Wqs = di("Wqs", [8, 128, 2048]); Wks = di("Wks", [16, 128, 2048])
    Wvs = di("Wvs", [16, 128, 2048]); Wgs = di("Wgs", [8, 128, 2048])
    Wos = di("Wos", [16, 128, 1024])
    wsmall_d = di("wsmall_d", [128, 512]); convw_d = di("convw_d", [128, 160])
    acol_d = di("acol_d", [128, 8]); di("onw_d", [128, 1])
    di("role_d", [1, 2], I32)
    di("cumsL_d", [128, 64]); di("dupB_d", [64, 128])
    di("dupE_d", [128, 128]); di("dupO_d", [128, 128])
    di("sel63_d", [64, 1])
    di("m1c4_d", [128, 512], BF16); di("m2c_d", [128, 64], BF16)
    W0t = di("W0t", [2048, 2048]); di("d_biasc", [128, 16])
    wqs = di("wqs", [16, 128, 2048]); wos = di("wos", [16, 128, 2048])
    di("wkvp", [128, 4096]); di("d_bq", [128, 16]); di("d_bkv", [128, 2])
    di("d_bo", [128, 16]); di("d_role", [1, 2], I32)
    W1s = di("W1s", [64, 128, 2048]); W2s = di("W2s", [16, 2, 128, 4096])
    di("d_ones_bf", [128, 128], BF16); di("d_ones_f", [128, 128])
    di("d_onesr_f", [1, 128]); di("d_id_bf", [128, 128], BF16)
    di("d_id_f", [128, 128])

    x_out = nc.dram_tensor("x_out", [2048, T], F32, kind="ExternalOutput").ap()
    x_work = nc.dram_tensor("x_work", [2048, T], F32, kind="Internal").ap()
    gg_work = nc.dram_tensor("gg_work", [8, 128, 1024], BF16, kind="Internal").ap()
    sh_halo = nc.dram_tensor("sh_halo", [2, 128, 128], BF16, kind="Internal", addr_space="Shared").ap()
    sh_smid = nc.dram_tensor("sh_smid", [1, 128, 1024], F32, kind="Internal", addr_space="Shared").ap()
    sh_flagB = nc.dram_tensor("sh_flagB", [4, 8], I32, kind="Internal", addr_space="Shared").ap()
    sh_kv = nc.dram_tensor("sh_kv", [2, 256, 1024], BF16, kind="Internal", addr_space="Shared").ap()
    sh_flag = nc.dram_tensor("sh_flag", [2, 8], I32, kind="Internal", addr_space="Shared").ap()

    with ExitStack() as ctx:
        prog = Prog(nc)
        C = Ctx(nc, prog)
        alloc_common(C, ctx, T)
        C.sb(ctx, "_ovl", [128, 9856], F32)
        C.ovl_cap = 9856 * 4
        C.ovl_skip = {"sroleA", "sflagA", "spollA", "sroleB", "sflagB", "spollB"}
        C.ovl_on = True
        C.ovl_cursor = 0; alloc_l0attn(C, ctx)
        ovl_l0a = C.ovl_cursor
        C.ovl_cursor = 0; alloc_l0ffn(C, ctx)
        ovl_l0f = C.ovl_cursor
        C.ovl_cursor = 0; alloc_l1attn(C, ctx)
        ovl_l1a = C.ovl_cursor
        C.ovl_cursor = 0; alloc_l1ffn(C, ctx)
        ovl_l1f = C.ovl_cursor
        C.ovl_on = False
        assert max(ovl_l0a, ovl_l0f, ovl_l1a, ovl_l1f) <= C.ovl_cap
        t = C.t
        p = prog

        def fence(tag):
            waits = [(p.sem(nm), p.sem(nm).n) for nm in ("cv", "cs", "ct", "cp")]
            p.dma("gpsimd", t["tok8"][0:1, 0:8], x_work[0:1, 0:8], wait=waits)
            p.dma("sync", t["tok8"][0:1, 8:16], x_work[0:1, 8:16], wait=waits)

        def loads_done(tag):
            for q in ("gpsimd", "sync"):
                s = p.sem("dma_" + q, "sw" if q == "gpsimd" else "hw")
                p.op(q if q == "gpsimd" else "sync", lambda e: e.nop(), wait=[(s, s.n)])
            barrier(C, tag)

        for c in range(16):
            p.dma("gpsimd", chunk(t["x_res"], c, T), x_in[128 * c:128 * (c + 1), :])
        p.dma("gpsimd", t["wsmallb"][:, :], dram["wsmall_d"][:, 0:256])
        for nm, tile in [("wsmall_d", "wsmall"), ("convw_d", "convw"), ("acol_d", "acol"),
                         ("onw_d", "onw"), ("role_d", "sroleB"), ("d_ones_bf", "ones128_bf"),
                         ("d_ones_f", "ones128_f"), ("d_onesr_f", "onesr_f"),
                         ("d_id_bf", "ident_bf"), ("d_id_f", "ident_f"),
                         ("cumsL_d", "cumsL"), ("dupB_d", "dupB"), ("dupE_d", "dupE"),
                         ("dupO_d", "dupO"), ("sel63_d", "sel63"), ("m1c4_d", "m1c4"),
                         ("m2c_d", "m2c")]:
            p.dma("gpsimd", t[tile][:, :], dram[nm])
        p.op("vector", lambda e: e.memset(t["xbf"][:, :], 0.0))
        p.op("vector", lambda e: e.memset(t["constcol"][:, 0:1], 1e-5))
        p.op("vector", lambda e: e.memset(t["constcol"][:, 1:2], 1.0))
        p.op("vector", lambda e: e.memset(t["sflagB"][:, :], 21))
        loads_done("loads0")

        emit_l0attn_pre(C, Wqs, Wks, Wvs, Wgs, wsmall_d, convw_d, acol_d,
                        x_work, gg_work, sh_halo, sh_flagB)
        emit_l0attn_scan(C, sh_smid, sh_flagB)
        emit_l0attn_post(C, Wos, gg_work, x_work)

        fence("b")
        p.dma("gpsimd", t["bias_cols"][:, :], dram["d_biasc"])
        loads_done("loadsB")
        emit_l0ffn(C, lambda c: chunk(t["x_res"], c, T), W0t)

        fence("c")
        p.dma("gpsimd", t["wkv"][:, :], dram["wkvp"])
        p.dma("gpsimd", t["biasq"][:, :], dram["d_bq"])
        p.dma("gpsimd", t["biaskv"][:, :], dram["d_bkv"])
        p.dma("gpsimd", t["biaso"][:, :], dram["d_bo"])
        p.dma("gpsimd", t["sroleA"][:, :], dram["d_role"])
        p.op("vector", lambda e: e.memset(t["sflagA"][:, :], 7))
        loads_done("loadsC")
        emit_l1attn(C, wqs, wos, dram["d_role"], sh_kv, sh_flag, tag=0, dbg=None)

        fence("d")
        loads_done("loadsD")
        emit_l1ffn(C, W1s, W2s, x_out)

        sfin = p.sem("dma_gpsimd", "sw")
        p.op("gpsimd", lambda e: e.nop(), wait=[(sfin, sfin.n)])
        p.op("sync", lambda e: e.nop())
        prog.emit(ctx)
    return nc


import ml_dtypes
from concourse.bass_utils import run_bass_kernel_spmd

T = 1024


def kernel(**kw):
    f32 = np.float32
    x = np.asarray(kw["L0_attn_x"], f32)
    cores = list(range(8))

    def slab(W, no):
        return np.ascontiguousarray(
            W.reshape(no, 128, 16, 128).transpose(0, 3, 2, 1).reshape(no, 128, 2048))

    Wb = np.asarray(kw["L0_attn_Wb"], f32); Wa = np.asarray(kw["L0_attn_Wa"], f32)
    wsmall = np.zeros((128, 512), f32)
    wsmall[:, 0:256] = Wb.reshape(16, 16, 128).transpose(2, 1, 0).reshape(128, 256)
    wsmall[:, 256:384] = Wa.reshape(8, 16, 128).transpose(2, 1, 0).reshape(128, 128)
    cq = np.asarray(kw["L0_attn_conv_q"], f32); ck = np.asarray(kw["L0_attn_conv_k"], f32)
    cvv = np.asarray(kw["L0_attn_conv_v"], f32)
    convw = np.zeros((128, 160), f32)
    convw[:, 0:32] = cq[:, ::-1].reshape(8, 128, 4).transpose(1, 0, 2).reshape(128, 32)
    convw[:, 32:96] = ck[:, ::-1].reshape(16, 128, 4).transpose(1, 0, 2).reshape(128, 64)
    convw[:, 96:160] = cvv[:, ::-1].reshape(16, 128, 4).transpose(1, 0, 2).reshape(128, 64)
    acol = np.zeros((128, 8), f32)
    acol[0:8, 0] = np.asarray(kw["L0_attn_A_log"], f32)
    acol[0:8, 1] = np.asarray(kw["L0_attn_dt_bias"], f32)
    sidx = np.arange(64)[:, None]; tidx = np.arange(128)[None, :]
    wk = np.asarray(kw["L1_attn_wk"], f32); wv = np.asarray(kw["L1_attn_wv"], f32)
    wks = wk.reshape(16, 128, 128).transpose(1, 0, 2).reshape(128, 2048)
    wvs = wv.reshape(16, 128, 128).transpose(1, 0, 2).reshape(128, 2048)
    wq1 = np.asarray(kw["L1_attn_wq"], f32); wo1 = np.asarray(kw["L1_attn_wo"], f32)
    W1 = np.asarray(kw["L1_ffn_W1"], f32); W2 = np.asarray(kw["L1_ffn_W2"], f32)
    common = {
        "Wqs": slab(np.asarray(kw["L0_attn_Wq"], f32), 8),
        "Wks": slab(np.asarray(kw["L0_attn_Wk"], f32), 16),
        "Wvs": slab(np.asarray(kw["L0_attn_Wv"], f32), 16),
        "Wgs": slab(np.asarray(kw["L0_attn_Wg"], f32), 8),
        "Wos": np.ascontiguousarray(np.asarray(kw["L0_attn_Wo"], f32)
                                    .reshape(16, 128, 8, 128).transpose(0, 3, 2, 1)
                                    .reshape(16, 128, 1024)),
        "wsmall_d": wsmall, "convw_d": convw, "acol_d": acol,
        "onw_d": np.ascontiguousarray(np.asarray(kw["L0_attn_o_norm_w"], f32).reshape(128, 1)),
        "cumsL_d": np.tile((sidx <= np.arange(64)[None, :]).astype(f32), (2, 1)),
        "dupB_d": (tidx // 2 == sidx).astype(f32),
        "dupE_d": np.tile((tidx == 2 * sidx).astype(f32), (2, 1)),
        "dupO_d": np.tile((tidx == 2 * sidx + 1).astype(f32), (2, 1)),
        "sel63_d": (sidx == 63).astype(f32),
        "m1c4_d": np.tile(-np.triu(np.ones((128, 128), f32), 1), (1, 4)).astype(ml_dtypes.bfloat16),
        "m2c_d": (np.arange(128)[:, None] <= 2 * np.arange(64)[None, :] + 1).astype(ml_dtypes.bfloat16),
        "W0t": np.ascontiguousarray(np.asarray(kw["L0_ffn_weight"], f32).T),
        "d_biasc": np.ascontiguousarray(np.asarray(kw["L0_ffn_bias"], f32).reshape(16, 128).T),
        "wqs": np.ascontiguousarray(wq1.reshape(16, 128, 16, 128).transpose(2, 1, 0, 3).reshape(16, 128, 2048)),
        "wos": np.ascontiguousarray(wo1.reshape(16, 128, 16, 128).transpose(2, 1, 0, 3).reshape(16, 128, 2048)),
        "wkvp": np.ascontiguousarray(np.concatenate([wks, wvs], axis=1)),
        "d_bq": np.ascontiguousarray(np.asarray(kw["L1_attn_bq"], f32).reshape(16, 128).T),
        "d_bkv": np.ascontiguousarray(np.stack([np.asarray(kw["L1_attn_bk"], f32),
                                                np.asarray(kw["L1_attn_bv"], f32)], 1)),
        "d_bo": np.ascontiguousarray(np.asarray(kw["L1_attn_bo"], f32).reshape(16, 128).T),
        "W1s": np.ascontiguousarray(W1.reshape(64, 128, 16, 128).transpose(0, 3, 2, 1).reshape(64, 128, 2048)),
        "W2s": np.ascontiguousarray(W2.reshape(16, 128, 2, 32, 128).transpose(0, 2, 4, 3, 1).reshape(16, 2, 128, 4096)),
        "d_ones_bf": np.ones((128, 128), ml_dtypes.bfloat16),
        "d_ones_f": np.ones((128, 128), f32),
        "d_onesr_f": np.ones((1, 128), f32),
        "d_id_bf": np.eye(128).astype(ml_dtypes.bfloat16),
        "d_id_f": np.eye(128).astype(f32),
    }

    in_maps = []
    for c in cores:
        m = dict(common)
        m["x_in"] = np.ascontiguousarray(x[c // 2, 1024 * (c % 2):1024 * (c % 2) + 1024, :].T)
        m["role_d"] = np.array([[c % 2, 33]], np.int32)
        m["d_role"] = np.array([[c % 2, 0]], np.int32)
        in_maps.append(m)

    nc = build_fused(T)
    res = run_bass_kernel_spmd(nc, in_maps, core_ids=cores)

    out = np.empty((4, 2048, 2048), f32)
    for c in cores:
        out[c // 2, 1024 * (c % 2):1024 * (c % 2) + 1024, :] = np.asarray(res.results[c]["x_out"]).T
    return out


# version 03
# speedup vs baseline: 1.1111x; 1.1111x over previous
import numpy as np
from contextlib import ExitStack

import concourse.bass as bass
import concourse.mybir as mybir

F32 = mybir.dt.float32
BF16 = mybir.dt.bfloat16
AF = mybir.ActivationFunctionType
ALU = mybir.AluOpType
AX = mybir.AxisListType

MAGIC = 1.5 * 2.0 ** 23
ENGINES = ("sync", "gpsimd", "vector", "scalar", "tensor")


class SemT:
    __slots__ = ("name", "n", "kind")

    def __init__(self, name, kind):
        self.name = name
        self.n = 0
        self.kind = kind


class Op:
    __slots__ = ("eng", "fn", "incs", "waits", "scope")

    def __init__(self, eng, fn, incs, waits, scope=None):
        self.eng = eng
        self.fn = fn
        self.incs = incs
        self.waits = waits
        self.scope = scope


class Prog:
    def __init__(self, nc):
        self.nc = nc
        self.ops = []
        self.sems = {}

    def sem(self, name, kind="c"):
        s = self.sems.get(name)
        if s is None:
            s = self.sems[name] = SemT(name, kind)
        assert s.kind == kind, (name, s.kind, kind)
        return s

    def op(self, eng, fn, inc=(), wait=()):
        assert eng in ENGINES
        self.ops.append(Op(eng, fn, tuple(inc), tuple(wait), getattr(self, "cur_scope", None)))
        for s, k in inc:
            s.n += k

    def raw(self, eng, fn, inc=(), wait=()):
        self.ops.append(Op(eng, fn, (), tuple(wait), getattr(self, "cur_scope", None)))
        for s, k in inc:
            s.n += k

    def now(self, *sems):
        return tuple((s, s.n) for s in sems)

    def dma(self, queue, out_ap, in_ap, wait=(), accum=False):
        kind = "sw" if queue == "gpsimd" else "hw"
        s = self.sem("dma_" + queue, kind)

        def fn(e, out_ap=out_ap, in_ap=in_ap):
            if accum:
                return e.dma_start(out=out_ap, in_=in_ap, accum_op=mybir.AluOpType.add)
            return e.dma_start(out=out_ap, in_=in_ap)
        self.op(queue, fn, inc=[(s, 16)], wait=wait)
        return (s, s.n)

    def emit(self, ctx):
        nc = self.nc
        real = {}
        for name, s in self.sems.items():
            assert s.n < 60000, (name, s.n)
            real[name] = ctx.enter_context(nc.semaphore(name))
        self.real = real
        block = ctx.enter_context(nc.Block())
        prog = self

        def mk(eng_name):
            def run(e):
                for o in prog.ops:
                    if o.eng != eng_name:
                        continue
                    for s, tgt in o.waits:
                        if tgt > 0:
                            e.wait_ge(real[s.name], tgt)
                    if o.scope:
                        with nc.named_scope(o.scope):
                            inst = o.fn(e)
                    else:
                        inst = o.fn(e)
                    for s, k in o.incs:
                        inst.then_inc(real[s.name], k)
            return run

        for eng_name in ENGINES:
            getattr(block, eng_name)(mk(eng_name))
        return real



def T_(t):
    return t[:, :]


class Ctx:

    def __init__(self, nc, prog):
        self.nc = nc
        self.p = prog
        self.t = {}

    def sb(self, ctx, name, shape, dt=F32):
        if getattr(self, "ovl_on", False) and name not in getattr(self, "ovl_skip", ()):
            epe = {F32: 4, BF16: 2}.get(dt, 4)
            off = (self.ovl_cursor + 63) // 64 * 64
            nbytes = shape[1] * epe
            base = self.t["_ovl"]
            ap = base[:, :].bitcast(dt)
            h = ap[0:shape[0], off // epe:off // epe + shape[1]]
            self.ovl_cursor = off + nbytes
            assert self.ovl_cursor <= self.ovl_cap, (
                f"overlay overflow at {name}: {self.ovl_cursor}")
            self.t[name] = h
            return h
        h = ctx.enter_context(self.nc.sbuf_tensor(name, shape, dt))
        self.t[name] = h
        return h

    def ps(self, ctx, name, shape, dt=F32):
        h = ctx.enter_context(self.nc.psum_tensor(name, shape, dt))
        self.t[name] = h
        return h



def alloc_common(C, ctx, T):
    C.T = T
    C.sb(ctx, "x_res", [128, 16 * T], F32)
    C.sb(ctx, "xbf", [128, 16 * T], BF16)
    C.sb(ctx, "ybf", [128, 16 * T], BF16)
    C.sb(ctx, "sqA", [128, T], BF16)
    C.sb(ctx, "sqB", [128, T], BF16)
    C.sb(ctx, "maxsq", [128, T], BF16)
    C.sb(ctx, "tmp32e", [128, 512], F32)
    C.sb(ctx, "e1", [128, 512], F32)
    C.sb(ctx, "rows32a", [128, T], F32)
    C.sb(ctx, "rows32b", [128, T], F32)
    C.sb(ctx, "m_B", [128, T], BF16)
    C.sb(ctx, "rstd_B", [128, T], F32)
    C.sb(ctx, "m2_B", [128, T], BF16)
    C.sb(ctx, "r2_B", [128, T], F32)
    C.sb(ctx, "F_B", [128, T], F32)
    C.sb(ctx, "AS_B", [128, T], F32)
    C.sb(ctx, "tok8", [128, 64], F32)
    C.sb(ctx, "colmisc", [128, 8], F32)
    C.sb(ctx, "constcol", [128, 4], F32)
    C.t["tmp32b"] = C.t["wabs"] if "wabs" in C.t else None
    C.sb(ctx, "ones128_bf", [128, 128], BF16)
    C.sb(ctx, "ones128_f", [128, 128], F32)
    C.sb(ctx, "onesr_f", [1, 128], F32)
    C.sb(ctx, "ident_bf", [128, 128], BF16)
    C.sb(ctx, "ident_f", [128, 128], F32)
    for i in range(8):
        C.ps(ctx, f"p{i}", [128, 512], F32)


def barrier(C, tag=""):
    p = C.p
    b = p.sem("bar")
    for eng in ENGINES:
        if eng != "sync":
            p.op(eng, lambda e: e.drain())
        p.op(eng, lambda e: e.nop(), inc=[(b, 1)])
    tgt = b.n
    for eng in ENGINES:
        p.op(eng, lambda e: e.nop(), wait=[(b, tgt)])


def chunk(tile, c, T):
    return tile[:, c * T: (c + 1) * T]


ESEM = {"vector": "cv", "scalar": "cs", "tensor": "ct", "gpsimd": "cp"}


def flush(C, eng):
    p = C.p
    s = p.sem(ESEM[eng])
    p.op(eng, lambda e: e.drain(), wait=[(s, s.n)])
    p.op(eng, lambda e: e.nop(), inc=[(s, 1)])
    return s.n



def emit_ln_stats(C, src, nch, out_m, out_r, cast_src=None):
    p, t, T = C.p, C.t, C.T
    cv, cs, ct = p.sem("cv"), p.sem("cs"), p.sem("ct")
    ones = t["ones128_bf"]
    banks = [t["p0"], t["p1"], t["p2"], t["p3"]]
    NG = T // 512
    sq = [t["sqA"], t["sqB"]]
    mm_done = {}
    for c in range(nch):
        if cast_src is not None:
            p.op("vector", (lambda e, c=c: e.tensor_copy(src(c), cast_src(c))))
        w = [(ct, mm_done[c - 2])] if c >= 2 else []
        p.op("vector", (lambda e, c=c: e.tensor_mul(sq[c % 2][:, :], src(c), src(c))),
             inc=[(cv, 1)], wait=w)
        cvt = cv.n
        for g in range(NG):
            p.op("tensor", (lambda e, c=c, g=g: e.matmul(
                banks[g][:, :], ones[:, :], src(c)[:, 512 * g:512 * (g + 1)],
                start=(c == 0), stop=(c == nch - 1))),
                 wait=[(cv, cvt)] if g == 0 else ())
        for g in range(NG):
            p.op("tensor", (lambda e, c=c, g=g: e.matmul(
                banks[NG + g][:, :], ones[:, :], sq[c % 2][:, 512 * g:512 * (g + 1)],
                start=(c == 0), stop=(c == nch - 1))),
                 inc=[(ct, 1)] if g == NG - 1 else ())
        mm_done[c] = ct.n
    stats_done = ct.n
    for g in range(NG):
        sl = slice(512 * g, 512 * (g + 1))
        p.op("vector", (lambda e, g=g, sl=sl: e.tensor_scalar(
            out_m[:, sl], banks[g][:, :], 1.0 / (128 * nch), None, op0=ALU.mult)),
             wait=[(ct, stats_done)] if g == 0 else ())
        p.op("vector", (lambda e, g=g, sl=sl: e.tensor_scalar(
            t["rows32a"][:, sl], banks[NG + g][:, :], 1.0 / (128 * nch), None, op0=ALU.mult)))
        p.op("vector", (lambda e, sl=sl: e.tensor_mul(
            t["rows32b"][:, sl], out_m[:, sl], out_m[:, sl])))
        p.op("vector", (lambda e, sl=sl: e.tensor_sub(
            t["rows32a"][:, sl], t["rows32a"][:, sl], t["rows32b"][:, sl])),
             inc=[(cv, 1)])
        cvt = cv.n
        p.op("scalar", (lambda e, sl=sl: e.activation(
            t["rows32b"][:, sl], t["rows32a"][:, sl], AF.Sqrt, bias=t["constcol"][:, 0:1])),
             inc=[(cs, 1)], wait=[(cv, cvt)])
        cst = cs.n
        p.op("vector", (lambda e, sl=sl: e.reciprocal(out_r[:, sl], t["rows32b"][:, sl])),
             inc=[(cv, 1)], wait=[(cs, cst)])
    return cv.n



def alloc_l0ffn(C, ctx):
    C.sb(ctx, "wraw0", [128, 512], F32)
    C.sb(ctx, "wraw1", [128, 512], F32)
    C.sb(ctx, "wabs", [128, 512], F32)
    C.sb(ctx, "tw", [128, 16 * 1024], BF16)
    C.sb(ctx, "bias_cols", [128, 16], F32)


def emit_l0ffn(C, get_x, W0t_dram, nch=16):
    p, t, T = C.p, C.t, C.T
    cv, cs, ct, cp = p.sem("cv"), p.sem("cs"), p.sem("ct"), p.sem("cp")
    xbf, ybf = t["xbf"], t["ybf"]
    NG = T // 512
    JT = T // 128
    wraw = [t["wraw0"], t["wraw1"]]
    wabs, tw = t["wabs"], t["tw"]

    abs_done, mm_done = {}, {}
    for hc in range(4 * nch):
        c, q = hc // 4, hc % 4
        w = [(cs, abs_done[hc - 2])] if hc >= 2 else []
        ldt = p.dma("sync", wraw[hc % 2][:, :],
                    W0t_dram[128 * c:128 * (c + 1), 512 * q:512 * (q + 1)], wait=w)
        w2 = [ldt] + ([(ct, mm_done[hc - 1])] if hc >= 1 else [])
        p.op("scalar", (lambda e, hc=hc: e.activation(wabs[:, :], wraw[hc % 2][:, :], AF.Abs)),
             inc=[(cs, 1)], wait=w2)
        abs_done[hc] = cs.n
        cst = cs.n
        p.op("tensor", (lambda e, hc=hc: e.matmul(
            t["p5"][:, :], t["ones128_f"][:, :], wabs[:, :],
            start=(hc == 0), stop=(hc == 4 * nch - 1))),
             inc=[(ct, 1)], wait=[(cs, cst)])
        mm_done[hc] = ct.n
    alpha_mm = ct.n
    p.op("vector", (lambda e: e.tensor_copy(t["rows32a"][0:1, 0:512], t["p5"][0:1, :])),
         inc=[(cv, 1)], wait=[(ct, alpha_mm)])
    p.op("vector", (lambda e: e.tensor_reduce(
        t["rows32a"][0:1, 520:521], t["rows32a"][0:1, 0:512], axis=AX.X, op=ALU.add)),
         inc=[(cv, 1)], wait=[(cv, cv.n)])
    p.op("vector", (lambda e: e.tensor_scalar(
        t["rows32a"][0:1, 1:2], t["rows32a"][0:1, 520:521], 1.0 / (2048.0 * 2048.0), 1e-5,
        op0=ALU.mult, op1=ALU.max)), inc=[(cv, 1)], wait=[(cv, cv.n)])
    p.op("vector", (lambda e: e.reciprocal(t["rows32a"][0:1, 2:3], t["rows32a"][0:1, 1:2])),
         inc=[(cv, 1)], wait=[(cv, cv.n)])
    cvt = flush(C, "vector")
    p.op("tensor", (lambda e: e.matmul(t["p4"][:, 0:2], t["onesr_f"][:, :],
                                       t["rows32a"][0:1, 1:3], start=True, stop=True)),
         inc=[(ct, 1)], wait=[(cv, cvt)])
    ctt = ct.n
    p.op("vector", (lambda e: e.tensor_copy(t["colmisc"][:, 0:2], t["p4"][:, 0:2])),
         inc=[(cv, 1)], wait=[(ct, ctt)])

    barrier(C, "alpha")

    emit_ln_stats(C, lambda c: chunk(xbf, c, T), nch, t["m_B"], t["rstd_B"],
                  cast_src=get_x)
    for c in range(nch):
        p.op("vector", (lambda e, c=c: e.tensor_sub(chunk(xbf, c, T), chunk(xbf, c, T), t["m_B"][:, :])))
        p.op("vector", (lambda e, c=c: e.tensor_mul(t["sqA"][:, :], chunk(xbf, c, T), chunk(xbf, c, T))))
        if c == 0:
            p.op("vector", (lambda e: e.tensor_copy(t["maxsq"][:, :], t["sqA"][:, :])), inc=[(cv, 1)])
        else:
            p.op("vector", (lambda e: e.tensor_max(t["maxsq"][:, :], t["maxsq"][:, :], t["sqA"][:, :])),
                 inc=[(cv, 1)])
    ln_done = cv.n

    red_done = {}
    for j in range(JT):
        pb = t[f"p{6 + j % 2}"][:, :].bitcast(BF16)
        w = [(cv, ln_done)] if j < 2 else [(cv, red_done[j - 2])]
        p.op("tensor", (lambda e, j=j, pb=pb: e.transpose(
            pb[:, 0:128], t["maxsq"][:, 128 * j:128 * (j + 1)], t["ident_bf"][:, :])),
             inc=[(ct, 1)], wait=w)
        ctt = ct.n
        p.op("vector", (lambda e, j=j, pb=pb: e.tensor_reduce(
            t["tok8"][:, j:j + 1], pb[:, 0:128], axis=AX.X, op=ALU.max)),
             inc=[(cv, 1)], wait=[(ct, ctt)])
        red_done[j] = cv.n
    cp_done = {}
    for j in range(JT):
        pb = [t["p4"], t["p5"]][j % 2]
        w = [] if j < 2 else [(cv, cp_done[j - 2])]
        p.op("tensor", (lambda e, j=j, pb=pb: e.transpose(
            pb[:, 0:128], t["rstd_B"][:, 128 * j:128 * (j + 1)], t["ident_f"][:, :])),
             inc=[(ct, 1)], wait=w)
        ctt = ct.n
        p.op("vector", (lambda e, j=j, pb=pb: e.tensor_copy(
            t["tok8"][:, 8 + j:9 + j], pb[:, 0:1])),
             inc=[(cv, 1)], wait=[(ct, ctt)])
        cp_done[j] = cv.n
    tok_ready = flush(C, "vector")
    p.op("scalar", (lambda e: e.activation(t["tok8"][:, 16:16 + JT], t["tok8"][:, 0:JT], AF.Sqrt)),
         inc=[(cs, 1)], wait=[(cv, tok_ready)])
    cst = flush(C, "scalar")
    p.op("vector", (lambda e: e.tensor_mul(t["tok8"][:, 24:24 + JT], t["tok8"][:, 8:8 + JT],
                                           t["tok8"][:, 16:16 + JT])), inc=[(cv, 1)],
         wait=[(cs, cst), (cv, tok_ready)])
    p.op("vector", (lambda e: e.tensor_scalar(t["tok8"][:, 32:32 + JT], t["tok8"][:, 24:24 + JT],
                                              1e-5, 1.0 / 127.0, op0=ALU.max, op1=ALU.mult)),
         inc=[(cv, 1)], wait=[(cv, cv.n)])
    p.op("vector", (lambda e: e.reciprocal(t["tok8"][:, 40:40 + JT], t["tok8"][:, 32:32 + JT])),
         inc=[(cv, 1)], wait=[(cv, cv.n)])
    p.op("vector", (lambda e: e.tensor_mul(t["tok8"][:, 48:48 + JT], t["tok8"][:, 8:8 + JT],
                                           t["tok8"][:, 40:40 + JT])), inc=[(cv, 1)], wait=[(cv, cv.n)])
    p.op("vector", (lambda e: e.tensor_scalar(t["tok8"][:, 56:56 + JT], t["tok8"][:, 32:32 + JT],
                                              t["colmisc"][:, 0:1], None, op0=ALU.mult)),
         inc=[(cv, 1)], wait=[(cv, cv.n)])
    ftok_ready = flush(C, "vector")
    ev_done = {}
    for j in range(JT):
        pb = [t["p4"], t["p5"]][j % 2]
        w = [(cv, ftok_ready)] if j < 2 else [(cv, ev_done[j - 2])]
        p.op("tensor", (lambda e, j=j, pb=pb: e.matmul(
            pb[0:1, 0:128], t["tok8"][:, 48 + j:49 + j], t["ident_f"][:, :],
            start=True, stop=True)), wait=w)
        p.op("tensor", (lambda e, j=j, pb=pb: e.matmul(
            pb[0:1, 128:256], t["tok8"][:, 56 + j:57 + j], t["ident_f"][:, :],
            start=True, stop=True)), inc=[(ct, 1)])
        ctt = ct.n
        p.op("vector", (lambda e, j=j, pb=pb: e.tensor_copy(
            t["rows32a"][0:1, 128 * j:128 * (j + 1)], pb[0:1, 0:128])), wait=[(ct, ctt)])
        p.op("vector", (lambda e, j=j, pb=pb: e.tensor_copy(
            t["rows32b"][0:1, 128 * j:128 * (j + 1)], pb[0:1, 128:256])),
             inc=[(cv, 1)])
        ev_done[j] = cv.n
    rows_ready = flush(C, "vector")
    bc_done = {}
    for g in range(NG):
        pb = [t["p4"], t["p5"]][g % 2]
        w = [(cv, rows_ready)] if g < 2 else [(cv, bc_done[g - 2])]
        p.op("tensor", (lambda e, g=g, pb=pb: e.matmul(
            pb[:, :], t["onesr_f"][:, :], t["rows32a"][0:1, 512 * g:512 * (g + 1)],
            start=True, stop=True)), inc=[(ct, 1)], wait=w)
        ctt = ct.n
        p.op("vector", (lambda e, g=g, pb=pb: e.tensor_copy(
            t["F_B"][:, 512 * g:512 * (g + 1)], pb[:, :])), inc=[(cv, 1)], wait=[(ct, ctt)])
        bc_done[g] = cv.n
    for g in range(NG):
        pb = [t["p4"], t["p5"]][g % 2]
        w = [(cv, bc_done[g])] if g >= NG - 2 else [(cv, bc_done[g])]
        p.op("tensor", (lambda e, g=g, pb=pb: e.matmul(
            pb[:, :], t["onesr_f"][:, :], t["rows32b"][0:1, 512 * g:512 * (g + 1)],
            start=True, stop=True)), inc=[(ct, 1)], wait=w)
        ctt = ct.n
        p.op("vector", (lambda e, g=g, pb=pb: e.tensor_copy(
            t["AS_B"][:, 512 * g:512 * (g + 1)], pb[:, :])), inc=[(cv, 1)], wait=[(ct, ctt)])

    barrier(C, "scales")

    for c in range(nch):
        for g in range(NG):
            sl = slice(T * c + 512 * g, T * c + 512 * (g + 1))
            p.op("vector", (lambda e, sl=sl, g=g: e.tensor_mul(
                t["tmp32e"][:, :], xbf[:, sl], t["F_B"][:, 512 * g:512 * (g + 1)])))
            p.op("vector", (lambda e, sl=sl: e.tensor_scalar(
                xbf[:, sl], t["tmp32e"][:, :], MAGIC, MAGIC, op0=ALU.add, op1=ALU.subtract)),
                 inc=[(cv, 1)])
    qx_done = cv.n

    barrier(C, "qx")

    pbanks = [t["p0"], t["p1"], t["p2"], t["p3"]]
    evac_done = {}
    i = 0
    for po in range(2):
        q_done = {}
        for hc in range(2 * nch):
            c, q = hc // 2, hc % 2
            w = [(cv, q_done[hc - 2])] if hc >= 2 else []
            if po == 1 and hc == 0:
                w = w + [(ct, p.sem("ct").n)]
            ldt = p.dma("sync", wraw[hc % 2][:, :],
                        W0t_dram[128 * c:128 * (c + 1), 1024 * po + 512 * q:1024 * po + 512 * (q + 1)],
                        wait=w)
            p.op("vector", (lambda e, hc=hc: e.tensor_scalar(
                wabs[:, :], wraw[hc % 2][:, :], t["colmisc"][:, 1:2], MAGIC,
                op0=ALU.mult, op1=ALU.add)), wait=[ldt])
            p.op("vector", (lambda e: e.tensor_scalar(
                wabs[:, :], wabs[:, :], MAGIC, 1.0, op0=ALU.subtract, op1=ALU.min)))
            p.op("vector", (lambda e, c=c, q=q: e.tensor_scalar(
                tw[:, 1024 * c + 512 * q:1024 * c + 512 * (q + 1)], wabs[:, :], -1.0, None, op0=ALU.max)),
                 inc=[(cv, 1)])
            q_done[hc] = cv.n
        qw_done = cv.n
        for oo in range(8):
            o = 8 * po + oo
            for g in range(NG):
                pb = pbanks[i % 4]
                w = [(cv, max(qx_done, qw_done))] if i % 4 == i or i < 4 else []
                w = [(cv, max(qx_done, qw_done))]
                if i >= 4:
                    w.append((cv, evac_done[i - 4]))
                for c in range(nch):
                    p.op("tensor", (lambda e, c=c, oo=oo, g=g, pb=pb: e.matmul(
                        pb[:, :], tw[:, 1024 * c + 128 * oo:1024 * c + 128 * oo + 128],
                        xbf[:, T * c + 512 * g:T * c + 512 * (g + 1)],
                        start=(c == 0), stop=(c == nch - 1))),
                         inc=[(ct, 1)] if c == nch - 1 else (), wait=w if c == 0 else ())
                mmt = ct.n
                p.op("vector", (lambda e, g=g, pb=pb: e.tensor_mul(
                    t["tmp32e"][:, :], pb[:, :], t["AS_B"][:, 512 * g:512 * (g + 1)])),
                     wait=[(ct, mmt)])
                p.op("vector", (lambda e, o=o, g=g: e.tensor_scalar(
                    ybf[:, T * o + 512 * g:T * o + 512 * (g + 1)], t["tmp32e"][:, :],
                    t["bias_cols"][:, o:o + 1], None, op0=ALU.add)), inc=[(cv, 1)])
                evac_done[i] = cv.n
                i += 1

    barrier(C, "l0ffn_mm")

    stats_ready = emit_ln_stats(C, lambda c: chunk(ybf, c, T), nch, t["m2_B"], t["r2_B"])
    for c in range(nch):
        for g in range(NG):
            slx = slice(T * c + 512 * g, T * c + 512 * (g + 1))
            slg = slice(512 * g, 512 * (g + 1))
            if c % 2 == 0:
                p.op("vector", (lambda e, slx=slx, slg=slg: e.tensor_sub(
                    t["sqA"][:, 0:512], ybf[:, slx], t["m2_B"][:, slg])))
                p.op("vector", (lambda e, slg=slg: e.tensor_mul(
                    t["tmp32e"][:, :], t["sqA"][:, 0:512], t["r2_B"][:, slg])))
                p.op("vector", (lambda e, slx=slx: e.tensor_add(
                    t["x_res"][:, slx], t["x_res"][:, slx], t["tmp32e"][:, :])),
                     inc=[(cv, 1)])
            else:
                p.op("gpsimd", (lambda e, slx=slx, slg=slg: e.tensor_sub(
                    t["sqB"][:, 0:512], ybf[:, slx], t["m2_B"][:, slg])),
                     wait=[(cv, stats_ready)] if (c == 1 and g == 0) else ())
                p.op("gpsimd", (lambda e, slg=slg: e.tensor_mul(
                    t["wabs"][:, :], t["sqB"][:, 0:512], t["r2_B"][:, slg])))
                p.op("gpsimd", (lambda e, slx=slx: e.tensor_add(
                    t["x_res"][:, slx], t["x_res"][:, slx], t["wabs"][:, :])),
                     inc=[(cp, 1)])
    barrier(C, "l0ffn_done")



GELU_A = 1.5957691216057308
GELU_B = GELU_A * 0.044715


def alloc_l1ffn(C, ctx):
    C.sb(ctx, "w1buf0", [128, 2048], BF16)
    C.sb(ctx, "w1buf1", [128, 2048], BF16)
    C.sb(ctx, "w2buf0", [128, 4096], BF16)
    C.sb(ctx, "w2buf1", [128, 4096], BF16)
    for i in range(2):
        C.sb(ctx, f"zbf{i}", [128, 512], BF16)
        C.sb(ctx, f"sqz{i}", [128, 512], BF16)
        C.sb(ctx, f"tbf{i}", [128, 512], BF16)
        C.sb(ctx, f"ubf{i}", [128, 512], BF16)
        C.sb(ctx, f"gbf{i}", [128, 512], BF16)


def emit_l1ffn(C, W1s_dram, W2s_dram, x_out_dram, nch=16):
    p, t, T = C.p, C.t, C.T
    cv, cs, ct, cp = p.sem("cv"), p.sem("cs"), p.sem("ct"), p.sem("cp")
    xbf, ybf = t["xbf"], t["ybf"]
    a_ap = t["x_res"][:, :].bitcast(BF16)
    p.cur_scope = "l1f_ln"

    emit_ln_stats(C, lambda c: chunk(xbf, c, T), nch, t["m_B"], t["rstd_B"],
                  cast_src=lambda c: chunk(t["x_res"], c, T))
    for c in range(nch):
        p.op("vector", (lambda e, c=c: e.tensor_sub(chunk(xbf, c, T), chunk(xbf, c, T), t["m_B"][:, :])))
        p.op("vector", (lambda e, c=c: e.tensor_mul(chunk(xbf, c, T), chunk(xbf, c, T), t["rstd_B"][:, :])),
             inc=[(cv, 1)] if c == nch - 1 else ())
    xln_done = cv.n
    sp = None
    for c in range(nch):
        sp = p.dma("sync", x_out_dram[128 * c:128 * (c + 1), :], chunk(t["x_res"], c, T))
    spill_done = sp

    barrier(C, "l1ffn_ln")
    p.cur_scope = "l1f_A"

    hb = [t["p0"], t["p1"]]
    yb = [t["p2"], t["p3"]]
    NHO = 64
    WQ = ["sync", "scalar"]
    for th in range(2):
        p.cur_scope = "l1f_A"
        if th == 0:
            h_mm_done = {}
            z_done = {}
            chain_done = {}
            sq_done = {}
        for ho in range(NHO):
            k = 2 * NHO * th + ho
            wb = [t["w1buf0"], t["w1buf1"]][ho % 2]
            w = [(ct, h_mm_done[k - 2])] if (k - 2) in h_mm_done else []
            if th == 1 and ho < 2:
                w = [(ct, h_mm_done[2 * NHO * 0 + NHO - 2 + ho])]
            ldt = p.dma(WQ[ho % 2], wb[:, :], W1s_dram[ho], wait=w)
            pb = hb[ho % 2]
            w2 = [ldt]
            if (k - 2) in z_done:
                w2.append((cv, z_done[k - 2]))
            if th == 1 and ho < 2:
                w2.append((cv, z_done[2 * NHO * 0 + NHO - 2 + ho]))
            if th == 0 and ho == 0:
                w2.append((cv, xln_done))
            for c in range(nch):
                xsl = slice(T * c + 512 * th, T * c + 512 * th + 512)
                p.op("tensor", (lambda e, c=c, pb=pb, wb=wb, xsl=xsl: e.matmul(
                    pb[:, :], wb[:, 128 * c:128 * (c + 1)], xbf[:, xsl],
                    start=(c == 0), stop=(c == nch - 1))),
                     inc=[(ct, 1)] if c == nch - 1 else (), wait=w2 if c == 0 else ())
            h_mm_done[k] = ct.n
            mmt = ct.n
            i2 = ho % 2
            zb, sqz, tb, ub, gb = t[f"zbf{i2}"], t[f"sqz{i2}"], t[f"tbf{i2}"], t[f"ubf{i2}"], t[f"gbf{i2}"]
            wA = [(ct, mmt)]
            if th == 0 and ho == 0:
                wA.append(spill_done)
            if (k - 2) in sq_done:
                wA.append((cs, sq_done[k - 2]))
            p.op("vector", (lambda e, pb=pb: e.tensor_scalar(
                t["e1"][:, :], pb[:, :], 10.0, MAGIC, op0=ALU.mult, op1=ALU.add)), wait=wA)
            p.op("vector", (lambda e, zb=zb: e.tensor_scalar(
                zb[:, :], t["e1"][:, :], MAGIC, 0.1, op0=ALU.subtract, op1=ALU.mult)),
                 inc=[(cv, 1)])
            z_done[k] = cv.n
            p.op("scalar", (lambda e, zb=zb, sqz=sqz: e.activation(sqz[:, :], zb[:, :], AF.Square)),
                 inc=[(cs, 1)], wait=[(cv, z_done[k])])
            sq_done[k] = cs.n
            p.op("vector", (lambda e, sqz=sqz, tb=tb: e.tensor_scalar(
                tb[:, :], sqz[:, :], GELU_B, GELU_A, op0=ALU.mult, op1=ALU.add)),
                 wait=[(cs, sq_done[k])])
            p.op("vector", (lambda e, tb=tb, ub=ub, zb=zb: e.tensor_mul(ub[:, :], tb[:, :], zb[:, :])),
                 inc=[(cv, 1)])
            cvt2 = cv.n
            p.op("scalar", (lambda e, ub=ub, gb=gb: e.activation(gb[:, :], ub[:, :], AF.Sigmoid)),
                 inc=[(cs, 1)], wait=[(cv, cvt2)])
            cst2 = cs.n
            p.op("vector", (lambda e, gb=gb, zb=zb, ho=ho: e.tensor_mul(
                a_ap[:, 512 * ho:512 * (ho + 1)], gb[:, :], zb[:, :])),
                 inc=[(cv, 1)], wait=[(cs, cst2)])
            chain_done[k] = cv.n
        a_done = cv.n

        p.cur_scope = "l1f_B"
        if th == 0:
            ev_done = {}
            qmm_done = {}
        w2q = [t["w2buf0"][:, 0:2048], t["w2buf0"][:, 2048:4096],
               t["w2buf1"][:, 0:2048], t["w2buf1"][:, 2048:4096]]
        for oc in range(nch):
            pb = yb[oc % 2]
            for qi in range(4):
                kb, hq = qi // 2, qi % 2
                qg = (2 * nch * th + oc) * 4 + qi
                wbuf = w2q[qg % 4]
                wld = []
                if (qg - 4) in qmm_done:
                    wld.append((ct, qmm_done[qg - 4]))
                ldt = p.dma(WQ[qg % 2], wbuf,
                            W2s_dram[oc, kb][:, 2048 * hq:2048 * (hq + 1)], wait=wld)
                wmm = [ldt]
                if qi == 0:
                    wmm.append((cv, a_done))
                    if (oc - 2) in ev_done:
                        wmm.append((cv, ev_done[oc - 2]))
                    elif th == 1 and (nch - 2 + oc) in ev_done:
                        wmm.append((cv, ev_done[nch - 2 + oc]))
                for hh in range(16):
                    ho = 32 * kb + 16 * hq + hh
                    p.op("tensor", (lambda e, hh=hh, ho=ho, pb=pb, wbuf=wbuf: e.matmul(
                        pb[:, :], wbuf[:, 128 * hh:128 * (hh + 1)],
                        a_ap[:, 512 * ho:512 * (ho + 1)],
                        start=(ho == 0), stop=(ho == NHO - 1))),
                         inc=[(ct, 1)] if hh == 15 else (), wait=wmm if hh == 0 else ())
                qmm_done[qg] = ct.n
            mmt = ct.n
            p.op("vector", (lambda e, oc=oc, pb=pb, th=th: e.tensor_copy(
                ybf[:, T * oc + 512 * th:T * oc + 512 * th + 512], pb[:, :])),
                 inc=[(cv, 1)], wait=[(ct, mmt)])
            ev_done[oc] = cv.n

    barrier(C, "l1ffn_mm")
    p.cur_scope = "l1f_post"

    stats_ready = emit_ln_stats(C, lambda c: chunk(ybf, c, T), nch, t["m2_B"], t["r2_B"])
    acc_tgt = {}
    spill_sync = spill_done
    for c in range(nch):
        for g in range(T // 512):
            slx = slice(T * c + 512 * g, T * c + 512 * (g + 1))
            slg = slice(512 * g, 512 * (g + 1))
            i2 = (2 * c + g) % 2
            stg = t["tmp32e"] if i2 == 0 else t["e1"]
            sname = p.sem("dma_gpsimd", "sw")
            wv = []
            if (2 * c + g) >= 2 and acc_tgt.get((2 * c + g) - 2):
                wv.append((sname, acc_tgt[(2 * c + g) - 2]))
            p.op("vector", (lambda e, slx=slx, slg=slg: e.tensor_sub(
                t["sqA"][:, 0:512], ybf[:, slx], t["m2_B"][:, slg])), wait=wv)
            p.op("vector", (lambda e, slg=slg, stg=stg: e.tensor_mul(
                stg[:, :], t["sqA"][:, 0:512], t["r2_B"][:, slg])),
                 inc=[(cv, 1)])
            cvt = cv.n
            tgt = p.dma("gpsimd", x_out_dram[128 * c:128 * (c + 1), 512 * g:512 * (g + 1)],
                        stg[:, :], wait=[(cv, cvt)] + ([spill_sync] if (c == 0 and g == 0) else []),
                        accum=True)
            acc_tgt[(2 * c + g)] = tgt[1]
    barrier(C, "l1ffn_done")




def alloc_l1attn(C, ctx):
    C.sb(ctx, "kfull", [128, 2048], BF16)
    C.sb(ctx, "vfull", [128, 2048], BF16)
    C.sb(ctx, "PT0", [128, 1024], BF16)
    C.sb(ctx, "PT1", [128, 1024], BF16)
    C.sb(ctx, "wbq0", [128, 2048], BF16)
    C.sb(ctx, "wbq1", [128, 2048], BF16)
    C.sb(ctx, "wkv", [128, 4096], BF16)
    C.sb(ctx, "biasq", [128, 16], F32)
    C.sb(ctx, "biaskv", [128, 2], F32)
    C.sb(ctx, "biaso", [128, 16], F32)
    C.sb(ctx, "negb", [128, 16], F32)
    C.sb(ctx, "sroleA", [1, 2], mybir.dt.int32)
    C.sb(ctx, "sflagA", [1, 8], mybir.dt.int32)
    C.sb(ctx, "spollA", [1, 8], mybir.dt.int32)


def emit_l1attn(C, wqs_dram, wos_dram, role_dram, sh_kv, sh_flag, nch=16, tag=0, dbg=None):
    p, t, T = C.p, C.t, C.T
    cv, cs, ct, cp = p.sem("cv"), p.sem("cs"), p.sem("ct"), p.sem("cp")
    xbf, ybf = t["xbf"], t["ybf"]
    qbf = ybf
    obf = xbf
    vT = t["AS_B"][:, :].bitcast(BF16)
    linvB = t["F_B"]
    MAGICF = 1234567.0

    emit_ln_stats(C, lambda c: chunk(xbf, c, T), nch, t["m_B"], t["rstd_B"],
                  cast_src=lambda c: chunk(t["x_res"], c, T))
    for c in range(nch):
        p.op("vector", (lambda e, c=c: e.tensor_sub(chunk(xbf, c, T), chunk(xbf, c, T), t["m_B"][:, :])))
        p.op("vector", (lambda e, c=c: e.tensor_mul(chunk(xbf, c, T), chunk(xbf, c, T), t["rstd_B"][:, :])),
             inc=[(cv, 1)] if c == nch - 1 else ())
    xln_done = cv.n

    barrier(C, f"l1a_ln{tag}")

    pb4 = [t["p0"], t["p1"], t["p2"], t["p3"]]
    ev_done = {}
    mm_done = {}
    for o in range(nch):
        wb = [t["wbq0"], t["wbq1"]][o % 2]
        w = [(ct, mm_done[o - 2])] if o >= 2 else []
        ldt = p.dma(["sync", "scalar"][o % 2], wb[:, :], wqs_dram[o], wait=w)
        for g in range(2):
            pb = pb4[(2 * o + g) % 4]
            wmm = [ldt]
            if (2 * o + g) >= 4:
                wmm.append((cv, ev_done[2 * o + g - 4]))
            for c in range(nch):
                p.op("tensor", (lambda e, c=c, o=o, g=g, pb=pb, wb=wb: e.matmul(
                    pb[:, :], wb[:, 128 * c:128 * (c + 1)],
                    xbf[:, T * c + 512 * g:T * c + 512 * (g + 1)],
                    start=(c == 0), stop=(c == nch - 1))),
                     inc=[(ct, 1)] if c == nch - 1 else (), wait=wmm if c == 0 else ())
            mmt = ct.n
            p.op("vector", (lambda e, o=o, g=g, pb=pb: e.tensor_scalar(
                qbf[:, 1024 * o + 512 * g:1024 * o + 512 * (g + 1)], pb[:, :],
                t["biasq"][:, o:o + 1], None, op0=ALU.add)),
                 inc=[(cv, 1)], wait=[(ct, mmt)])
            ev_done[2 * o + g] = cv.n
        mm_done[o] = ct.n
    kv_ev = {}
    for kv in range(2):
        dst = [t["kfull"], t["vfull"]][kv]
        for g in range(2):
            pb = pb4[(kv * 2 + g) % 4]
            wmm = [(cv, ev_done[28 + kv * 2 + g])] if False else []
            wmm = [(cv, list(ev_done.values())[-1])]
            for c in range(nch):
                p.op("tensor", (lambda e, c=c, kv=kv, g=g, pb=pb: e.matmul(
                    pb[:, :], t["wkv"][:, 2048 * kv + 128 * c:2048 * kv + 128 * (c + 1)],
                    xbf[:, T * c + 512 * g:T * c + 512 * (g + 1)],
                    start=(c == 0), stop=(c == nch - 1))),
                     inc=[(ct, 1)] if c == nch - 1 else (), wait=wmm if c == 0 else ())
            mmt = ct.n
            dstt = [t["PT0"], t["PT1"]][kv]
            p.op("vector", (lambda e, kv=kv, g=g, pb=pb, dstt=dstt: e.tensor_scalar(
                dstt[:, 512 * g:512 * (g + 1)], pb[:, :],
                t["biaskv"][:, kv:kv + 1], None, op0=ALU.add)),
                 inc=[(cv, 1)], wait=[(ct, mmt)])
            kv_ev[(kv, g)] = cv.n
    kv_done = cv.n

    ex = p.sem("exch", "hw")
    exd = p.sem("exdone")

    def exchange(e):
        e.wait_ge(p.real["cv"], kv_done)
        with e.register(f"cnt{tag}") as cnt, e.register(f"rrole{tag}") as rrole, \
             e.register(f"rflag{tag}") as rflag, e.register(f"rcond{tag}") as rcond:
            e.reg_mov(cnt, 0)
            e.reg_load(rrole, t["sroleA"][0:1, 0:1])
            for myrow in (0, 1):
                other = 1 - myrow
                with e.If_eq(rrole, myrow):
                    e.dma_start(out=sh_kv[myrow, 0:128, :], in_=t["PT0"][:, :]).then_inc(p.real["exch"], 16)
                    e.dma_start(out=sh_kv[myrow, 128:256, :], in_=t["PT1"][:, :]).then_inc(p.real["exch"], 16)
                    e.dma_start(out=t["kfull"][:, 1024 * myrow:1024 * (myrow + 1)], in_=t["PT0"][:, :]).then_inc(p.real["exch"], 16)
                    e.dma_start(out=t["vfull"][:, 1024 * myrow:1024 * (myrow + 1)], in_=t["PT1"][:, :]).then_inc(p.real["exch"], 16)
                    e.reg_add(cnt, cnt, 64)
                    e.wait_ge(p.real["exch"], cnt)
                    e.dma_start(out=sh_flag[myrow:myrow + 1, :], in_=t["sflagA"][:, :]).then_inc(p.real["exch"], 16)
                    e.reg_add(cnt, cnt, 16)
                    e.wait_ge(p.real["exch"], cnt)

                    def cond(other=other, cnt=cnt, rflag=rflag, rcond=rcond, e=e):
                        e.dma_start(out=t["spollA"][:, :], in_=sh_flag[other:other + 1, :]).then_inc(p.real["exch"], 16)
                        e.reg_add(cnt, cnt, 16)
                        e.wait_ge(p.real["exch"], cnt)
                        e.reg_load(rflag, t["spollA"][0:1, 0:1])
                        e.reg_alu(rcond, rflag, tag + 7, mybir.AluOpType.not_equal)
                        return rcond
                    with e.While(cond):
                        e.nop()
                    e.dma_start(out=t["kfull"][:, 1024 * other:1024 * (other + 1)],
                                in_=sh_kv[other, 0:128, :]).then_inc(p.real["exch"], 16)
                    e.dma_start(out=t["vfull"][:, 1024 * other:1024 * (other + 1)],
                                in_=sh_kv[other, 128:256, :]).then_inc(p.real["exch"], 16)
                    e.reg_add(cnt, cnt, 32)
                    e.wait_ge(p.real["exch"], cnt)
        e.nop().then_inc(p.real["exdone"], 1)

    p.raw("sync", exchange, inc=[(ex, 0), (exd, 1)])
    exd_t = exd.n

    barrier(C, f"l1a_ex{tag}")

    vt_done = {}
    for kt in range(16):
        pbt = [t["p4"], t["p5"]][kt % 2]
        pbb = pbt[:, :].bitcast(BF16)
        w = [(p.sem("exdone"), exd_t)] if kt < 2 else [(cv, vt_done[kt - 2])]
        p.op("tensor", (lambda e, kt=kt, pbb=pbb: e.transpose(
            pbb[:, 0:128], t["vfull"][:, 128 * kt:128 * (kt + 1)], t["ident_bf"][:, :])),
             inc=[(ct, 1)], wait=w)
        ctt = ct.n
        p.op("vector", (lambda e, kt=kt, pbb=pbb: e.tensor_copy(
            vT[:, 128 * kt:128 * (kt + 1)], pbb[:, 0:128])),
             inc=[(cv, 1)], wait=[(ct, ctt)])
        vt_done[kt] = cv.n
    vt_all = cv.n

    R = t["rows32a"]
    km_done = {}
    for g in range(4):
        w = [(p.sem("exdone"), exd_t)] if g == 0 else [(cv, km_done[g - 1])]
        p.op("scalar", (lambda e, g=g: e.activation(
            t["sqA"][:, 0:512], t["kfull"][:, 512 * g:512 * (g + 1)], AF.Square)),
             inc=[(cs, 1)], wait=w + [(cp, 0)])
        cst = cs.n
        p.op("tensor", (lambda e: e.matmul(
            t["p6"][0:1, :], t["ones128_bf"][:, 0:1], t["sqA"][:, 0:512],
            start=True, stop=True)), inc=[(ct, 1)], wait=[(cs, cst)])
        ctt = ct.n
        p.op("vector", (lambda e, g=g: e.tensor_reduce(
            R[0:1, 700 + g:701 + g], t["p6"][0:1, :], axis=AX.X, op=ALU.max)),
             inc=[(cv, 1)], wait=[(ct, ctt)])
        km_done[g] = cv.n
    p.op("vector", (lambda e: e.tensor_reduce(
        R[0:1, 710:711], R[0:1, 700:704], axis=AX.X, op=ALU.max)),
         inc=[(cv, 1)], wait=[(cv, cv.n)])
    kmax_t = flush(C, "vector")
    for h in range(16):
        w0 = [(cv, kmax_t)] if h == 0 else [(cv, p.sem("cv").n)]
        p.op("scalar", (lambda e, h=h: e.activation(
            t["sqA"][:, 0:1024], qbf[:, 1024 * h:1024 * (h + 1)], AF.Square)),
             inc=[(cs, 1)], wait=w0)
        cst = cs.n
        p.op("tensor", (lambda e: e.matmul(
            t["p6"][0:1, :], t["ones128_bf"][:, 0:1], t["sqA"][:, 0:512],
            start=True, stop=True)), wait=[(cs, cst)])
        p.op("tensor", (lambda e: e.matmul(
            t["p7"][0:1, :], t["ones128_bf"][:, 0:1], t["sqA"][:, 512:1024],
            start=True, stop=True)), inc=[(ct, 1)])
        ctt = ct.n
        p.op("vector", (lambda e, h=h: e.tensor_reduce(
            R[0:1, 712:713], t["p6"][0:1, :], axis=AX.X, op=ALU.max)),
             inc=[(cv, 1)], wait=[(ct, ctt)])
        p.op("vector", (lambda e, h=h: e.tensor_reduce(
            R[0:1, 713:714], t["p7"][0:1, :], axis=AX.X, op=ALU.max)),
             inc=[(cv, 1)], wait=[(cv, cv.n)])
        p.op("vector", (lambda e: e.tensor_max(
            R[0:1, 714:715], R[0:1, 712:713], R[0:1, 713:714])),
             inc=[(cv, 1)], wait=[(cv, cv.n)])
        p.op("vector", (lambda e: e.tensor_mul(
            R[0:1, 715:716], R[0:1, 714:715], R[0:1, 710:711])),
             inc=[(cv, 1)], wait=[(cv, cv.n)])
        cvt = flush(C, "vector")
        p.op("scalar", (lambda e: e.activation(
            R[0:1, 716:717], R[0:1, 715:716], AF.Sqrt)), inc=[(cs, 1)], wait=[(cv, cvt)])
        cst = flush(C, "scalar")
        p.op("vector", (lambda e: e.tensor_scalar(
            R[0:1, 717:718], R[0:1, 716:717], -1.0 / (128.0 ** 0.5), None, op0=ALU.mult)),
             inc=[(cv, 1)], wait=[(cs, cst)])
        cvt = flush(C, "vector")
        p.op("tensor", (lambda e: e.matmul(
            t["p6"][:, 0:1], t["onesr_f"][:, :], R[0:1, 717:718],
            start=True, stop=True)), inc=[(ct, 1)], wait=[(cv, cvt)])
        ctt = ct.n
        p.op("vector", (lambda e, h=h: e.tensor_copy(t["negb"][:, h:h + 1], t["p6"][:, 0:1])),
             inc=[(cv, 1)], wait=[(ct, ctt)])
    bounds_done = cv.n

    if dbg is not None:
        dbg("post_bounds")
    barrier(C, f"l1a_bounds{tag}")

    spair = [(t["p0"], t["p1"]), (t["p2"], t["p3"])]
    PT = [t["PT0"], t["PT1"]]
    for h in range(16):
        exp_done = {}
        luse_done = {}
        for kt in range(16):
            sp_ = spair[kt % 2]
            for g in range(2):
                w = []
                if kt >= 2:
                    w.append((cs, exp_done[kt - 2]))
                if h > 0 and kt < 2:
                    w.append((cs, h_exp_last))
                if h > 0 and kt == 0 and g == 0:
                    w.append((cv, h_prev_obf))
                p.op("tensor", (lambda e, h=h, kt=kt, g=g, sp_=sp_: e.matmul(
                    sp_[g][:, :], t["kfull"][:, 128 * kt:128 * (kt + 1)],
                    qbf[:, 1024 * h + 512 * g:1024 * h + 512 * (g + 1)],
                    start=True, stop=True)),
                     inc=[(ct, 1)] if g == 1 else (), wait=w if g == 0 else ())
            mmt = ct.n
            wexp = [(ct, mmt)]
            if kt >= 2:
                wexp.append((ct, luse_done[kt - 2]))
            for g in range(2):
                p.op("scalar", (lambda e, h=h, kt=kt, g=g, sp_=sp_: e.activation(
                    PT[kt % 2][:, 512 * g:512 * (g + 1)], sp_[g][:, :], AF.Exp,
                    bias=t["negb"][:, h:h + 1], scale=1.0 / (128.0 ** 0.5))),
                     inc=[(cs, 1)] if g == 1 else (), wait=wexp if g == 0 else ())
            exp_done[kt] = cs.n
            cst = cs.n
            for g in range(2):
                p.op("tensor", (lambda e, kt=kt, g=g: e.matmul(
                    [t["p6"], t["p7"]][g][0:1, :], t["ones128_bf"][:, 0:1],
                    PT[kt % 2][:, 512 * g:512 * (g + 1)],
                    start=(kt == 0), stop=(kt == 15))), wait=[(cs, cst)] if g == 0 else ())
            for g in range(2):
                p.op("tensor", (lambda e, kt=kt, g=g: e.matmul(
                    [t["p4"], t["p5"]][g][:, :], vT[:, 128 * kt:128 * (kt + 1)],
                    PT[kt % 2][:, 512 * g:512 * (g + 1)],
                    start=(kt == 0), stop=(kt == 15))),
                     inc=[(ct, 1)] if g == 1 else ())
            p.op("tensor", (lambda e: e.nop()), inc=[(ct, 1)])
            luse_done[kt] = ct.n
        h_exp_last = exp_done[15]
        avl_done = ct.n
        p.op("vector", (lambda e: e.reciprocal(R[0:1, 0:512], t["p6"][0:1, :])),
             inc=[(cv, 1)], wait=[(ct, avl_done)])
        p.op("vector", (lambda e: e.reciprocal(R[0:1, 512:1024], t["p7"][0:1, :])),
             inc=[(cv, 1)], wait=[(cv, cv.n)])
        cvt = flush(C, "vector")
        for g in range(2):
            p.op("tensor", (lambda e, g=g: e.matmul(
                [t["p0"], t["p1"]][g][:, :], t["onesr_f"][:, :], R[0:1, 512 * g:512 * (g + 1)],
                start=True, stop=True)), inc=[(ct, 1)] if g == 1 else (), wait=[(cv, cvt)] if g == 0 else ())
        ctt = ct.n
        for g in range(2):
            p.op("vector", (lambda e, g=g: e.tensor_copy(
                linvB[:, 512 * g:512 * (g + 1)], [t["p0"], t["p1"]][g][:, :])),
                 inc=[(cv, 1)], wait=[(ct, ctt)] if g == 0 else ())
        cvt = cv.n
        for g in range(2):
            p.op("vector", (lambda e, h=h, g=g: e.tensor_mul(
                obf[:, 1024 * h + 512 * g:1024 * h + 512 * (g + 1)],
                [t["p4"], t["p5"]][g][:, :], linvB[:, 512 * g:512 * (g + 1)])),
                 inc=[(cv, 1)], wait=[(cv, cvt)] if g == 0 else ())
        h_prev_obf = cv.n

    if dbg is not None:
        dbg("post_heads")
    barrier(C, f"l1a_heads{tag}")

    ev_done = {}
    mm_done = {}
    for o in range(nch):
        wb = [t["wbq0"], t["wbq1"]][o % 2]
        w = [(ct, mm_done[o - 2])] if o >= 2 else []
        ldt = p.dma(["sync", "scalar"][o % 2], wb[:, :], wos_dram[o], wait=w)
        for g in range(2):
            pb = pb4[(2 * o + g) % 4]
            wmm = [ldt]
            if (2 * o + g) >= 4:
                wmm.append((cv, ev_done[2 * o + g - 4]))
            for c in range(nch):
                p.op("tensor", (lambda e, c=c, o=o, g=g, pb=pb, wb=wb: e.matmul(
                    pb[:, :], wb[:, 128 * c:128 * (c + 1)],
                    obf[:, T * c + 512 * g:T * c + 512 * (g + 1)],
                    start=(c == 0), stop=(c == nch - 1))),
                     inc=[(ct, 1)] if c == nch - 1 else (), wait=wmm if c == 0 else ())
            mmt = ct.n
            p.op("vector", (lambda e, o=o, g=g, pb=pb: e.tensor_scalar(
                ybf[:, T * o + 512 * g:T * o + 512 * (g + 1)], pb[:, :],
                t["biaso"][:, o:o + 1], None, op0=ALU.add)),
                 inc=[(cv, 1)], wait=[(ct, mmt)])
            ev_done[2 * o + g] = cv.n
        mm_done[o] = ct.n

    barrier(C, f"l1a_wo{tag}")

    stats_ready = emit_ln_stats(C, lambda c: chunk(ybf, c, T), nch, t["m2_B"], t["r2_B"])
    for c in range(nch):
        for g in range(T // 512):
            slx = slice(T * c + 512 * g, T * c + 512 * (g + 1))
            slg = slice(512 * g, 512 * (g + 1))
            if c % 2 == 0:
                p.op("vector", (lambda e, slx=slx, slg=slg: e.tensor_sub(
                    t["sqA"][:, 0:512], ybf[:, slx], t["m2_B"][:, slg])))
                p.op("vector", (lambda e, slg=slg: e.tensor_mul(
                    t["tmp32e"][:, :], t["sqA"][:, 0:512], t["r2_B"][:, slg])))
                p.op("vector", (lambda e, slx=slx: e.tensor_add(
                    t["x_res"][:, slx], t["x_res"][:, slx], t["tmp32e"][:, :])),
                     inc=[(cv, 1)])
            else:
                p.op("gpsimd", (lambda e, slx=slx, slg=slg: e.tensor_sub(
                    t["sqB"][:, 0:512], ybf[:, slx], t["m2_B"][:, slg])),
                     wait=[(cv, stats_ready)] if (c == 1 and g == 0) else ())
                p.op("gpsimd", (lambda e, slg=slg: e.tensor_mul(
                    t["e1"][:, :], t["sqB"][:, 0:512], t["r2_B"][:, slg])))
                p.op("gpsimd", (lambda e, slx=slx: e.tensor_add(
                    t["x_res"][:, slx], t["x_res"][:, slx], t["e1"][:, :])),
                     inc=[(cp, 1)])
    barrier(C, f"l1a_done{tag}")


RT128 = 1.0 / (128.0 ** 0.5)


def alloc_l0attn(C, ctx):
    C.sb(ctx, "wsA", [128, 2048], BF16)
    C.sb(ctx, "wsB", [128, 2048], BF16)
    C.sb(ctx, "wsmall", [128, 512], F32)
    C.sb(ctx, "wsmallb", [128, 256], BF16)
    C.sb(ctx, "convw", [128, 160], F32)
    C.sb(ctx, "convext", [128, 1027], BF16)
    C.sb(ctx, "haloloc", [128, 128], BF16)
    C.sb(ctx, "halorem", [128, 128], BF16)
    C.sb(ctx, "ggtmp", [128, 1024], BF16)
    C.sb(ctx, "scanprep", [128, 1792], F32)
    C.sb(ctx, "acol", [128, 8], F32)
    C.sb(ctx, "onw", [128, 1], F32)
    C.sb(ctx, "cumsL", [128, 64], F32)
    C.sb(ctx, "dupB", [64, 128], F32)
    C.sb(ctx, "dupE", [128, 128], F32)
    C.sb(ctx, "dupO", [128, 128], F32)
    C.sb(ctx, "sel63", [64, 1], F32)
    C.sb(ctx, "m1c4", [128, 512], BF16)
    C.sb(ctx, "m2c", [128, 64], BF16)
    C.sb(ctx, "sroleB", [1, 2], mybir.dt.int32)
    C.sb(ctx, "sflagB", [1, 8], mybir.dt.int32)
    C.sb(ctx, "spollB", [1, 8], mybir.dt.int32)


def emit_l0attn_pre(C, Wqs, Wks, Wvs, Wgs, wsmall_d, convw_d, acol_d,
                    x_work, gg_work, sh_halo, sh_flag, nch=16):
    p, t, T = C.p, C.t, C.T
    cv, cs, ct, cp = p.sem("cv"), p.sem("cs"), p.sem("ct"), p.sem("cp")
    xbf, ybf = t["xbf"], t["ybf"]
    KCVC = t["x_res"][:, :].bitcast(BF16)
    qc = lambda o: ybf[:, 1024 * o:1024 * (o + 1)]
    kc = lambda o: KCVC[:, 1024 * o:1024 * (o + 1)]
    vc = lambda o: KCVC[:, 16384 + 1024 * o:16384 + 1024 * (o + 1)]

    emit_ln_stats(C, lambda c: chunk(xbf, c, T), nch, t["m_B"], t["rstd_B"],
                  cast_src=lambda c: chunk(t["x_res"], c, T))
    for c in range(nch):
        p.op("vector", (lambda e, c=c: e.tensor_sub(chunk(xbf, c, T), chunk(xbf, c, T), t["m_B"][:, :])))
        p.op("vector", (lambda e, c=c: e.tensor_mul(chunk(xbf, c, T), chunk(xbf, c, T), t["rstd_B"][:, :])),
             inc=[(cv, 1)] if c == nch - 1 else ())
    xln_done = cv.n
    sp = None
    for c in range(nch):
        sp = p.dma("sync", x_work[128 * c:128 * (c + 1), :], chunk(t["x_res"], c, T))
    spill_done = sp

    barrier(C, "l0a_ln")

    p.op("vector", lambda e: e.memset(t["haloloc"][:, :], 0))
    p.op("vector", lambda e: e.memset(t["halorem"][:, :], 0))

    for c in range(nch):
        p.op("tensor", (lambda e, c=c: e.matmul(
            t["p6"][0:8, 0:1], t["wsmall"][:, 256 + 8 * c:256 + 8 * (c + 1)],
            t["constcol"][:, 1:2], start=(c == 0), stop=(c == nch - 1))),
             inc=[(ct, 1)] if c == nch - 1 else ())
    ctt = ct.n
    p.op("vector", (lambda e: e.tensor_copy(t["acol"][0:8, 3:4], t["p6"][0:8, 0:1])),
         inc=[(cv, 1)], wait=[(ct, ctt)])
    wsum_t = flush(C, "vector")

    sig_done = 0
    aev_done = 0
    for g in range(2):
        wb0 = [(cv, xln_done)]
        if g == 1:
            wb0.append((cs, sig_done))
            wb0.append((cv, aev_done))
        for c in range(nch):
            p.op("tensor", (lambda e, c=c, g=g: e.matmul(
                t["p4"][0:16, :], t["wsmallb"][:, 16 * c:16 * (c + 1)],
                xbf[:, T * c + 512 * g:T * c + 512 * (g + 1)],
                start=(c == 0), stop=(c == nch - 1))), wait=wb0 if c == 0 else ())
        for c in range(nch):
            p.op("tensor", (lambda e, c=c, g=g: e.matmul(
                t["p5"][0:8, :], t["wsmall"][:, 256 + 8 * c:256 + 8 * (c + 1)],
                chunk(t["x_res"], c, T)[:, 512 * g:512 * (g + 1)],
                start=(c == 0), stop=(c == nch - 1))),
                 inc=[(ct, 1)] if c == nch - 1 else ())
        ctt = ct.n
        p.op("scalar", (lambda e, g=g: e.activation(
            t["rows32b"][0:16, 512 * g:512 * (g + 1)], t["p4"][0:16, :], AF.Sigmoid)),
             inc=[(cs, 1)], wait=[(ct, ctt)])
        sig_done = cs.n
        p.op("vector", (lambda e, g=g: e.tensor_scalar(
            t["rows32b"][0:16, 512 * g:512 * (g + 1)], t["rows32b"][0:16, 512 * g:512 * (g + 1)],
            2.0, None, op0=ALU.mult)), wait=[(cs, cs.n)])
        p.op("vector", (lambda e, g=g: e.tensor_scalar(
            t["tmp32e"][0:8, :], t["m_B"][0:8, 512 * g:512 * (g + 1)],
            t["acol"][0:8, 3:4], None, op0=ALU.mult)), inc=[(cv, 1)], wait=[(cv, wsum_t)])
        p.op("vector", (lambda e, g=g: e.scalar_tensor_tensor(
            t["F_B"][0:8, 512 * g:512 * (g + 1)], t["p5"][0:8, :], 1.0,
            t["tmp32e"][0:8, :], op0=ALU.mult, op1=ALU.subtract)), wait=[(cv, cv.n)])
        p.op("vector", (lambda e, g=g: e.tensor_mul(
            t["F_B"][0:8, 512 * g:512 * (g + 1)], t["F_B"][0:8, 512 * g:512 * (g + 1)],
            t["rstd_B"][0:8, 512 * g:512 * (g + 1)])))
        p.op("vector", (lambda e, g=g: e.tensor_scalar(
            t["F_B"][0:8, 512 * g:512 * (g + 1)], t["F_B"][0:8, 512 * g:512 * (g + 1)],
            t["acol"][0:8, 1:2], None, op0=ALU.add)), inc=[(cv, 1)])
        aev_done = cv.n
    ba_done = cv.n
    if getattr(C, "dbg_pre", None):
        C.dbg_pre("a_done")
    A = t["rows32b"]
    p.op("scalar", (lambda e: e.activation(t["r2_B"][0:8, :], t["F_B"][0:8, :], AF.Relu)),
         inc=[(cs, 1)], wait=[(cv, ba_done)])
    relu_t = cs.n
    p.op("vector", (lambda e: e.tensor_scalar(t["AS_B"][0:8, :], t["F_B"][0:8, :], -1.0, None, op0=ALU.mult)))
    p.op("vector", (lambda e: e.tensor_max(t["AS_B"][0:8, :], t["F_B"][0:8, :], t["AS_B"][0:8, :])),
         inc=[(cv, 1)], wait=[(cv, cv.n)])
    cvt = cv.n
    p.op("scalar", (lambda e: e.activation(t["AS_B"][0:8, :], t["AS_B"][0:8, :], AF.Exp, scale=-1.0)),
         inc=[(cs, 1)], wait=[(cv, cvt)])
    cst = cs.n
    p.op("scalar", (lambda e: e.activation(t["AS_B"][0:8, :], t["AS_B"][0:8, :], AF.Ln,
                                           bias=t["constcol"][0:8, 1:2])), inc=[(cs, 1)],
         wait=[(cs, cst)])
    cst = cs.n
    p.op("vector", (lambda e: e.tensor_add(t["F_B"][0:8, :], t["AS_B"][0:8, :], t["r2_B"][0:8, :])),
         wait=[(cs, cst)])
    p.op("scalar", (lambda e: e.activation(t["acol"][0:8, 2:3], t["acol"][0:8, 0:1], AF.Exp)),
         inc=[(cs, 1)])
    cstA = flush(C, "scalar")
    p.op("vector", (lambda e: e.tensor_scalar(t["acol"][0:8, 2:3], t["acol"][0:8, 2:3],
                                              -1.0, None, op0=ALU.mult)),
         inc=[(cv, 1)], wait=[(cs, cstA)])
    cvA = flush(C, "vector")
    p.op("vector", (lambda e: e.tensor_scalar(t["F_B"][0:8, :], t["F_B"][0:8, :], t["acol"][0:8, 2:3],
                                              None, op0=ALU.mult)), inc=[(cv, 1)], wait=[(cv, cv.n)])
    g_done = flush(C, "vector")
    if getattr(C, "dbg_pre", None):
        C.dbg_pre("g_done")

    mm_done = {}
    for o in range(8):
        wb = [t["wsA"], t["wsB"]][o % 2]
        w = [(ct, mm_done[o - 2])] if o >= 2 else []
        ldt = p.dma("gpsimd", wb[:, :], Wgs[o], wait=w)
        for g in range(2):
            pb = [t["p6"], t["p7"]][g]
            for c in range(nch):
                p.op("tensor", (lambda e, c=c, g=g, pb=pb, wb=wb: e.matmul(
                    pb[:, :], wb[:, 128 * c:128 * (c + 1)],
                    xbf[:, T * c + 512 * g:T * c + 512 * (g + 1)],
                    start=(c == 0), stop=(c == nch - 1))),
                     inc=[(ct, 1)] if c == nch - 1 else (),
                     wait=[ldt] + ([(cs, p.sem("cs").n)] if o > 0 else []) if c == 0 else ())
        mm_done[o] = ct.n
        ctt = ct.n
        for g in range(2):
            pb = [t["p6"], t["p7"]][g]
            wsl = [(ct, ctt)]
            if o >= 1:
                wsl.append(gg_dma_t)
            p.op("scalar", (lambda e, g=g, pb=pb: e.activation(
                t["ggtmp"][:, 512 * g:512 * (g + 1)], pb[:, :], AF.Silu)),
                 inc=[(cs, 1)], wait=wsl if g == 0 else ())
        cst = cs.n
        gg_dma_t = p.dma("gpsimd", gg_work[o], t["ggtmp"][:, :], wait=[(cs, cst)])

    PLAN = [("q", 8, Wqs, qc), ("k", 16, Wks, kc), ("v", 16, Wvs, vc)]
    hcol = 0
    halo_done = {}
    pbr = [t["p0"], t["p1"], t["p2"], t["p3"]]
    i = 0
    raw_done = {}
    for nm, nо, Ws, dst in PLAN:
        for o in range(nо):
            wb = [t["wsA"], t["wsB"]][i % 2]
            w = [(ct, raw_done[i - 2])] if i >= 2 else [(ct, mm_done[7])]
            ldt = p.dma("gpsimd", wb[:, :], Ws[o], wait=w)
            for g in range(2):
                pb = pbr[(2 * i + g) % 4]
                wmm = [ldt]
                if i >= 2:
                    wmm.append((cv, halo_done[i - 2]))
                for c in range(nch):
                    p.op("tensor", (lambda e, c=c, g=g, pb=pb, wb=wb: e.matmul(
                        pb[:, :], wb[:, 128 * c:128 * (c + 1)],
                        xbf[:, T * c + 512 * g:T * c + 512 * (g + 1)],
                        start=(c == 0), stop=(c == nch - 1))),
                         inc=[(ct, 1)] if c == nch - 1 else (), wait=wmm if c == 0 else ())
            raw_done[i] = ct.n
            ctt = ct.n
            for g in range(2):
                pb = pbr[(2 * i + g) % 4]
                wv = [(ct, ctt)]
                if i == 0 and g == 0:
                    wv.append(spill_done)
                p.op("vector", (lambda e, g=g, pb=pb, dst=dst, o=o: e.tensor_copy(
                    dst(o)[:, 512 * g:512 * (g + 1)], pb[:, :])), wait=wv)
            pb1 = pbr[(2 * i + 1) % 4]
            p.op("vector", (lambda e, pb1=pb1, hcol=hcol: e.tensor_copy(
                t["haloloc"][:, 3 * hcol:3 * hcol + 3], pb1[:, 509:512])),
                 inc=[(cv, 1)])
            halo_done[i] = cv.n
            hcol += 1
            i += 1
    allraw = flush(C, "vector")

    exd = p.sem("exdB")
    exs = p.sem("exchB", "hw")

    def exchange(e):
        e.wait_ge(p.real["cv"], allraw)
        with e.register("cntB") as cnt, e.register("rroleB") as rrole, \
             e.register("rflagB") as rflag, e.register("rcondB") as rcond:
            e.reg_mov(cnt, 0)
            e.reg_load(rrole, t["sroleB"][0:1, 0:1])
            with e.If_eq(rrole, 0):
                e.dma_start(out=sh_halo[0], in_=t["haloloc"][:, :]).then_inc(p.real["exchB"], 16)
                e.reg_add(cnt, cnt, 16)
                e.wait_ge(p.real["exchB"], cnt)
                e.dma_start(out=sh_flag[0:1, :], in_=t["sflagB"][:, :]).then_inc(p.real["exchB"], 16)
                e.reg_add(cnt, cnt, 16)
                e.wait_ge(p.real["exchB"], cnt)
            with e.If_eq(rrole, 1):
                def cond():
                    e.dma_start(out=t["spollB"][:, :], in_=sh_flag[0:1, :]).then_inc(p.real["exchB"], 16)
                    e.reg_add(cnt, cnt, 16)
                    e.wait_ge(p.real["exchB"], cnt)
                    e.reg_load(rflag, t["spollB"][0:1, 0:1])
                    e.reg_alu(rcond, rflag, 21, mybir.AluOpType.not_equal)
                    return rcond
                with e.While(cond):
                    e.nop()
                e.dma_start(out=t["halorem"][:, :], in_=sh_halo[0]).then_inc(p.real["exchB"], 16)
                e.reg_add(cnt, cnt, 16)
                e.wait_ge(p.real["exchB"], cnt)
        e.nop().then_inc(p.real["exdB"], 1)

    p.raw("sync", exchange, inc=[(exd, 1)])
    exd_t = exd.n

    barrier(C, "l0a_halo")

    conv_done = {}
    ci = 0
    for nm, nо, Ws, dst in PLAN:
        base = {"q": 0, "k": 32, "v": 96}[nm]
        for o in range(nо):
            hcolо = {"q": 0, "k": 8, "v": 24}[nm] + o
            p.op("vector", (lambda e, dst=dst, o=o: e.tensor_copy(
                t["convext"][:, 3:1027], dst(o)[:, :])))
            p.op("vector", (lambda e, hcolо=hcolо: e.tensor_copy(
                t["convext"][:, 0:3], t["halorem"][:, 3 * hcolо:3 * hcolо + 3])))
            p.op("vector", (lambda e, base=base, o=o: e.tensor_scalar(
                t["sqA"][:, :], t["convext"][:, 3:1027], t["convw"][:, base + 4 * o:base + 4 * o + 1],
                None, op0=ALU.mult)),
                 wait=[(cs, conv_done[ci - 1])] if ci >= 1 else [(p.sem("exdB"), exd_t)])
            for j in range(1, 4):
                p.op("vector", (lambda e, base=base, o=o, j=j: e.scalar_tensor_tensor(
                    t["sqA"][:, :], t["convext"][:, 3 - j:1027 - j],
                    t["convw"][:, base + 4 * o + j:base + 4 * o + j + 1], t["sqA"][:, :],
                    op0=ALU.mult, op1=ALU.add)))
            p.op("vector", (lambda e: e.engine_nop()), inc=[(cv, 1)])
            cvt = cv.n
            p.op("scalar", (lambda e, dst=dst, o=o: e.activation(
                dst(o)[:, :], t["sqA"][:, :], AF.Silu)), inc=[(cs, 1)], wait=[(cv, cvt)])
            conv_done[ci] = cs.n
            ci += 1
    convs_done = cs.n

    barrier(C, "l0a_conv")

    prev_sqrt = 0
    for which, n_, get in (("k", 16, kc), ("q", 8, qc)):
        for o in range(n_):
            p.op("vector", (lambda e, get=get, o=o: e.tensor_mul(
                t["sqA"][:, :], get(o)[:, :], get(o)[:, :])), inc=[(cv, 1)])
            cvt = cv.n
            for g in range(2):
                wmm2 = [(cv, cvt)] + ([(cs, prev_sqrt)] if prev_sqrt else [])
                p.op("tensor", (lambda e, g=g, get=get, o=o: e.matmul(
                    t["p6"][0:1, :] if g == 0 else t["p7"][0:1, :],
                    t["ones128_bf"][:, 0:1], t["sqA"][:, 512 * g:512 * (g + 1)],
                    start=True, stop=True)),
                     inc=[(ct, 1)] if g == 1 else (), wait=wmm2 if g == 0 else ())
            ctt = ct.n
            R = t["rows32a"]
            p.op("scalar", (lambda e: e.activation(R[0:1, 0:512], t["p6"][0:1, :], AF.Sqrt)),
                 wait=[(ct, ctt)])
            p.op("scalar", (lambda e: e.activation(R[0:1, 512:1024], t["p7"][0:1, :], AF.Sqrt)),
                 inc=[(cs, 1)])
            cst = cs.n
            prev_sqrt = cst
            p.op("vector", (lambda e: e.tensor_scalar(R[0:1, :], R[0:1, :], 1e-12, None, op0=ALU.max)),
                 inc=[(cv, 1)], wait=[(cs, cst)])
            p.op("vector", (lambda e: e.reciprocal(R[0:1, :], R[0:1, :])),
                 inc=[(cv, 1)], wait=[(cv, cv.n)])
            if which == "q":
                p.op("vector", (lambda e: e.tensor_scalar(R[0:1, :], R[0:1, :], RT128, None, op0=ALU.mult)),
                     inc=[(cv, 1)], wait=[(cv, cv.n)])
            cvt = flush(C, "vector")
            for g in range(2):
                p.op("tensor", (lambda e, g=g: e.matmul(
                    [t["p6"], t["p7"]][g][:, :], t["onesr_f"][:, :], R[0:1, 512 * g:512 * (g + 1)],
                    start=True, stop=True)), inc=[(ct, 1)] if g == 1 else (),
                     wait=[(cv, cvt)] if g == 0 else ())
            ctt = ct.n
            for g in range(2):
                p.op("vector", (lambda e, g=g, get=get, o=o: e.tensor_mul(
                    get(o)[:, 512 * g:512 * (g + 1)], get(o)[:, 512 * g:512 * (g + 1)],
                    [t["p6"], t["p7"]][g][:, :])),
                     inc=[(cv, 1)] if g == 1 else (), wait=[(ct, ctt)] if g == 0 else ())
    norm_done = cv.n

    SP = t["scanprep"]
    R = t["rows32a"]
    for jj in range(8):
        pb = [t["p4"], t["p5"]][jj % 2]
        p.op("tensor", (lambda e, jj=jj, pb=pb: e.transpose(
            pb[:, 0:8], t["F_B"][0:8, 128 * jj:128 * (jj + 1)], t["ident_f"][0:8, 0:8])),
             inc=[(ct, 1)], wait=[(cv, p.sem("cv").n), (cv, g_done)])
        ctt = ct.n
        p.op("vector", (lambda e, jj=jj, pb=pb: e.tensor_copy(
            SP[:, 1024 + 8 * jj:1024 + 8 * (jj + 1)], pb[:, 0:8])),
             inc=[(cv, 1)], wait=[(ct, ctt)])
    for jj in range(8):
        pb = [t["p4"], t["p5"]][jj % 2]
        p.op("tensor", (lambda e, jj=jj, pb=pb: e.transpose(
            pb[:, 0:16], A[0:16, 128 * jj:128 * (jj + 1)], t["ident_f"][0:16, 0:16])),
             inc=[(ct, 1)], wait=[(cv, p.sem("cv").n)])
        ctt = ct.n
        p.op("vector", (lambda e, jj=jj, pb=pb: e.tensor_copy(
            SP[:, 1152 + 16 * jj:1152 + 16 * (jj + 1)], pb[:, 0:16])),
             inc=[(cv, 1)], wait=[(ct, ctt)])
    gt_done = flush(C, "vector")
    for par in range(2):
        p.op("tensor", (lambda e, par=par: e.matmul(
            t["p4"][0:64, 0:64], t["cumsL"][64 * par:64 * (par + 1), :],
            SP[64 * par:64 * (par + 1), 1024:1088], start=True, stop=True)),
             inc=[(ct, 1)], wait=[(cv, gt_done)] if par == 0 else [(cv, p.sem("cv").n)])
        ctt = ct.n
        p.op("vector", (lambda e, par=par: e.tensor_copy(
            SP[0:64, 384 + 64 * par:448 + 64 * par], t["p4"][0:64, 0:64])),
             inc=[(cv, 1)], wait=[(ct, ctt)])
    cs_done = flush(C, "vector")
    p.op("tensor", (lambda e: e.matmul(
        t["p4"][:, 0:128], t["dupB"][:, :], SP[0:64, 384:512], start=True, stop=True)),
         inc=[(ct, 1)], wait=[(cv, cs_done)])
    ctt = ct.n
    p.op("vector", (lambda e: e.tensor_copy(SP[:, 0:128], t["p4"][:, 0:128])),
         inc=[(cv, 1)], wait=[(ct, ctt)])
    bT3 = SP[:, 1152:1280].rearrange("p (j r) -> p j r", r=16)
    for par in range(2):
        p.op("tensor", (lambda e, par=par: e.matmul(
            t["p5"][:, 0:64], t["dupE"][64 * par:64 * (par + 1), :], bT3[64 * par:64 * (par + 1), :, 0:8],
            start=True, stop=False)), wait=[(cv, p.sem("cv").n)] if par == 1 else ())
        p.op("tensor", (lambda e, par=par: e.matmul(
            t["p5"][:, 0:64], t["dupO"][64 * par:64 * (par + 1), :], bT3[64 * par:64 * (par + 1), :, 8:16],
            start=False, stop=True)), inc=[(ct, 1)])
        ctt = ct.n
        p.op("vector", (lambda e, par=par: e.tensor_copy(
            SP[:, 128 + 64 * par:192 + 64 * par], t["p5"][:, 0:64])),
             inc=[(cv, 1)], wait=[(ct, ctt)])
    ball_done = flush(C, "vector")
    p.op("tensor", (lambda e: e.matmul(
        t["p6"][0:1, 0:128], t["sel63"][:, :], SP[0:64, 384:512], start=True, stop=True)),
         inc=[(ct, 1)], wait=[(cv, ball_done)])
    ctt = ct.n
    p.op("vector", (lambda e: e.tensor_copy(R[0:1, 0:128], t["p6"][0:1, 0:128])),
         inc=[(cv, 1)], wait=[(ct, ctt)])
    cvt = flush(C, "vector")
    p.op("scalar", (lambda e: e.activation(R[0:1, 128:256], R[0:1, 0:128], AF.Exp)),
         inc=[(cs, 1)], wait=[(cv, cvt)])
    cst = flush(C, "scalar")
    p.op("tensor", (lambda e: e.matmul(
        t["p6"][:, 0:128], t["onesr_f"][:, :], R[0:1, 128:256], start=True, stop=True)),
         wait=[(cs, cst)])
    p.op("tensor", (lambda e: e.matmul(
        t["p6"][:, 128:256], t["onesr_f"][:, :], R[0:1, 0:128], start=True, stop=True)),
         inc=[(ct, 1)])
    ctt = ct.n
    p.op("vector", (lambda e: e.tensor_copy(SP[:, 256:384], t["p6"][:, 0:128])),
         wait=[(ct, ctt)])
    p.op("vector", (lambda e: e.tensor_copy(SP[:, 640:768], t["p6"][:, 128:256])),
         inc=[(cv, 1)])
    p.op("scalar", (lambda e: e.activation(SP[0:64, 512:640], SP[0:64, 384:512], AF.Exp)),
         inc=[(cs, 1)], wait=[(cv, cs_done)])
    barrier(C, "l0a_prep")


def emit_l0attn_scan(C, sh_smid, sh_flag, nch=16):
    p, t, T = C.p, C.t, C.T
    cv, cs, ct, cp = p.sem("cv"), p.sem("cs"), p.sem("ct"), p.sem("cp")
    xbf, ybf = t["xbf"], t["ybf"]
    SCR = xbf
    SCRF = xbf[:, :].bitcast(F32)
    SP = t["scanprep"]
    R = t["rows32a"]
    S_ar = t["F_B"]

    G = [512 * k for k in range(7)]
    H = [3584 + 512 * k for k in range(5)]
    Kq, Vq, KTq, KtTq, Yq, Uq, Sbq, Kngq, EgBq, VTq = (
        6144, 6656, 7168, 7680, 8192, 8704, 9216, 9728, 10240, 11264)
    Poq, Qppq = 10752, 11008
    bf = lambda c0, w=512: SCR[:, c0:c0 + w]
    Ef = SCRF[:, 6144:6656]
    Gtf = SCRF[:, 6656:7168]
    edf = SCRF[:, 7168:7176]

    p0, p1, p2, p3, p4, p5 = t["p0"], t["p1"], t["p2"], t["p3"], t["p4"], t["p5"]
    p6b = t["p6"][:, :].bitcast(BF16)
    p7b = t["p7"][:, :].bitcast(BF16)

    p.op("vector", (lambda e: e.memset(S_ar[:, :], 0.0)), inc=[(cv, 1)])
    s_upd = {h: cv.n for h in range(8)}
    smem_t = cv.n

    exdC = p.sem("exdC")
    exC = p.sem("exchC", "hw")

    def exchange_load(e):
        e.wait_ge(p.real["cv"], smem_t)
        with e.register("cntC") as cnt, e.register("rroleC") as rrole, \
             e.register("rflagC") as rflag, e.register("rcondC") as rcond:
            e.reg_mov(cnt, 0)
            e.reg_load(rrole, t["sroleB"][0:1, 0:1])
            with e.If_eq(rrole, 1):
                def cond():
                    e.dma_start(out=t["spollB"][:, :], in_=sh_flag[2:3, :]).then_inc(p.real["exchC"], 16)
                    e.reg_add(cnt, cnt, 16)
                    e.wait_ge(p.real["exchC"], cnt)
                    e.reg_load(rflag, t["spollB"][0:1, 0:1])
                    e.reg_alu(rcond, rflag, 33, mybir.AluOpType.not_equal)
                    return rcond
                with e.While(cond):
                    e.nop()
                e.dma_start(out=S_ar[:, :], in_=sh_smid[0]).then_inc(p.real["exchC"], 16)
                e.reg_add(cnt, cnt, 16)
                e.wait_ge(p.real["exchC"], cnt)
        e.nop().then_inc(p.real["exdC"], 1)

    p.raw("sync", exchange_load, inc=[(exdC, 1)])
    exdC_t = exdC.n

    H5R = 11776

    prev_pe_end = 0
    prev_gbuild = 0
    prev_ets = 0
    prev_expg = 0
    prev_acts = 0
    KCV = t["x_res"][:, :].bitcast(BF16)
    for j in range(nch):
        par, jj = j % 2, j // 2
        for q in range(2):
            heads = [4 * q + i for i in range(4)]
            ncol = [64 * par + 8 * jj + h for h in heads]
            w0 = []
            if prev_pe_end:
                w0.append((ct, prev_pe_end))
            if j == 0 and q == 0:
                w0.append((exdC, exdC_t))
            first = True
            for idx, h in enumerate(heads):
                for nh in range(2):
                    koff = 1024 * (8 * nh + h) + 64 * j
                    voff = 16384 + 1024 * (8 * nh + h) + 64 * j
                    p.op("vector", (lambda e, idx=idx, nh=nh, koff=koff: e.tensor_copy(
                        bf(Kq)[:, 128 * idx + nh:128 * (idx + 1):2], KCV[:, koff:koff + 64])),
                         wait=w0 if first else ())
                    first = False
                    p.op("vector", (lambda e, idx=idx, nh=nh, voff=voff: e.tensor_copy(
                        bf(Vq)[:, 128 * idx + nh:128 * (idx + 1):2], KCV[:, voff:voff + 64])))
                p.op("vector", (lambda e, idx=idx, h=h: e.tensor_copy(
                    bf(Sbq)[:, 128 * idx:128 * (idx + 1)], S_ar[:, 128 * h:128 * (h + 1)])),
                     inc=[(cv, 1)] if idx == 3 else ())
            t0 = cv.n
            w1 = [(cv, t0)]
            if prev_gbuild:
                w1.append((cv, prev_gbuild))
            if prev_ets:
                w1.append((cv, prev_ets))
            if prev_expg:
                w1.append((cs, prev_expg))
            if prev_acts:
                w1.append((cs, prev_acts))
            for idx in range(4):
                p.op("tensor", (lambda e, idx=idx: e.matmul(
                    p0[:, 128 * idx:128 * (idx + 1)], bf(Kq)[:, 128 * idx:128 * (idx + 1)],
                    bf(Kq)[:, 128 * idx:128 * (idx + 1)], start=True, stop=True)),
                     wait=w1 if idx == 0 else ())
            for idx in range(4):
                p.op("tensor", (lambda e, idx=idx, n=ncol[idx]: e.matmul(
                    p4[0:1, 128 * idx:128 * (idx + 1)], SP[:, n:n + 1], t["ident_f"][:, :],
                    start=True, stop=True)))
            for idx in range(4):
                p.op("tensor", (lambda e, idx=idx: e.transpose(
                    p6b[:, 128 * idx:128 * (idx + 1)], bf(Kq)[:, 128 * idx:128 * (idx + 1)],
                    t["ident_bf"][:, :])))
            for idx in range(4):
                p.op("tensor", (lambda e, idx=idx: e.transpose(
                    p7b[:, 128 * idx:128 * (idx + 1)], bf(Vq)[:, 128 * idx:128 * (idx + 1)],
                    t["ident_bf"][:, :])), inc=[(ct, 1)] if idx == 3 else ())
            t1 = ct.n
            p.op("vector", (lambda e: e.tensor_copy(R[0:1, 256:768], p4[0:1, :])),
                 inc=[(cv, 1)], wait=[(ct, t1)])
            t2v = flush(C, "vector")
            p.op("scalar", (lambda e: e.activation(bf(KTq)[:, :], p6b[:, 0:512], AF.Copy)),
                 wait=[(ct, t1)])
            p.op("scalar", (lambda e: e.activation(bf(VTq)[:, :], p7b[:, 0:512], AF.Copy)),
                 inc=[(cs, 1)])
            t2s = cs.n
            for idx in range(4):
                p.op("tensor", (lambda e, idx=idx: e.matmul(
                    p4[:, 128 * idx:128 * (idx + 1)], t["onesr_f"][:, :],
                    R[0:1, 256 + 128 * idx:256 + 128 * (idx + 1)], start=True, stop=True)),
                     inc=[(ct, 1)] if idx == 3 else (), wait=[(cv, t2v)] if idx == 0 else ())
            t3 = ct.n
            for idx in range(4):
                p.op("vector", (lambda e, idx=idx, n=ncol[idx]: e.tensor_scalar(
                    Ef[:, 128 * idx:128 * (idx + 1)], p4[:, 128 * idx:128 * (idx + 1)],
                    SP[:, n:n + 1], 0.0, op0=ALU.subtract, op1=ALU.min)),
                     inc=[(cv, 1)] if idx == 3 else (), wait=[(ct, t3)] if idx == 0 else ())
            t4v = cv.n
            p.op("scalar", (lambda e: e.activation(Ef[:, :], Ef[:, :], AF.Exp)),
                 wait=[(cv, t4v)])
            p.op("scalar", (lambda e: e.activation(bf(EgBq)[:, :], p4[:, :], AF.Exp)),
                 inc=[(cs, 1)])
            t5s = cs.n
            prev_ets = t4v
            prev_expg = t5s
            p.op("vector", (lambda e: e.tensor_mul(Gtf[:, :], Ef[:, :], p0[:, :])),
                 wait=[(cs, t5s)])
            for idx in range(4):
                p.op("vector", (lambda e, idx=idx, n=ncol[idx]: e.tensor_scalar(
                    Gtf[:, 128 * idx:128 * (idx + 1)], Gtf[:, 128 * idx:128 * (idx + 1)],
                    SP[:, 128 + n:129 + n], None, op0=ALU.mult)))
            p.op("vector", (lambda e: e.tensor_mul(bf(G[0])[:, :], Gtf[:, :], t["m1c4"][:, :])))
            p.op("vector", (lambda e: e.scalar_tensor_tensor(
                bf(Kngq)[:, :], bf(Kq)[:, :], -1.0, bf(EgBq)[:, :], op0=ALU.mult, op1=ALU.mult)),
                 inc=[(cv, 1)])
            t6v = cv.n
            prev_gbuild = t6v
            for idx in range(4):
                p.op("tensor", (lambda e, idx=idx: e.transpose(
                    p6b[:, 128 * idx:128 * (idx + 1)], bf(G[0])[:, 128 * idx:128 * (idx + 1)],
                    t["ident_bf"][:, :])), wait=[(cv, t6v), (cs, t2s)] if idx == 0 else ())
            for idx in range(4):
                p.op("tensor", (lambda e, idx=idx: e.matmul(
                    p1[:, 128 * idx:128 * (idx + 1)], bf(Kngq)[:, 128 * idx:128 * (idx + 1)],
                    bf(Sbq)[:, 128 * idx:128 * (idx + 1)], start=True, stop=True)),
                     inc=[(ct, 1)] if idx == 3 else ())
            t7 = ct.n
            p.op("scalar", (lambda e: e.activation(bf(H[0])[:, :], p6b[:, 0:512], AF.Copy)),
                 inc=[(cs, 1)], wait=[(ct, t7)])
            t8s = cs.n
            prev_acts = t8s
            p.op("vector", (lambda e: e.tensor_add(bf(Yq)[:, :], p1[:, :], bf(VTq)[:, :])),
                 inc=[(cv, 1)], wait=[(ct, t7)])
            t8v = cv.n
            for idx in range(4):
                p.op("vector", (lambda e, idx=idx, n=ncol[idx]: e.tensor_sub(
                    edf[:, idx:idx + 1], SP[:, 640 + n:641 + n], SP[:, n:n + 1])),
                     inc=[(cv, 1)] if idx == 3 else ())
            edt = flush(C, "vector")
            p.op("scalar", (lambda e: e.activation(edf[:, 0:4], edf[:, 0:4], AF.Exp)),
                 inc=[(cs, 1)], wait=[(cv, edt)])
            edts = flush(C, "scalar")
            for idx in range(4):
                p.op("vector", (lambda e, idx=idx: e.tensor_scalar(
                    bf(KtTq)[:, 128 * idx:128 * (idx + 1)], bf(KTq)[:, 128 * idx:128 * (idx + 1)],
                    edf[:, idx:idx + 1], None, op0=ALU.mult)),
                     inc=[(cv, 1)] if idx == 3 else (), wait=[(cs, edts)] if idx == 0 else ())
            ktt_done = cv.n
            seq = [(G[1], H[0], G[0]), (H[1], G[0], H[0]),
                   (G[2], H[1], G[1]), (H[2], G[1], H[1]),
                   (G[3], H[2], G[2]), (H[3], G[2], H[2]),
                   (G[4], H[3], G[3]), (H[4], G[3], H[3]),
                   (G[5], H[4], G[4]), (H5R, G[4], H[4]),
                   (G[6], H5R, G[5])]
            ev = {G[0]: (cv, t6v), H[0]: (cs, t8s)}
            bank_ev = {}
            banks2 = [p1, p5]
            rr = 0
            for dst, lh, rh in seq:
                pb = banks2[rr % 2]
                wv = [ev[lh], ev[rh]]
                if rr == 0:
                    wv.append((cv, t8v))
                if rr >= 2:
                    wv.append(bank_ev[rr - 2])
                for idx in range(4):
                    p.op("tensor", (lambda e, idx=idx, lh=lh, rh=rh, pb=pb: e.matmul(
                        pb[:, 128 * idx:128 * (idx + 1)], bf(lh)[:, 128 * idx:128 * (idx + 1)],
                        bf(rh)[:, 128 * idx:128 * (idx + 1)], start=True, stop=True)),
                         inc=[(ct, 1)] if idx == 3 else (), wait=wv if idx == 0 else ())
                mmt = ct.n
                if rr % 2 == 0:
                    p.op("vector", (lambda e, dst=dst, pb=pb: e.tensor_copy(bf(dst)[:, :], pb[:, :])),
                         inc=[(cv, 1)], wait=[(ct, mmt)])
                    ev[dst] = (cv, cv.n)
                else:
                    p.op("scalar", (lambda e, dst=dst, pb=pb: e.activation(
                        bf(dst)[:, :], pb[:, :], AF.Copy)), inc=[(cs, 1)], wait=[(ct, mmt)])
                    ev[dst] = (cs, cs.n)
                bank_ev[rr] = ev[dst]
                rr += 1
            ylast = (cv, t8v)
            for k in range(7):
                pb = banks2[rr % 2]
                wv = [ev[G[k]], ylast]
                if rr >= 2:
                    wv.append(bank_ev[rr - 2])
                for idx in range(4):
                    p.op("tensor", (lambda e, idx=idx, k=k, pb=pb: e.matmul(
                        pb[:, 128 * idx:128 * (idx + 1)], bf(G[k])[:, 128 * idx:128 * (idx + 1)],
                        bf(Yq)[:, 128 * idx:128 * (idx + 1)], start=True, stop=False)),
                         wait=wv if idx == 0 else ())
                    p.op("tensor", (lambda e, idx=idx, pb=pb: e.matmul(
                        pb[:, 128 * idx:128 * (idx + 1)], t["ident_bf"][:, :],
                        bf(Yq)[:, 128 * idx:128 * (idx + 1)], start=False, stop=True)),
                         inc=[(ct, 1)] if idx == 3 else ())
                mmt = ct.n
                if rr % 2 == 0:
                    p.op("vector", (lambda e, pb=pb: e.tensor_copy(bf(Yq)[:, :], pb[:, :])),
                         inc=[(cv, 1)], wait=[(ct, mmt)])
                    ylast = (cv, cv.n)
                else:
                    p.op("scalar", (lambda e, pb=pb: e.activation(bf(Yq)[:, :], pb[:, :], AF.Copy)),
                         inc=[(cs, 1)], wait=[(ct, mmt)])
                    ylast = (cs, cs.n)
                bank_ev[rr] = ylast
                rr += 1
            for idx in range(4):
                p.op("vector", (lambda e, idx=idx, n=ncol[idx]: e.tensor_scalar(
                    bf(Uq)[:, 128 * idx:128 * (idx + 1)], bf(Yq)[:, 128 * idx:128 * (idx + 1)],
                    SP[:, 128 + n:129 + n], None, op0=ALU.mult)),
                     inc=[(cv, 1)] if idx == 3 else (),
                     wait=[ylast] if (idx == 0 and ylast[0] is cs) else ())
            u_done = cv.n
            wpq = [bank_ev[rr - 2], bank_ev[rr - 1]]
            for idx in range(4):
                p.op("tensor", (lambda e, idx=idx, hh=heads[idx], j=j: e.matmul(
                    p2[:, 64 * idx:64 * (idx + 1)], bf(Kq)[:, 128 * idx:128 * (idx + 1)],
                    ybf[:, 1024 * hh + 64 * j:1024 * hh + 64 * j + 64],
                    start=True, stop=True)))
            for idx in range(4):
                p.op("tensor", (lambda e, idx=idx, n=ncol[idx]: e.matmul(
                    p2[0:1, 256 + 64 * idx:256 + 64 * (idx + 1)],
                    SP[0:64, 512 + n:513 + n], t["ident_f"][0:64, 0:64],
                    start=True, stop=True)), inc=[(ct, 1)] if idx == 3 else ())
            tpq = ct.n
            p.op("vector", (lambda e: e.tensor_copy(R[0:1, 768:1024], p2[0:1, 256:512])),
                 inc=[(cv, 1)], wait=[(ct, tpq)])
            tlrow = flush(C, "vector")
            for idx in range(4):
                p.op("tensor", (lambda e, idx=idx: e.matmul(
                    p2[:, 256 + 64 * idx:256 + 64 * (idx + 1)], t["onesr_f"][:, :],
                    R[0:1, 768 + 64 * idx:768 + 64 * (idx + 1)], start=True, stop=True)),
                     inc=[(ct, 1)] if idx == 3 else (), wait=[(cv, tlrow)] if idx == 0 else ())
            tlb = ct.n
            for idx in range(4):
                p.op("vector", (lambda e, idx=idx: e.tensor_mul(
                    Gtf[:, 128 * idx:128 * idx + 64], p2[:, 64 * idx:64 * (idx + 1)],
                    Ef[:, 128 * idx + 1:128 * (idx + 1):2])), wait=[(ct, tlb)] if idx == 0 else ())
            for idx in range(4):
                p.op("vector", (lambda e, idx=idx: e.tensor_mul(
                    bf(Poq)[:, 64 * idx:64 * (idx + 1)], Gtf[:, 128 * idx:128 * idx + 64],
                    t["m2c"][:, :])))
            for idx in range(4):
                p.op("vector", (lambda e, idx=idx, hh=heads[idx], j=j: e.tensor_mul(
                    bf(Qppq)[:, 64 * idx:64 * (idx + 1)],
                    ybf[:, 1024 * hh + 64 * j:1024 * hh + 64 * j + 64],
                    p2[:, 256 + 64 * idx:256 + 64 * (idx + 1)])),
                     inc=[(cv, 1)] if idx == 3 else ())
            tpo = cv.n
            for idx in range(4):
                p.op("tensor", (lambda e, idx=idx: e.matmul(
                    p3[:, 64 * idx:64 * (idx + 1)], bf(Uq)[:, 128 * idx:128 * (idx + 1)],
                    bf(Poq)[:, 64 * idx:64 * (idx + 1)], start=True, stop=False)),
                     wait=[(cv, tpo), (cv, u_done)] if idx == 0 else ())
                p.op("tensor", (lambda e, idx=idx: e.matmul(
                    p3[:, 64 * idx:64 * (idx + 1)], bf(Sbq)[:, 128 * idx:128 * (idx + 1)],
                    bf(Qppq)[:, 64 * idx:64 * (idx + 1)], start=False, stop=True)))
            for idx in range(4):
                p.op("tensor", (lambda e, idx=idx: e.matmul(
                    p5[:, 128 * idx:128 * (idx + 1)], bf(KtTq)[:, 128 * idx:128 * (idx + 1)],
                    bf(Uq)[:, 128 * idx:128 * (idx + 1)], start=True, stop=True)),
                     inc=[(ct, 1)] if idx == 3 else (),
                     wait=[(cv, ktt_done), bank_ev[rr - 2]] if idx == 0 else ())
            tos = ct.n
            prev_pe_end = tos
            for idx, h in enumerate(heads):
                p.op("vector", (lambda e, idx=idx, h=h, j=j: e.tensor_copy(
                    ybf[:, 8192 + 1024 * h + 64 * j:8192 + 1024 * h + 64 * j + 64],
                    p3[:, 64 * idx:64 * (idx + 1)])), wait=[(ct, tos)] if idx == 0 else ())
            for idx, h in enumerate(heads):
                p.op("vector", (lambda e, idx=idx, h=h, n=ncol[idx]: e.tensor_scalar(
                    S_ar[:, 128 * h:128 * (h + 1)], S_ar[:, 128 * h:128 * (h + 1)],
                    SP[:, 256 + n:257 + n], None, op0=ALU.mult)))
                p.op("vector", (lambda e, idx=idx, h=h: e.tensor_add(
                    S_ar[:, 128 * h:128 * (h + 1)], S_ar[:, 128 * h:128 * (h + 1)],
                    p5[:, 128 * idx:128 * (idx + 1)])), inc=[(cv, 1)])
                s_upd[h] = cv.n
            if getattr(C, "dbg_scan", None):
                C.dbg_scan(j, q)

    sfinal = flush(C, "vector")

    def exchange_pub2(e):
        e.wait_ge(p.real["cv"], sfinal)
        with e.register("cntD") as cnt, e.register("rroleD") as rrole:
            e.reg_mov(cnt, 0)
            e.reg_load(rrole, t["sroleB"][0:1, 0:1])
            with e.If_eq(rrole, 0):
                e.dma_start(out=sh_smid[0], in_=S_ar[:, :]).then_inc(p.real["exchC"], 16)
                e.reg_add(cnt, cnt, 16)
                e.wait_ge(p.real["exchC"], cnt)
                e.dma_start(out=sh_flag[2:3, 0:1], in_=t["sroleB"][0:1, 1:2]).then_inc(p.real["exchC"], 16)
                e.reg_add(cnt, cnt, 16)
                e.wait_ge(p.real["exchC"], cnt)
        e.nop().then_inc(p.real["exdC"], 1)

    p.raw("sync", exchange_pub2, inc=[(exdC, 1)])
    barrier(C, "l0a_scan")


def emit_l0attn_post(C, Wos, gg_work, x_work, nch=16):
    p, t, T = C.p, C.t, C.T
    cv, cs, ct, cp = p.sem("cv"), p.sem("cs"), p.sem("ct"), p.sem("cp")
    ybf = t["ybf"]
    KCV = t["x_res"][:, :].bitcast(BF16)
    R = t["rows32a"]
    oar = lambda h: ybf[:, 8192 + 1024 * h:8192 + 1024 * (h + 1)]

    prev_gg = 0
    for h in range(8):
        ld = p.dma("gpsimd", t["ggtmp"][:, :], gg_work[h],
                   wait=[(cv, prev_gg)] if h > 0 else ())
        p.op("vector", (lambda e, h=h: e.tensor_mul(t["sqA"][:, :], oar(h)[:, :], oar(h)[:, :])),
             inc=[(cv, 1)])
        cvt = cv.n
        for g in range(2):
            p.op("tensor", (lambda e, g=g: e.matmul(
                [t["p6"], t["p7"]][g][0:1, :], t["ones128_bf"][:, 0:1],
                t["sqA"][:, 512 * g:512 * (g + 1)], start=True, stop=True)),
                 inc=[(ct, 1)] if g == 1 else (), wait=[(cv, cvt)] if g == 0 else ())
        ctt = ct.n
        p.op("scalar", (lambda e: e.activation(R[0:1, 0:512], t["p6"][0:1, :], AF.Sqrt,
                                               scale=1.0 / 128.0, bias=t["constcol"][0:1, 0:1])),
             wait=[(ct, ctt)])
        p.op("scalar", (lambda e: e.activation(R[0:1, 512:1024], t["p7"][0:1, :], AF.Sqrt,
                                               scale=1.0 / 128.0, bias=t["constcol"][0:1, 0:1])),
             inc=[(cs, 1)])
        cst = cs.n
        p.op("vector", (lambda e: e.reciprocal(R[0:1, :], R[0:1, :])),
             inc=[(cv, 1)], wait=[(cs, cst)])
        cvt = flush(C, "vector")
        for g in range(2):
            p.op("tensor", (lambda e, g=g: e.matmul(
                [t["p6"], t["p7"]][g][:, :], t["onesr_f"][:, :], R[0:1, 512 * g:512 * (g + 1)],
                start=True, stop=True)), inc=[(ct, 1)] if g == 1 else (),
                 wait=[(cv, cvt)] if g == 0 else ())
        ctt = ct.n
        for g in range(2):
            p.op("vector", (lambda e, h=h, g=g: e.tensor_mul(
                t["sqB"][:, 0:512], oar(h)[:, 512 * g:512 * (g + 1)],
                [t["p6"], t["p7"]][g][:, :])), wait=[(ct, ctt), ld] if g == 0 else [ld])
            p.op("vector", (lambda e, g=g: e.tensor_scalar(
                t["sqB"][:, 0:512], t["sqB"][:, 0:512], t["onw"][:, 0:1], None, op0=ALU.mult)))
            p.op("vector", (lambda e, h=h, g=g: e.tensor_mul(
                oar(h)[:, 512 * g:512 * (g + 1)], t["sqB"][:, 0:512],
                t["ggtmp"][:, 512 * g:512 * (g + 1)])),
                 inc=[(cv, 1)] if g == 1 else ())
        prev_gg = cv.n
    opost_done = flush(C, "vector")

    barrier(C, "l0a_opost")

    pb4 = [t["p0"], t["p1"], t["p2"], t["p3"]]
    mm_done = {}
    ev_done = {}
    for o in range(nch):
        wb = [t["wsA"], t["wsB"]][o % 2]
        w = [(ct, mm_done[o - 2])] if o >= 2 else []
        ldt = p.dma("gpsimd", wb[:, 0:1024], Wos[o], wait=w)
        for g in range(2):
            pb = pb4[(2 * o + g) % 4]
            wmm = [ldt, (cv, opost_done)]
            if (2 * o + g) >= 4:
                wmm.append((cv, ev_done[2 * o + g - 4]))
            for c in range(8):
                p.op("tensor", (lambda e, c=c, o=o, g=g, pb=pb, wb=wb: e.matmul(
                    pb[:, :], wb[:, 128 * c:128 * (c + 1)],
                    ybf[:, 8192 + 1024 * c + 512 * g:8192 + 1024 * c + 512 * (g + 1)],
                    start=(c == 0), stop=(c == 7))),
                     inc=[(ct, 1)] if c == 7 else (), wait=wmm if c == 0 else ())
            mmt = ct.n
            p.op("vector", (lambda e, o=o, g=g, pb=pb: e.tensor_copy(
                KCV[:, 1024 * o + 512 * g:1024 * o + 512 * (g + 1)], pb[:, :])),
                 inc=[(cv, 1)], wait=[(ct, mmt)])
            ev_done[2 * o + g] = cv.n
        mm_done[o] = ct.n

    barrier(C, "l0a_wo")

    stats_ready = emit_ln_stats(C, lambda c: KCV[:, 1024 * c:1024 * (c + 1)], nch,
                                t["m2_B"], t["r2_B"])
    acc_tgt = {}
    sname = p.sem("dma_gpsimd", "sw")
    for c in range(nch):
        for g in range(2):
            i2 = (2 * c + g) % 2
            stg = t["tmp32e"] if i2 == 0 else t["e1"]
            wv = []
            if (2 * c + g) >= 2:
                wv.append((sname, acc_tgt[(2 * c + g) - 2]))
            sl = slice(1024 * c + 512 * g, 1024 * c + 512 * (g + 1))
            slg = slice(512 * g, 512 * (g + 1))
            p.op("vector", (lambda e, sl=sl, slg=slg: e.tensor_sub(
                t["sqA"][:, 0:512], KCV[:, sl], t["m2_B"][:, slg])), wait=wv)
            p.op("vector", (lambda e, slg=slg, stg=stg: e.tensor_mul(
                stg[:, :], t["sqA"][:, 0:512], t["r2_B"][:, slg])),
                 inc=[(cv, 1)])
            cvt = cv.n
            tgt = p.dma("gpsimd", x_work[128 * c:128 * (c + 1), 512 * g:512 * (g + 1)],
                        stg[:, :], wait=[(cv, cvt)], accum=True)
            acc_tgt[(2 * c + g)] = tgt[1]
    p.op("gpsimd", lambda e: e.nop(), wait=[(sname, sname.n)])
    rl = None
    for c in range(nch):
        rl = p.dma("sync", chunk(t["x_res"], c, T), x_work[128 * c:128 * (c + 1), :],
                   wait=[(sname, sname.n)] if c == 0 else ())
    p.op("sync", (lambda e: e.nop()), wait=[rl])
    barrier(C, "l0a_done")



def build_fused(T=1024):
    I32 = mybir.dt.int32
    nc = bass.Bass("TRN2", target_bir_lowering=False, debug=False,
                   detect_race_conditions=False)
    dram = {}

    def di(nm, sh, dt=F32):
        dram[nm] = nc.dram_tensor(nm, sh, dt, kind="ExternalInput").ap()
        return dram[nm]

    x_in = di("x_in", [2048, T])
    Wqs = di("Wqs", [8, 128, 2048]); Wks = di("Wks", [16, 128, 2048])
    Wvs = di("Wvs", [16, 128, 2048]); Wgs = di("Wgs", [8, 128, 2048])
    Wos = di("Wos", [16, 128, 1024])
    wsmall_d = di("wsmall_d", [128, 512]); convw_d = di("convw_d", [128, 160])
    acol_d = di("acol_d", [128, 8]); di("onw_d", [128, 1])
    di("role_d", [1, 2], I32)
    di("cumsL_d", [128, 64]); di("dupB_d", [64, 128])
    di("dupE_d", [128, 128]); di("dupO_d", [128, 128])
    di("sel63_d", [64, 1])
    di("m1c4_d", [128, 512], BF16); di("m2c_d", [128, 64], BF16)
    W0t = di("W0t", [2048, 2048]); di("d_biasc", [128, 16])
    wqs = di("wqs", [16, 128, 2048]); wos = di("wos", [16, 128, 2048])
    di("wkvp", [128, 4096]); di("d_bq", [128, 16]); di("d_bkv", [128, 2])
    di("d_bo", [128, 16]); di("d_role", [1, 2], I32)
    W1s = di("W1s", [64, 128, 2048]); W2s = di("W2s", [16, 2, 128, 4096])
    di("d_ones_bf", [128, 128], BF16); di("d_ones_f", [128, 128])
    di("d_onesr_f", [1, 128]); di("d_id_bf", [128, 128], BF16)
    di("d_id_f", [128, 128])

    x_out = nc.dram_tensor("x_out", [2048, T], F32, kind="ExternalOutput").ap()
    x_work = nc.dram_tensor("x_work", [2048, T], F32, kind="Internal").ap()
    W1b = nc.dram_tensor("W1b", [64, 128, 2048], BF16, kind="Internal").ap()
    W2b = nc.dram_tensor("W2b", [16, 2, 128, 4096], BF16, kind="Internal").ap()
    wqb = nc.dram_tensor("wqb", [16, 128, 2048], BF16, kind="Internal").ap()
    wob = nc.dram_tensor("wob", [16, 128, 2048], BF16, kind="Internal").ap()
    gg_work = nc.dram_tensor("gg_work", [8, 128, 1024], BF16, kind="Internal").ap()
    sh_halo = nc.dram_tensor("sh_halo", [2, 128, 128], BF16, kind="Internal", addr_space="Shared").ap()
    sh_smid = nc.dram_tensor("sh_smid", [1, 128, 1024], F32, kind="Internal", addr_space="Shared").ap()
    sh_flagB = nc.dram_tensor("sh_flagB", [4, 8], I32, kind="Internal", addr_space="Shared").ap()
    sh_kv = nc.dram_tensor("sh_kv", [2, 256, 1024], BF16, kind="Internal", addr_space="Shared").ap()
    sh_flag = nc.dram_tensor("sh_flag", [2, 8], I32, kind="Internal", addr_space="Shared").ap()

    with ExitStack() as ctx:
        prog = Prog(nc)
        C = Ctx(nc, prog)
        alloc_common(C, ctx, T)
        C.sb(ctx, "_ovl", [128, 9856], F32)
        C.ovl_cap = 9856 * 4
        C.ovl_skip = {"sroleA", "sflagA", "spollA", "sroleB", "sflagB", "spollB"}
        C.ovl_on = True
        C.ovl_cursor = 0; alloc_l0attn(C, ctx)
        ovl_l0a = C.ovl_cursor
        C.ovl_cursor = 0; alloc_l0ffn(C, ctx)
        ovl_l0f = C.ovl_cursor
        C.ovl_cursor = 0; alloc_l1attn(C, ctx)
        ovl_l1a = C.ovl_cursor
        C.ovl_cursor = 0; alloc_l1ffn(C, ctx)
        ovl_l1f = C.ovl_cursor
        C.ovl_on = False
        assert max(ovl_l0a, ovl_l0f, ovl_l1a, ovl_l1f) <= C.ovl_cap
        t = C.t
        p = prog

        def fence(tag):
            waits = [(p.sem(nm), p.sem(nm).n) for nm in ("cv", "cs", "ct", "cp")]
            p.dma("gpsimd", t["tok8"][0:1, 0:8], x_work[0:1, 0:8], wait=waits)
            p.dma("sync", t["tok8"][0:1, 8:16], x_work[0:1, 8:16], wait=waits)

        def loads_done(tag):
            for q in ("gpsimd", "sync"):
                s = p.sem("dma_" + q, "sw" if q == "gpsimd" else "hw")
                p.op(q if q == "gpsimd" else "sync", lambda e: e.nop(), wait=[(s, s.n)])
            barrier(C, tag)

        for c in range(16):
            p.dma("gpsimd", chunk(t["x_res"], c, T), x_in[128 * c:128 * (c + 1), :])
        p.dma("gpsimd", t["wsmallb"][:, :], dram["wsmall_d"][:, 0:256])
        for nm, tile in [("wsmall_d", "wsmall"), ("convw_d", "convw"), ("acol_d", "acol"),
                         ("onw_d", "onw"), ("role_d", "sroleB"), ("d_ones_bf", "ones128_bf"),
                         ("d_ones_f", "ones128_f"), ("d_onesr_f", "onesr_f"),
                         ("d_id_bf", "ident_bf"), ("d_id_f", "ident_f"),
                         ("cumsL_d", "cumsL"), ("dupB_d", "dupB"), ("dupE_d", "dupE"),
                         ("dupO_d", "dupO"), ("sel63_d", "sel63"), ("m1c4_d", "m1c4"),
                         ("m2c_d", "m2c")]:
            p.dma("gpsimd", t[tile][:, :], dram[nm])
        p.op("vector", lambda e: e.memset(t["xbf"][:, :], 0.0))
        p.op("vector", lambda e: e.memset(t["constcol"][:, 0:1], 1e-5))
        p.op("vector", lambda e: e.memset(t["constcol"][:, 1:2], 1.0))
        p.op("vector", lambda e: e.memset(t["sflagB"][:, :], 21))
        loads_done("loads0")

        p.cur_scope = "l0a_pre"
        emit_l0attn_pre(C, Wqs, Wks, Wvs, Wgs, wsmall_d, convw_d, acol_d,
                        x_work, gg_work, sh_halo, sh_flagB)
        p.cur_scope = "wconv"
        for i in range(64):
            p.dma("gpsimd", W1b[i], W1s[i])
        for oc in range(16):
            for kb in range(2):
                p.dma("gpsimd", W2b[oc, kb], W2s[oc, kb])
        for o in range(16):
            p.dma("gpsimd", wqb[o], wqs[o])
            p.dma("gpsimd", wob[o], wos[o])
        p.cur_scope = "l0a_scan"
        emit_l0attn_scan(C, sh_smid, sh_flagB)
        p.cur_scope = "l0a_post"
        emit_l0attn_post(C, Wos, gg_work, x_work)
        p.cur_scope = None

        fence("b")
        p.dma("gpsimd", t["bias_cols"][:, :], dram["d_biasc"])
        loads_done("loadsB")
        p.cur_scope = "l0ffn"
        emit_l0ffn(C, lambda c: chunk(t["x_res"], c, T), W0t)
        p.cur_scope = None

        fence("c")
        p.dma("gpsimd", t["wkv"][:, :], dram["wkvp"])
        p.dma("gpsimd", t["biasq"][:, :], dram["d_bq"])
        p.dma("gpsimd", t["biaskv"][:, :], dram["d_bkv"])
        p.dma("gpsimd", t["biaso"][:, :], dram["d_bo"])
        p.dma("gpsimd", t["sroleA"][:, :], dram["d_role"])
        p.op("vector", lambda e: e.memset(t["sflagA"][:, :], 7))
        loads_done("loadsC")
        p.cur_scope = "l1attn"
        emit_l1attn(C, wqb, wob, dram["d_role"], sh_kv, sh_flag, tag=0, dbg=None)
        p.cur_scope = None

        fence("d")
        loads_done("loadsD")
        emit_l1ffn(C, W1b, W2b, x_out)
        p.cur_scope = None

        sfin = p.sem("dma_gpsimd", "sw")
        p.op("gpsimd", lambda e: e.nop(), wait=[(sfin, sfin.n)])
        p.op("sync", lambda e: e.nop())
        prog.emit(ctx)
    return nc


import ml_dtypes
from concourse.bass_utils import run_bass_kernel_spmd

T = 1024


def kernel(**kw):
    f32 = np.float32
    x = np.asarray(kw["L0_attn_x"], f32)
    cores = list(range(8))

    def slab(W, no):
        return np.ascontiguousarray(
            W.reshape(no, 128, 16, 128).transpose(0, 3, 2, 1).reshape(no, 128, 2048))

    Wb = np.asarray(kw["L0_attn_Wb"], f32); Wa = np.asarray(kw["L0_attn_Wa"], f32)
    wsmall = np.zeros((128, 512), f32)
    wsmall[:, 0:256] = Wb.reshape(16, 16, 128).transpose(2, 1, 0).reshape(128, 256)
    wsmall[:, 256:384] = Wa.reshape(8, 16, 128).transpose(2, 1, 0).reshape(128, 128)
    cq = np.asarray(kw["L0_attn_conv_q"], f32); ck = np.asarray(kw["L0_attn_conv_k"], f32)
    cvv = np.asarray(kw["L0_attn_conv_v"], f32)
    convw = np.zeros((128, 160), f32)
    convw[:, 0:32] = cq[:, ::-1].reshape(8, 128, 4).transpose(1, 0, 2).reshape(128, 32)
    convw[:, 32:96] = ck[:, ::-1].reshape(16, 128, 4).transpose(1, 0, 2).reshape(128, 64)
    convw[:, 96:160] = cvv[:, ::-1].reshape(16, 128, 4).transpose(1, 0, 2).reshape(128, 64)
    acol = np.zeros((128, 8), f32)
    acol[0:8, 0] = np.asarray(kw["L0_attn_A_log"], f32)
    acol[0:8, 1] = np.asarray(kw["L0_attn_dt_bias"], f32)
    sidx = np.arange(64)[:, None]; tidx = np.arange(128)[None, :]
    wk = np.asarray(kw["L1_attn_wk"], f32); wv = np.asarray(kw["L1_attn_wv"], f32)
    wks = wk.reshape(16, 128, 128).transpose(1, 0, 2).reshape(128, 2048)
    wvs = wv.reshape(16, 128, 128).transpose(1, 0, 2).reshape(128, 2048)
    wq1 = np.asarray(kw["L1_attn_wq"], f32); wo1 = np.asarray(kw["L1_attn_wo"], f32)
    W1 = np.asarray(kw["L1_ffn_W1"], f32); W2 = np.asarray(kw["L1_ffn_W2"], f32)
    common = {
        "Wqs": slab(np.asarray(kw["L0_attn_Wq"], f32), 8),
        "Wks": slab(np.asarray(kw["L0_attn_Wk"], f32), 16),
        "Wvs": slab(np.asarray(kw["L0_attn_Wv"], f32), 16),
        "Wgs": slab(np.asarray(kw["L0_attn_Wg"], f32), 8),
        "Wos": np.ascontiguousarray(np.asarray(kw["L0_attn_Wo"], f32)
                                    .reshape(16, 128, 8, 128).transpose(0, 3, 2, 1)
                                    .reshape(16, 128, 1024)),
        "wsmall_d": wsmall, "convw_d": convw, "acol_d": acol,
        "onw_d": np.ascontiguousarray(np.asarray(kw["L0_attn_o_norm_w"], f32).reshape(128, 1)),
        "cumsL_d": np.tile((sidx <= np.arange(64)[None, :]).astype(f32), (2, 1)),
        "dupB_d": (tidx // 2 == sidx).astype(f32),
        "dupE_d": np.tile((tidx == 2 * sidx).astype(f32), (2, 1)),
        "dupO_d": np.tile((tidx == 2 * sidx + 1).astype(f32), (2, 1)),
        "sel63_d": (sidx == 63).astype(f32),
        "m1c4_d": np.tile(-np.triu(np.ones((128, 128), f32), 1), (1, 4)).astype(ml_dtypes.bfloat16),
        "m2c_d": (np.arange(128)[:, None] <= 2 * np.arange(64)[None, :] + 1).astype(ml_dtypes.bfloat16),
        "W0t": np.ascontiguousarray(np.asarray(kw["L0_ffn_weight"], f32).T),
        "d_biasc": np.ascontiguousarray(np.asarray(kw["L0_ffn_bias"], f32).reshape(16, 128).T),
        "wqs": np.ascontiguousarray(wq1.reshape(16, 128, 16, 128).transpose(2, 1, 0, 3).reshape(16, 128, 2048)),
        "wos": np.ascontiguousarray(wo1.reshape(16, 128, 16, 128).transpose(2, 1, 0, 3).reshape(16, 128, 2048)),
        "wkvp": np.ascontiguousarray(np.concatenate([wks, wvs], axis=1)),
        "d_bq": np.ascontiguousarray(np.asarray(kw["L1_attn_bq"], f32).reshape(16, 128).T),
        "d_bkv": np.ascontiguousarray(np.stack([np.asarray(kw["L1_attn_bk"], f32),
                                                np.asarray(kw["L1_attn_bv"], f32)], 1)),
        "d_bo": np.ascontiguousarray(np.asarray(kw["L1_attn_bo"], f32).reshape(16, 128).T),
        "W1s": np.ascontiguousarray(W1.reshape(64, 128, 16, 128).transpose(0, 3, 2, 1).reshape(64, 128, 2048)),
        "W2s": np.ascontiguousarray(W2.reshape(16, 128, 2, 32, 128).transpose(0, 2, 4, 3, 1).reshape(16, 2, 128, 4096)),
        "d_ones_bf": np.ones((128, 128), ml_dtypes.bfloat16),
        "d_ones_f": np.ones((128, 128), f32),
        "d_onesr_f": np.ones((1, 128), f32),
        "d_id_bf": np.eye(128).astype(ml_dtypes.bfloat16),
        "d_id_f": np.eye(128).astype(f32),
    }

    in_maps = []
    for c in cores:
        m = dict(common)
        m["x_in"] = np.ascontiguousarray(x[c // 2, 1024 * (c % 2):1024 * (c % 2) + 1024, :].T)
        m["role_d"] = np.array([[c % 2, 33]], np.int32)
        m["d_role"] = np.array([[c % 2, 0]], np.int32)
        in_maps.append(m)

    nc = build_fused(T)
    res = run_bass_kernel_spmd(nc, in_maps, core_ids=cores)

    out = np.empty((4, 2048, 2048), f32)
    for c in cores:
        out[c // 2, 1024 * (c % 2):1024 * (c % 2) + 1024, :] = np.asarray(res.results[c]["x_out"]).T
    return out


# version 04 (final)
# speedup vs baseline: 1.2157x; 1.0941x over previous
import numpy as np
from contextlib import ExitStack

import concourse.bass as bass
import concourse.mybir as mybir

F32 = mybir.dt.float32
BF16 = mybir.dt.bfloat16
AF = mybir.ActivationFunctionType
ALU = mybir.AluOpType
AX = mybir.AxisListType

MAGIC = 1.5 * 2.0 ** 23
ENGINES = ("sync", "gpsimd", "vector", "scalar", "tensor")


class SemT:
    __slots__ = ("name", "n", "kind")

    def __init__(self, name, kind):
        self.name = name
        self.n = 0
        self.kind = kind


class Op:
    __slots__ = ("eng", "fn", "incs", "waits", "scope")

    def __init__(self, eng, fn, incs, waits, scope=None):
        self.eng = eng
        self.fn = fn
        self.incs = incs
        self.waits = waits
        self.scope = scope


class Prog:
    def __init__(self, nc):
        self.nc = nc
        self.ops = []
        self.sems = {}

    def sem(self, name, kind="c"):
        s = self.sems.get(name)
        if s is None:
            s = self.sems[name] = SemT(name, kind)
        assert s.kind == kind, (name, s.kind, kind)
        return s

    def op(self, eng, fn, inc=(), wait=()):
        assert eng in ENGINES
        self.ops.append(Op(eng, fn, tuple(inc), tuple(wait), getattr(self, "cur_scope", None)))
        for s, k in inc:
            s.n += k

    def raw(self, eng, fn, inc=(), wait=()):
        self.ops.append(Op(eng, fn, (), tuple(wait), getattr(self, "cur_scope", None)))
        for s, k in inc:
            s.n += k

    def now(self, *sems):
        return tuple((s, s.n) for s in sems)

    def dma(self, queue, out_ap, in_ap, wait=(), accum=False):
        kind = "sw" if queue == "gpsimd" else "hw"
        s = self.sem("dma_" + queue, kind)

        def fn(e, out_ap=out_ap, in_ap=in_ap):
            if accum:
                return e.dma_start(out=out_ap, in_=in_ap, accum_op=mybir.AluOpType.add)
            return e.dma_start(out=out_ap, in_=in_ap)
        self.op(queue, fn, inc=[(s, 16)], wait=wait)
        return (s, s.n)

    def emit(self, ctx):
        nc = self.nc
        real = {}
        for name, s in self.sems.items():
            assert s.n < 60000, (name, s.n)
            real[name] = ctx.enter_context(nc.semaphore(name))
        self.real = real
        block = ctx.enter_context(nc.Block())
        prog = self

        def mk(eng_name):
            def run(e):
                for o in prog.ops:
                    if o.eng != eng_name:
                        continue
                    for s, tgt in o.waits:
                        if tgt > 0:
                            e.wait_ge(real[s.name], tgt)
                    if o.scope:
                        with nc.named_scope(o.scope):
                            inst = o.fn(e)
                    else:
                        inst = o.fn(e)
                    for s, k in o.incs:
                        inst.then_inc(real[s.name], k)
            return run

        for eng_name in ENGINES:
            getattr(block, eng_name)(mk(eng_name))
        return real



def T_(t):
    return t[:, :]


class Ctx:

    def __init__(self, nc, prog):
        self.nc = nc
        self.p = prog
        self.t = {}

    def sb(self, ctx, name, shape, dt=F32):
        if getattr(self, "ovl_on", False) and name not in getattr(self, "ovl_skip", ()):
            epe = {F32: 4, BF16: 2}.get(dt, 4)
            off = (self.ovl_cursor + 63) // 64 * 64
            nbytes = shape[1] * epe
            base = self.t["_ovl"]
            ap = base[:, :].bitcast(dt)
            h = ap[0:shape[0], off // epe:off // epe + shape[1]]
            self.ovl_cursor = off + nbytes
            assert self.ovl_cursor <= self.ovl_cap, (
                f"overlay overflow at {name}: {self.ovl_cursor}")
            self.t[name] = h
            return h
        h = ctx.enter_context(self.nc.sbuf_tensor(name, shape, dt))
        self.t[name] = h
        return h

    def ps(self, ctx, name, shape, dt=F32):
        h = ctx.enter_context(self.nc.psum_tensor(name, shape, dt))
        self.t[name] = h
        return h



def alloc_common(C, ctx, T):
    C.T = T
    C.sb(ctx, "x_res", [128, 16 * T], F32)
    C.sb(ctx, "xbf", [128, 16 * T], BF16)
    C.sb(ctx, "ybf", [128, 16 * T], BF16)
    C.sb(ctx, "sqA", [128, T], BF16)
    C.sb(ctx, "sqB", [128, T], BF16)
    C.sb(ctx, "maxsq", [128, T], BF16)
    C.sb(ctx, "tmp32e", [128, 512], F32)
    C.sb(ctx, "e1", [128, 512], F32)
    C.sb(ctx, "rows32a", [128, T], F32)
    C.sb(ctx, "rows32b", [128, T], F32)
    C.sb(ctx, "m_B", [128, T], BF16)
    C.sb(ctx, "rstd_B", [128, T], F32)
    C.sb(ctx, "m2_B", [128, T], BF16)
    C.sb(ctx, "r2_B", [128, T], F32)
    C.sb(ctx, "F_B", [128, T], F32)
    C.sb(ctx, "AS_B", [128, T], F32)
    C.sb(ctx, "tok8", [128, 64], F32)
    C.sb(ctx, "colmisc", [128, 8], F32)
    C.sb(ctx, "constcol", [128, 4], F32)
    C.t["tmp32b"] = C.t["wabs"] if "wabs" in C.t else None
    C.sb(ctx, "ones128_bf", [128, 128], BF16)
    C.sb(ctx, "ones128_f", [128, 128], F32)
    C.sb(ctx, "onesr_f", [1, 128], F32)
    C.sb(ctx, "ident_bf", [128, 128], BF16)
    C.sb(ctx, "ident_f", [128, 128], F32)
    for i in range(8):
        C.ps(ctx, f"p{i}", [128, 512], F32)


def barrier(C, tag=""):
    p = C.p
    b = p.sem("bar")
    for eng in ENGINES:
        if eng != "sync":
            p.op(eng, lambda e: e.drain())
        p.op(eng, lambda e: e.nop(), inc=[(b, 1)])
    tgt = b.n
    for eng in ENGINES:
        p.op(eng, lambda e: e.nop(), wait=[(b, tgt)])


def chunk(tile, c, T):
    return tile[:, c * T: (c + 1) * T]


ESEM = {"vector": "cv", "scalar": "cs", "tensor": "ct", "gpsimd": "cp"}


def flush(C, eng):
    p = C.p
    s = p.sem(ESEM[eng])
    p.op(eng, lambda e: e.drain(), wait=[(s, s.n)])
    p.op(eng, lambda e: e.nop(), inc=[(s, 1)])
    return s.n



def emit_ln_stats(C, src, nch, out_m, out_r, cast_src=None):
    p, t, T = C.p, C.t, C.T
    cv, cs, ct = p.sem("cv"), p.sem("cs"), p.sem("ct")
    ones = t["ones128_bf"]
    banks = [t["p0"], t["p1"], t["p2"], t["p3"]]
    NG = T // 512
    sq = [t["sqA"], t["sqB"]]
    mm_done = {}
    for c in range(nch):
        if cast_src is not None:
            p.op("vector", (lambda e, c=c: e.tensor_copy(src(c), cast_src(c))))
        w = [(ct, mm_done[c - 2])] if c >= 2 else []
        p.op("vector", (lambda e, c=c: e.tensor_mul(sq[c % 2][:, :], src(c), src(c))),
             inc=[(cv, 1)], wait=w)
        cvt = cv.n
        for g in range(NG):
            p.op("tensor", (lambda e, c=c, g=g: e.matmul(
                banks[g][:, :], ones[:, :], src(c)[:, 512 * g:512 * (g + 1)],
                start=(c == 0), stop=(c == nch - 1))),
                 wait=[(cv, cvt)] if g == 0 else ())
        for g in range(NG):
            p.op("tensor", (lambda e, c=c, g=g: e.matmul(
                banks[NG + g][:, :], ones[:, :], sq[c % 2][:, 512 * g:512 * (g + 1)],
                start=(c == 0), stop=(c == nch - 1))),
                 inc=[(ct, 1)] if g == NG - 1 else ())
        mm_done[c] = ct.n
    stats_done = ct.n
    for g in range(NG):
        sl = slice(512 * g, 512 * (g + 1))
        p.op("vector", (lambda e, g=g, sl=sl: e.tensor_scalar(
            out_m[:, sl], banks[g][:, :], 1.0 / (128 * nch), None, op0=ALU.mult)),
             wait=[(ct, stats_done)] if g == 0 else ())
        p.op("vector", (lambda e, g=g, sl=sl: e.tensor_scalar(
            t["rows32a"][:, sl], banks[NG + g][:, :], 1.0 / (128 * nch), None, op0=ALU.mult)))
        p.op("vector", (lambda e, sl=sl: e.tensor_mul(
            t["rows32b"][:, sl], out_m[:, sl], out_m[:, sl])))
        p.op("vector", (lambda e, sl=sl: e.tensor_sub(
            t["rows32a"][:, sl], t["rows32a"][:, sl], t["rows32b"][:, sl])),
             inc=[(cv, 1)])
        cvt = cv.n
        p.op("scalar", (lambda e, sl=sl: e.activation(
            t["rows32b"][:, sl], t["rows32a"][:, sl], AF.Sqrt, bias=t["constcol"][:, 0:1])),
             inc=[(cs, 1)], wait=[(cv, cvt)])
        cst = cs.n
        p.op("vector", (lambda e, sl=sl: e.reciprocal(out_r[:, sl], t["rows32b"][:, sl])),
             inc=[(cv, 1)], wait=[(cs, cst)])
    return cv.n



def alloc_l0ffn(C, ctx):
    C.sb(ctx, "wraw0", [128, 512], F32)
    C.sb(ctx, "wraw1", [128, 512], F32)
    C.sb(ctx, "wabs", [128, 512], F32)
    C.sb(ctx, "tw", [128, 16 * 1024], BF16)
    C.sb(ctx, "bias_cols", [128, 16], F32)


def emit_l0ffn(C, get_x, W0t_dram, nch=16):
    p, t, T = C.p, C.t, C.T
    cv, cs, ct, cp = p.sem("cv"), p.sem("cs"), p.sem("ct"), p.sem("cp")
    xbf, ybf = t["xbf"], t["ybf"]
    NG = T // 512
    JT = T // 128
    wraw = [t["wraw0"], t["wraw1"]]
    wabs, tw = t["wabs"], t["tw"]

    abs_done, mm_done = {}, {}
    for hc in range(4 * nch):
        c, q = hc // 4, hc % 4
        w = [(cs, abs_done[hc - 2])] if hc >= 2 else []
        ldt = p.dma("sync", wraw[hc % 2][:, :],
                    W0t_dram[128 * c:128 * (c + 1), 512 * q:512 * (q + 1)], wait=w)
        w2 = [ldt] + ([(ct, mm_done[hc - 1])] if hc >= 1 else [])
        p.op("scalar", (lambda e, hc=hc: e.activation(wabs[:, :], wraw[hc % 2][:, :], AF.Abs)),
             inc=[(cs, 1)], wait=w2)
        abs_done[hc] = cs.n
        cst = cs.n
        p.op("tensor", (lambda e, hc=hc: e.matmul(
            t["p5"][:, :], t["ones128_f"][:, :], wabs[:, :],
            start=(hc == 0), stop=(hc == 4 * nch - 1))),
             inc=[(ct, 1)], wait=[(cs, cst)])
        mm_done[hc] = ct.n
    alpha_mm = ct.n
    p.op("vector", (lambda e: e.tensor_copy(t["rows32a"][0:1, 0:512], t["p5"][0:1, :])),
         inc=[(cv, 1)], wait=[(ct, alpha_mm)])
    p.op("vector", (lambda e: e.tensor_reduce(
        t["rows32a"][0:1, 520:521], t["rows32a"][0:1, 0:512], axis=AX.X, op=ALU.add)),
         inc=[(cv, 1)], wait=[(cv, cv.n)])
    p.op("vector", (lambda e: e.tensor_scalar(
        t["rows32a"][0:1, 1:2], t["rows32a"][0:1, 520:521], 1.0 / (2048.0 * 2048.0), 1e-5,
        op0=ALU.mult, op1=ALU.max)), inc=[(cv, 1)], wait=[(cv, cv.n)])
    p.op("vector", (lambda e: e.reciprocal(t["rows32a"][0:1, 2:3], t["rows32a"][0:1, 1:2])),
         inc=[(cv, 1)], wait=[(cv, cv.n)])
    cvt = flush(C, "vector")
    p.op("tensor", (lambda e: e.matmul(t["p4"][:, 0:2], t["onesr_f"][:, :],
                                       t["rows32a"][0:1, 1:3], start=True, stop=True)),
         inc=[(ct, 1)], wait=[(cv, cvt)])
    ctt = ct.n
    p.op("vector", (lambda e: e.tensor_copy(t["colmisc"][:, 0:2], t["p4"][:, 0:2])),
         inc=[(cv, 1)], wait=[(ct, ctt)])

    barrier(C, "alpha")

    emit_ln_stats(C, lambda c: chunk(xbf, c, T), nch, t["m_B"], t["rstd_B"],
                  cast_src=get_x)
    for c in range(nch):
        p.op("vector", (lambda e, c=c: e.tensor_sub(chunk(xbf, c, T), chunk(xbf, c, T), t["m_B"][:, :])))
        p.op("vector", (lambda e, c=c: e.tensor_mul(t["sqA"][:, :], chunk(xbf, c, T), chunk(xbf, c, T))))
        if c == 0:
            p.op("vector", (lambda e: e.tensor_copy(t["maxsq"][:, :], t["sqA"][:, :])), inc=[(cv, 1)])
        else:
            p.op("vector", (lambda e: e.tensor_max(t["maxsq"][:, :], t["maxsq"][:, :], t["sqA"][:, :])),
                 inc=[(cv, 1)])
    ln_done = cv.n

    red_done = {}
    for j in range(JT):
        pb = t[f"p{6 + j % 2}"][:, :].bitcast(BF16)
        w = [(cv, ln_done)] if j < 2 else [(cv, red_done[j - 2])]
        p.op("tensor", (lambda e, j=j, pb=pb: e.transpose(
            pb[:, 0:128], t["maxsq"][:, 128 * j:128 * (j + 1)], t["ident_bf"][:, :])),
             inc=[(ct, 1)], wait=w)
        ctt = ct.n
        p.op("vector", (lambda e, j=j, pb=pb: e.tensor_reduce(
            t["tok8"][:, j:j + 1], pb[:, 0:128], axis=AX.X, op=ALU.max)),
             inc=[(cv, 1)], wait=[(ct, ctt)])
        red_done[j] = cv.n
    cp_done = {}
    for j in range(JT):
        pb = [t["p4"], t["p5"]][j % 2]
        w = [] if j < 2 else [(cv, cp_done[j - 2])]
        p.op("tensor", (lambda e, j=j, pb=pb: e.transpose(
            pb[:, 0:128], t["rstd_B"][:, 128 * j:128 * (j + 1)], t["ident_f"][:, :])),
             inc=[(ct, 1)], wait=w)
        ctt = ct.n
        p.op("vector", (lambda e, j=j, pb=pb: e.tensor_copy(
            t["tok8"][:, 8 + j:9 + j], pb[:, 0:1])),
             inc=[(cv, 1)], wait=[(ct, ctt)])
        cp_done[j] = cv.n
    tok_ready = flush(C, "vector")
    p.op("scalar", (lambda e: e.activation(t["tok8"][:, 16:16 + JT], t["tok8"][:, 0:JT], AF.Sqrt)),
         inc=[(cs, 1)], wait=[(cv, tok_ready)])
    cst = flush(C, "scalar")
    p.op("vector", (lambda e: e.tensor_mul(t["tok8"][:, 24:24 + JT], t["tok8"][:, 8:8 + JT],
                                           t["tok8"][:, 16:16 + JT])), inc=[(cv, 1)],
         wait=[(cs, cst), (cv, tok_ready)])
    p.op("vector", (lambda e: e.tensor_scalar(t["tok8"][:, 32:32 + JT], t["tok8"][:, 24:24 + JT],
                                              1e-5, 1.0 / 127.0, op0=ALU.max, op1=ALU.mult)),
         inc=[(cv, 1)], wait=[(cv, cv.n)])
    p.op("vector", (lambda e: e.reciprocal(t["tok8"][:, 40:40 + JT], t["tok8"][:, 32:32 + JT])),
         inc=[(cv, 1)], wait=[(cv, cv.n)])
    p.op("vector", (lambda e: e.tensor_mul(t["tok8"][:, 48:48 + JT], t["tok8"][:, 8:8 + JT],
                                           t["tok8"][:, 40:40 + JT])), inc=[(cv, 1)], wait=[(cv, cv.n)])
    p.op("vector", (lambda e: e.tensor_scalar(t["tok8"][:, 56:56 + JT], t["tok8"][:, 32:32 + JT],
                                              t["colmisc"][:, 0:1], None, op0=ALU.mult)),
         inc=[(cv, 1)], wait=[(cv, cv.n)])
    ftok_ready = flush(C, "vector")
    ev_done = {}
    for j in range(JT):
        pb = [t["p4"], t["p5"]][j % 2]
        w = [(cv, ftok_ready)] if j < 2 else [(cv, ev_done[j - 2])]
        p.op("tensor", (lambda e, j=j, pb=pb: e.matmul(
            pb[0:1, 0:128], t["tok8"][:, 48 + j:49 + j], t["ident_f"][:, :],
            start=True, stop=True)), wait=w)
        p.op("tensor", (lambda e, j=j, pb=pb: e.matmul(
            pb[0:1, 128:256], t["tok8"][:, 56 + j:57 + j], t["ident_f"][:, :],
            start=True, stop=True)), inc=[(ct, 1)])
        ctt = ct.n
        p.op("vector", (lambda e, j=j, pb=pb: e.tensor_copy(
            t["rows32a"][0:1, 128 * j:128 * (j + 1)], pb[0:1, 0:128])), wait=[(ct, ctt)])
        p.op("vector", (lambda e, j=j, pb=pb: e.tensor_copy(
            t["rows32b"][0:1, 128 * j:128 * (j + 1)], pb[0:1, 128:256])),
             inc=[(cv, 1)])
        ev_done[j] = cv.n
    rows_ready = flush(C, "vector")
    bc_done = {}
    for g in range(NG):
        pb = [t["p4"], t["p5"]][g % 2]
        w = [(cv, rows_ready)] if g < 2 else [(cv, bc_done[g - 2])]
        p.op("tensor", (lambda e, g=g, pb=pb: e.matmul(
            pb[:, :], t["onesr_f"][:, :], t["rows32a"][0:1, 512 * g:512 * (g + 1)],
            start=True, stop=True)), inc=[(ct, 1)], wait=w)
        ctt = ct.n
        p.op("vector", (lambda e, g=g, pb=pb: e.tensor_copy(
            t["F_B"][:, 512 * g:512 * (g + 1)], pb[:, :])), inc=[(cv, 1)], wait=[(ct, ctt)])
        bc_done[g] = cv.n
    for g in range(NG):
        pb = [t["p4"], t["p5"]][g % 2]
        w = [(cv, bc_done[g])] if g >= NG - 2 else [(cv, bc_done[g])]
        p.op("tensor", (lambda e, g=g, pb=pb: e.matmul(
            pb[:, :], t["onesr_f"][:, :], t["rows32b"][0:1, 512 * g:512 * (g + 1)],
            start=True, stop=True)), inc=[(ct, 1)], wait=w)
        ctt = ct.n
        p.op("vector", (lambda e, g=g, pb=pb: e.tensor_copy(
            t["AS_B"][:, 512 * g:512 * (g + 1)], pb[:, :])), inc=[(cv, 1)], wait=[(ct, ctt)])

    barrier(C, "scales")

    for c in range(nch):
        for g in range(NG):
            sl = slice(T * c + 512 * g, T * c + 512 * (g + 1))
            p.op("vector", (lambda e, sl=sl, g=g: e.tensor_mul(
                t["tmp32e"][:, :], xbf[:, sl], t["F_B"][:, 512 * g:512 * (g + 1)])))
            p.op("vector", (lambda e, sl=sl: e.tensor_scalar(
                xbf[:, sl], t["tmp32e"][:, :], MAGIC, MAGIC, op0=ALU.add, op1=ALU.subtract)),
                 inc=[(cv, 1)])
    qx_done = cv.n

    barrier(C, "qx")

    pbanks = [t["p0"], t["p1"], t["p2"], t["p3"]]
    evac_done = {}
    i = 0
    for po in range(2):
        q_done = {}
        for hc in range(2 * nch):
            c, q = hc // 2, hc % 2
            w = [(cv, q_done[hc - 2])] if hc >= 2 else []
            if po == 1 and hc == 0:
                w = w + [(ct, p.sem("ct").n)]
            ldt = p.dma("sync", wraw[hc % 2][:, :],
                        W0t_dram[128 * c:128 * (c + 1), 1024 * po + 512 * q:1024 * po + 512 * (q + 1)],
                        wait=w)
            p.op("vector", (lambda e, hc=hc: e.tensor_scalar(
                wabs[:, :], wraw[hc % 2][:, :], t["colmisc"][:, 1:2], MAGIC,
                op0=ALU.mult, op1=ALU.add)), wait=[ldt])
            p.op("vector", (lambda e: e.tensor_scalar(
                wabs[:, :], wabs[:, :], MAGIC, 1.0, op0=ALU.subtract, op1=ALU.min)))
            p.op("vector", (lambda e, c=c, q=q: e.tensor_scalar(
                tw[:, 1024 * c + 512 * q:1024 * c + 512 * (q + 1)], wabs[:, :], -1.0, None, op0=ALU.max)),
                 inc=[(cv, 1)])
            q_done[hc] = cv.n
        qw_done = cv.n
        for oo in range(8):
            o = 8 * po + oo
            for g in range(NG):
                pb = pbanks[i % 4]
                w = [(cv, max(qx_done, qw_done))] if i % 4 == i or i < 4 else []
                w = [(cv, max(qx_done, qw_done))]
                if i >= 4:
                    w.append((cv, evac_done[i - 4]))
                for c in range(nch):
                    p.op("tensor", (lambda e, c=c, oo=oo, g=g, pb=pb: e.matmul(
                        pb[:, :], tw[:, 1024 * c + 128 * oo:1024 * c + 128 * oo + 128],
                        xbf[:, T * c + 512 * g:T * c + 512 * (g + 1)],
                        start=(c == 0), stop=(c == nch - 1))),
                         inc=[(ct, 1)] if c == nch - 1 else (), wait=w if c == 0 else ())
                mmt = ct.n
                p.op("vector", (lambda e, g=g, pb=pb: e.tensor_mul(
                    t["tmp32e"][:, :], pb[:, :], t["AS_B"][:, 512 * g:512 * (g + 1)])),
                     wait=[(ct, mmt)])
                p.op("vector", (lambda e, o=o, g=g: e.tensor_scalar(
                    ybf[:, T * o + 512 * g:T * o + 512 * (g + 1)], t["tmp32e"][:, :],
                    t["bias_cols"][:, o:o + 1], None, op0=ALU.add)), inc=[(cv, 1)])
                evac_done[i] = cv.n
                i += 1

    barrier(C, "l0ffn_mm")

    stats_ready = emit_ln_stats(C, lambda c: chunk(ybf, c, T), nch, t["m2_B"], t["r2_B"])
    for c in range(nch):
        for g in range(NG):
            slx = slice(T * c + 512 * g, T * c + 512 * (g + 1))
            slg = slice(512 * g, 512 * (g + 1))
            if c % 2 == 0:
                p.op("vector", (lambda e, slx=slx, slg=slg: e.tensor_sub(
                    t["sqA"][:, 0:512], ybf[:, slx], t["m2_B"][:, slg])))
                p.op("vector", (lambda e, slg=slg: e.tensor_mul(
                    t["tmp32e"][:, :], t["sqA"][:, 0:512], t["r2_B"][:, slg])))
                p.op("vector", (lambda e, slx=slx: e.tensor_add(
                    t["x_res"][:, slx], t["x_res"][:, slx], t["tmp32e"][:, :])),
                     inc=[(cv, 1)])
            else:
                p.op("gpsimd", (lambda e, slx=slx, slg=slg: e.tensor_sub(
                    t["sqB"][:, 0:512], ybf[:, slx], t["m2_B"][:, slg])),
                     wait=[(cv, stats_ready)] if (c == 1 and g == 0) else ())
                p.op("gpsimd", (lambda e, slg=slg: e.tensor_mul(
                    t["wabs"][:, :], t["sqB"][:, 0:512], t["r2_B"][:, slg])))
                p.op("gpsimd", (lambda e, slx=slx: e.tensor_add(
                    t["x_res"][:, slx], t["x_res"][:, slx], t["wabs"][:, :])),
                     inc=[(cp, 1)])
    barrier(C, "l0ffn_done")



GELU_A = 1.5957691216057308
GELU_B = GELU_A * 0.044715


def alloc_l1ffn(C, ctx):
    C.sb(ctx, "w1buf0", [128, 2048], BF16)
    C.sb(ctx, "w1buf1", [128, 2048], BF16)
    C.sb(ctx, "w2buf0", [128, 4096], BF16)
    C.sb(ctx, "w2buf1", [128, 4096], BF16)
    for i in range(2):
        C.sb(ctx, f"zbf{i}", [128, 512], BF16)
        C.sb(ctx, f"sqz{i}", [128, 512], BF16)
        C.sb(ctx, f"tbf{i}", [128, 512], BF16)
        C.sb(ctx, f"ubf{i}", [128, 512], BF16)
        C.sb(ctx, f"gbf{i}", [128, 512], BF16)


def emit_l1ffn(C, W1s_dram, W2s_dram, x_out_dram, nch=16):
    p, t, T = C.p, C.t, C.T
    cv, cs, ct, cp = p.sem("cv"), p.sem("cs"), p.sem("ct"), p.sem("cp")
    xbf, ybf = t["xbf"], t["ybf"]
    a_ap = t["x_res"][:, :].bitcast(BF16)
    p.cur_scope = "l1f_ln"

    emit_ln_stats(C, lambda c: chunk(xbf, c, T), nch, t["m_B"], t["rstd_B"],
                  cast_src=lambda c: chunk(t["x_res"], c, T))
    for c in range(nch):
        p.op("vector", (lambda e, c=c: e.tensor_sub(chunk(xbf, c, T), chunk(xbf, c, T), t["m_B"][:, :])))
        p.op("vector", (lambda e, c=c: e.tensor_mul(chunk(xbf, c, T), chunk(xbf, c, T), t["rstd_B"][:, :])),
             inc=[(cv, 1)] if c == nch - 1 else ())
    xln_done = cv.n
    sp = None
    for c in range(nch):
        sp = p.dma("sync", x_out_dram[128 * c:128 * (c + 1), :], chunk(t["x_res"], c, T))
    spill_done = sp

    barrier(C, "l1ffn_ln")
    p.cur_scope = "l1f_A"

    hb = [t["p0"], t["p1"]]
    yb = [t["p2"], t["p3"]]
    NHO = 64
    WQ = ["sync", "scalar"]
    for th in range(2):
        p.cur_scope = f"l1f_A{th}"
        if th == 0:
            h_mm_done = {}
            z_done = {}
            chain_done = {}
            sq_done = {}
        for ho in range(NHO):
            k = 2 * NHO * th + ho
            wb = [t["w1buf0"], t["w1buf1"]][ho % 2]
            w = [(ct, h_mm_done[k - 2])] if (k - 2) in h_mm_done else []
            if th == 1 and ho < 2:
                w = [(ct, h_mm_done[2 * NHO * 0 + NHO - 2 + ho])]
            ldt = p.dma(WQ[ho % 2], wb[:, :], W1s_dram[ho], wait=w)
            pb = hb[ho % 2]
            w2 = [ldt]
            if (k - 2) in z_done:
                w2.append((cv, z_done[k - 2]))
            if th == 1 and ho < 2:
                w2.append((cv, z_done[2 * NHO * 0 + NHO - 2 + ho]))
            if th == 0 and ho == 0:
                w2.append((cv, xln_done))
            for c in range(nch):
                xsl = slice(T * c + 512 * th, T * c + 512 * th + 512)
                p.op("tensor", (lambda e, c=c, pb=pb, wb=wb, xsl=xsl: e.matmul(
                    pb[:, :], wb[:, 128 * c:128 * (c + 1)], xbf[:, xsl],
                    start=(c == 0), stop=(c == nch - 1))),
                     inc=[(ct, 1)] if c == nch - 1 else (), wait=w2 if c == 0 else ())
            h_mm_done[k] = ct.n
            mmt = ct.n
            i2 = ho % 2
            zb, sqz, tb, ub, gb = t[f"zbf{i2}"], t[f"sqz{i2}"], t[f"tbf{i2}"], t[f"ubf{i2}"], t[f"gbf{i2}"]
            wA = [(ct, mmt)]
            if th == 0 and ho == 0:
                wA.append(spill_done)
            if (k - 2) in sq_done:
                wA.append((cs, sq_done[k - 2]))
            p.op("vector", (lambda e, pb=pb: e.tensor_scalar(
                t["e1"][:, :], pb[:, :], 10.0, MAGIC, op0=ALU.mult, op1=ALU.add)), wait=wA)
            p.op("vector", (lambda e, zb=zb: e.tensor_scalar(
                zb[:, :], t["e1"][:, :], MAGIC, 0.1, op0=ALU.subtract, op1=ALU.mult)),
                 inc=[(cv, 1)])
            z_done[k] = cv.n
            p.op("scalar", (lambda e, zb=zb, sqz=sqz: e.activation(sqz[:, :], zb[:, :], AF.Square)),
                 inc=[(cs, 1)], wait=[(cv, z_done[k])])
            sq_done[k] = cs.n
            p.op("vector", (lambda e, sqz=sqz, tb=tb: e.tensor_scalar(
                tb[:, :], sqz[:, :], GELU_B, GELU_A, op0=ALU.mult, op1=ALU.add)),
                 wait=[(cs, sq_done[k])])
            p.op("vector", (lambda e, tb=tb, ub=ub, zb=zb: e.tensor_mul(ub[:, :], tb[:, :], zb[:, :])),
                 inc=[(cv, 1)])
            cvt2 = cv.n
            p.op("scalar", (lambda e, ub=ub, gb=gb: e.activation(gb[:, :], ub[:, :], AF.Sigmoid)),
                 inc=[(cs, 1)], wait=[(cv, cvt2)])
            cst2 = cs.n
            wa = [(cs, cst2)]
            if th == 1:
                qg_last = (2 * nch * 0 + nch - 1) * 4 + (ho // 32) * 2 + (ho % 32) // 16
                if qg_last in qmm_done:
                    wa.append((ct, qmm_done[qg_last]))
            p.op("vector", (lambda e, gb=gb, zb=zb, ho=ho: e.tensor_mul(
                a_ap[:, 512 * ho:512 * (ho + 1)], gb[:, :], zb[:, :])),
                 inc=[(cv, 1)], wait=wa)
            chain_done[k] = cv.n
        a_done = cv.n

        p.cur_scope = f"l1f_B{th}"
        if th == 0:
            ev_done = {}
            qmm_done = {}
        w2q = [t["w2buf0"][:, 0:2048], t["w2buf0"][:, 2048:4096],
               t["w2buf1"][:, 0:2048], t["w2buf1"][:, 2048:4096]]
        for oc in range(nch):
            pb = yb[oc % 2]
            for qi in range(4):
                kb, hq = qi // 2, qi % 2
                qg = (2 * nch * th + oc) * 4 + qi
                wbuf = w2q[qg % 4]
                wld = []
                if (qg - 4) in qmm_done:
                    wld.append((ct, qmm_done[qg - 4]))
                ldt = p.dma(WQ[qg % 2], wbuf,
                            W2s_dram[oc, kb][:, 2048 * hq:2048 * (hq + 1)], wait=wld)
                wmm = [ldt]
                if qi == 0:
                    wmm.append((cv, a_done))
                    if (oc - 2) in ev_done:
                        wmm.append((cv, ev_done[oc - 2]))
                    elif th == 1 and (nch - 2 + oc) in ev_done:
                        wmm.append((cv, ev_done[nch - 2 + oc]))
                for hh in range(16):
                    ho = 32 * kb + 16 * hq + hh
                    p.op("tensor", (lambda e, hh=hh, ho=ho, pb=pb, wbuf=wbuf: e.matmul(
                        pb[:, :], wbuf[:, 128 * hh:128 * (hh + 1)],
                        a_ap[:, 512 * ho:512 * (ho + 1)],
                        start=(ho == 0), stop=(ho == NHO - 1))),
                         inc=[(ct, 1)] if hh == 15 else (), wait=wmm if hh == 0 else ())
                qmm_done[qg] = ct.n
            mmt = ct.n
            p.op("vector", (lambda e, oc=oc, pb=pb, th=th: e.tensor_copy(
                ybf[:, T * oc + 512 * th:T * oc + 512 * th + 512], pb[:, :])),
                 inc=[(cv, 1)], wait=[(ct, mmt)])
            ev_done[oc] = cv.n

    barrier(C, "l1ffn_mm")
    p.cur_scope = "l1f_post"

    stats_ready = emit_ln_stats(C, lambda c: chunk(ybf, c, T), nch, t["m2_B"], t["r2_B"])
    acc_tgt = {}
    spill_sync = spill_done
    for c in range(nch):
        for g in range(T // 512):
            slx = slice(T * c + 512 * g, T * c + 512 * (g + 1))
            slg = slice(512 * g, 512 * (g + 1))
            i2 = (2 * c + g) % 2
            stg = t["tmp32e"] if i2 == 0 else t["e1"]
            sname = p.sem("dma_gpsimd", "sw")
            wv = []
            if (2 * c + g) >= 2 and acc_tgt.get((2 * c + g) - 2):
                wv.append((sname, acc_tgt[(2 * c + g) - 2]))
            p.op("vector", (lambda e, slx=slx, slg=slg: e.tensor_sub(
                t["sqA"][:, 0:512], ybf[:, slx], t["m2_B"][:, slg])), wait=wv)
            p.op("vector", (lambda e, slg=slg, stg=stg: e.tensor_mul(
                stg[:, :], t["sqA"][:, 0:512], t["r2_B"][:, slg])),
                 inc=[(cv, 1)])
            cvt = cv.n
            tgt = p.dma("gpsimd", x_out_dram[128 * c:128 * (c + 1), 512 * g:512 * (g + 1)],
                        stg[:, :], wait=[(cv, cvt)] + ([spill_sync] if (c == 0 and g == 0) else []),
                        accum=True)
            acc_tgt[(2 * c + g)] = tgt[1]
    barrier(C, "l1ffn_done")




def alloc_l1attn(C, ctx):
    C.sb(ctx, "kfull", [128, 2048], BF16)
    C.sb(ctx, "vfull", [128, 2048], BF16)
    C.sb(ctx, "PT0", [128, 1024], BF16)
    C.sb(ctx, "PT1", [128, 1024], BF16)
    C.sb(ctx, "wbq0", [128, 2048], BF16)
    C.sb(ctx, "wbq1", [128, 2048], BF16)
    C.sb(ctx, "wkv", [128, 4096], BF16)
    C.sb(ctx, "biasq", [128, 16], F32)
    C.sb(ctx, "biaskv", [128, 2], F32)
    C.sb(ctx, "biaso", [128, 16], F32)
    C.sb(ctx, "negb", [128, 16], F32)
    C.sb(ctx, "sroleA", [1, 2], mybir.dt.int32)
    C.sb(ctx, "sflagA", [1, 8], mybir.dt.int32)
    C.sb(ctx, "spollA", [1, 8], mybir.dt.int32)


def emit_l1attn(C, wqs_dram, wos_dram, role_dram, sh_kv, sh_flag, nch=16, tag=0, dbg=None):
    p, t, T = C.p, C.t, C.T
    cv, cs, ct, cp = p.sem("cv"), p.sem("cs"), p.sem("ct"), p.sem("cp")
    xbf, ybf = t["xbf"], t["ybf"]
    qbf = ybf
    obf = xbf
    vT = t["AS_B"][:, :].bitcast(BF16)
    linvB = t["F_B"]
    MAGICF = 1234567.0

    emit_ln_stats(C, lambda c: chunk(xbf, c, T), nch, t["m_B"], t["rstd_B"],
                  cast_src=lambda c: chunk(t["x_res"], c, T))
    for c in range(nch):
        p.op("vector", (lambda e, c=c: e.tensor_sub(chunk(xbf, c, T), chunk(xbf, c, T), t["m_B"][:, :])))
        p.op("vector", (lambda e, c=c: e.tensor_mul(chunk(xbf, c, T), chunk(xbf, c, T), t["rstd_B"][:, :])),
             inc=[(cv, 1)] if c == nch - 1 else ())
    xln_done = cv.n

    barrier(C, f"l1a_ln{tag}")

    if p.cur_scope:
        p.cur_scope = "l1a_qproj"
    pb4 = [t["p0"], t["p1"], t["p2"], t["p3"]]
    ev_done = {}
    mm_done = {}
    for o in range(nch):
        wb = [t["wbq0"], t["wbq1"]][o % 2]
        w = [(ct, mm_done[o - 2])] if o >= 2 else []
        ldt = p.dma(["sync", "scalar"][o % 2], wb[:, :], wqs_dram[o], wait=w)
        for g in range(2):
            pb = pb4[(2 * o + g) % 4]
            wmm = [ldt]
            if (2 * o + g) >= 4:
                wmm.append((cv, ev_done[2 * o + g - 4]))
            for c in range(nch):
                p.op("tensor", (lambda e, c=c, o=o, g=g, pb=pb, wb=wb: e.matmul(
                    pb[:, :], wb[:, 128 * c:128 * (c + 1)],
                    xbf[:, T * c + 512 * g:T * c + 512 * (g + 1)],
                    start=(c == 0), stop=(c == nch - 1))),
                     inc=[(ct, 1)] if c == nch - 1 else (), wait=wmm if c == 0 else ())
            mmt = ct.n
            p.op("vector", (lambda e, o=o, g=g, pb=pb: e.tensor_scalar(
                qbf[:, 1024 * o + 512 * g:1024 * o + 512 * (g + 1)], pb[:, :],
                t["biasq"][:, o:o + 1], None, op0=ALU.add)),
                 inc=[(cv, 1)], wait=[(ct, mmt)])
            ev_done[2 * o + g] = cv.n
        mm_done[o] = ct.n
    if p.cur_scope:
        p.cur_scope = "l1a_kv"
    kv_ev = {}
    for kv in range(2):
        dst = [t["kfull"], t["vfull"]][kv]
        for g in range(2):
            pb = pb4[(kv * 2 + g) % 4]
            wmm = [(cv, ev_done[28 + kv * 2 + g])] if False else []
            wmm = [(cv, list(ev_done.values())[-1])]
            for c in range(nch):
                p.op("tensor", (lambda e, c=c, kv=kv, g=g, pb=pb: e.matmul(
                    pb[:, :], t["wkv"][:, 2048 * kv + 128 * c:2048 * kv + 128 * (c + 1)],
                    xbf[:, T * c + 512 * g:T * c + 512 * (g + 1)],
                    start=(c == 0), stop=(c == nch - 1))),
                     inc=[(ct, 1)] if c == nch - 1 else (), wait=wmm if c == 0 else ())
            mmt = ct.n
            dstt = [t["PT0"], t["PT1"]][kv]
            p.op("vector", (lambda e, kv=kv, g=g, pb=pb, dstt=dstt: e.tensor_scalar(
                dstt[:, 512 * g:512 * (g + 1)], pb[:, :],
                t["biaskv"][:, kv:kv + 1], None, op0=ALU.add)),
                 inc=[(cv, 1)], wait=[(ct, mmt)])
            kv_ev[(kv, g)] = cv.n
    kv_done = cv.n

    ex = p.sem("exch", "hw")
    exd = p.sem("exdone")

    def exchange(e):
        e.wait_ge(p.real["cv"], kv_done)
        with e.register(f"cnt{tag}") as cnt, e.register(f"rrole{tag}") as rrole, \
             e.register(f"rflag{tag}") as rflag, e.register(f"rcond{tag}") as rcond:
            e.reg_mov(cnt, 0)
            e.reg_load(rrole, t["sroleA"][0:1, 0:1])
            for myrow in (0, 1):
                other = 1 - myrow
                with e.If_eq(rrole, myrow):
                    e.dma_start(out=sh_kv[myrow, 0:128, :], in_=t["PT0"][:, :]).then_inc(p.real["exch"], 16)
                    e.dma_start(out=sh_kv[myrow, 128:256, :], in_=t["PT1"][:, :]).then_inc(p.real["exch"], 16)
                    e.dma_start(out=t["kfull"][:, 1024 * myrow:1024 * (myrow + 1)], in_=t["PT0"][:, :]).then_inc(p.real["exch"], 16)
                    e.dma_start(out=t["vfull"][:, 1024 * myrow:1024 * (myrow + 1)], in_=t["PT1"][:, :]).then_inc(p.real["exch"], 16)
                    e.reg_add(cnt, cnt, 64)
                    e.wait_ge(p.real["exch"], cnt)
                    e.dma_start(out=sh_flag[myrow:myrow + 1, :], in_=t["sflagA"][:, :]).then_inc(p.real["exch"], 16)
                    e.reg_add(cnt, cnt, 16)
                    e.wait_ge(p.real["exch"], cnt)

                    def cond(other=other, cnt=cnt, rflag=rflag, rcond=rcond, e=e):
                        e.dma_start(out=t["spollA"][:, :], in_=sh_flag[other:other + 1, :]).then_inc(p.real["exch"], 16)
                        e.reg_add(cnt, cnt, 16)
                        e.wait_ge(p.real["exch"], cnt)
                        e.reg_load(rflag, t["spollA"][0:1, 0:1])
                        e.reg_alu(rcond, rflag, tag + 7, mybir.AluOpType.not_equal)
                        return rcond
                    with e.While(cond):
                        e.nop()
                    e.dma_start(out=t["kfull"][:, 1024 * other:1024 * (other + 1)],
                                in_=sh_kv[other, 0:128, :]).then_inc(p.real["exch"], 16)
                    e.dma_start(out=t["vfull"][:, 1024 * other:1024 * (other + 1)],
                                in_=sh_kv[other, 128:256, :]).then_inc(p.real["exch"], 16)
                    e.reg_add(cnt, cnt, 32)
                    e.wait_ge(p.real["exch"], cnt)
        e.nop().then_inc(p.real["exdone"], 1)

    p.raw("sync", exchange, inc=[(ex, 0), (exd, 1)])
    exd_t = exd.n

    barrier(C, f"l1a_ex{tag}")

    if p.cur_scope:
        p.cur_scope = "l1a_vt"
    vt_done = {}
    for kt in range(16):
        pbt = [t["p4"], t["p5"]][kt % 2]
        pbb = pbt[:, :].bitcast(BF16)
        w = [(p.sem("exdone"), exd_t)] if kt < 2 else [(cv, vt_done[kt - 2])]
        p.op("tensor", (lambda e, kt=kt, pbb=pbb: e.transpose(
            pbb[:, 0:128], t["vfull"][:, 128 * kt:128 * (kt + 1)], t["ident_bf"][:, :])),
             inc=[(ct, 1)], wait=w)
        ctt = ct.n
        p.op("vector", (lambda e, kt=kt, pbb=pbb: e.tensor_copy(
            vT[:, 128 * kt:128 * (kt + 1)], pbb[:, 0:128])),
             inc=[(cv, 1)], wait=[(ct, ctt)])
        vt_done[kt] = cv.n
    vt_all = cv.n

    if p.cur_scope:
        p.cur_scope = "l1a_bounds"
    R = t["rows32a"]
    km_done = {}
    for g in range(4):
        w = [(p.sem("exdone"), exd_t)] if g == 0 else [(cv, km_done[g - 1])]
        p.op("scalar", (lambda e, g=g: e.activation(
            t["sqA"][:, 0:512], t["kfull"][:, 512 * g:512 * (g + 1)], AF.Square)),
             inc=[(cs, 1)], wait=w + [(cp, 0)])
        cst = cs.n
        p.op("tensor", (lambda e: e.matmul(
            t["p6"][0:1, :], t["ones128_bf"][:, 0:1], t["sqA"][:, 0:512],
            start=True, stop=True)), inc=[(ct, 1)], wait=[(cs, cst)])
        ctt = ct.n
        p.op("vector", (lambda e, g=g: e.tensor_reduce(
            R[0:1, 700 + g:701 + g], t["p6"][0:1, :], axis=AX.X, op=ALU.max)),
             inc=[(cv, 1)], wait=[(ct, ctt)])
        km_done[g] = cv.n
    p.op("vector", (lambda e: e.tensor_reduce(
        R[0:1, 710:711], R[0:1, 700:704], axis=AX.X, op=ALU.max)),
         inc=[(cv, 1)], wait=[(cv, cv.n)])
    kmax_t = flush(C, "vector")
    for h in range(16):
        w0 = [(cv, kmax_t)] if h == 0 else [(cv, p.sem("cv").n)]
        p.op("scalar", (lambda e, h=h: e.activation(
            t["sqA"][:, 0:1024], qbf[:, 1024 * h:1024 * (h + 1)], AF.Square)),
             inc=[(cs, 1)], wait=w0)
        cst = cs.n
        p.op("tensor", (lambda e: e.matmul(
            t["p6"][0:1, :], t["ones128_bf"][:, 0:1], t["sqA"][:, 0:512],
            start=True, stop=True)), wait=[(cs, cst)])
        p.op("tensor", (lambda e: e.matmul(
            t["p7"][0:1, :], t["ones128_bf"][:, 0:1], t["sqA"][:, 512:1024],
            start=True, stop=True)), inc=[(ct, 1)])
        ctt = ct.n
        p.op("vector", (lambda e, h=h: e.tensor_reduce(
            R[0:1, 712:713], t["p6"][0:1, :], axis=AX.X, op=ALU.max)),
             inc=[(cv, 1)], wait=[(ct, ctt)])
        p.op("vector", (lambda e, h=h: e.tensor_reduce(
            R[0:1, 713:714], t["p7"][0:1, :], axis=AX.X, op=ALU.max)),
             inc=[(cv, 1)], wait=[(cv, cv.n)])
        p.op("vector", (lambda e: e.tensor_max(
            R[0:1, 714:715], R[0:1, 712:713], R[0:1, 713:714])),
             inc=[(cv, 1)], wait=[(cv, cv.n)])
        p.op("vector", (lambda e: e.tensor_mul(
            R[0:1, 715:716], R[0:1, 714:715], R[0:1, 710:711])),
             inc=[(cv, 1)], wait=[(cv, cv.n)])
        cvt = flush(C, "vector")
        p.op("scalar", (lambda e: e.activation(
            R[0:1, 716:717], R[0:1, 715:716], AF.Sqrt)), inc=[(cs, 1)], wait=[(cv, cvt)])
        cst = flush(C, "scalar")
        p.op("vector", (lambda e: e.tensor_scalar(
            R[0:1, 717:718], R[0:1, 716:717], -1.0 / (128.0 ** 0.5), None, op0=ALU.mult)),
             inc=[(cv, 1)], wait=[(cs, cst)])
        cvt = flush(C, "vector")
        p.op("tensor", (lambda e: e.matmul(
            t["p6"][:, 0:1], t["onesr_f"][:, :], R[0:1, 717:718],
            start=True, stop=True)), inc=[(ct, 1)], wait=[(cv, cvt)])
        ctt = ct.n
        p.op("vector", (lambda e, h=h: e.tensor_copy(t["negb"][:, h:h + 1], t["p6"][:, 0:1])),
             inc=[(cv, 1)], wait=[(ct, ctt)])
    bounds_done = cv.n

    if dbg is not None:
        dbg("post_bounds")
    barrier(C, f"l1a_bounds{tag}")

    if p.cur_scope:
        p.cur_scope = "l1a_heads"
    spair = [(t["p0"], t["p1"]), (t["p2"], t["p3"])]
    PT = [t["PT0"], t["PT1"]]
    for h in range(16):
        exp_done = {}
        luse_done = {}
        for kt in range(16):
            sp_ = spair[kt % 2]
            for g in range(2):
                w = []
                if kt >= 2:
                    w.append((cs, exp_done[kt - 2]))
                if h > 0 and kt < 2:
                    w.append((cs, h_exp_last))
                if h > 0 and kt == 0 and g == 0:
                    w.append((cv, h_prev_obf))
                p.op("tensor", (lambda e, h=h, kt=kt, g=g, sp_=sp_: e.matmul(
                    sp_[g][:, :], t["kfull"][:, 128 * kt:128 * (kt + 1)],
                    qbf[:, 1024 * h + 512 * g:1024 * h + 512 * (g + 1)],
                    start=True, stop=True)),
                     inc=[(ct, 1)] if g == 1 else (), wait=w if g == 0 else ())
            mmt = ct.n
            wexp = [(ct, mmt)]
            if kt >= 2:
                wexp.append((ct, luse_done[kt - 2]))
            for g in range(2):
                p.op("scalar", (lambda e, h=h, kt=kt, g=g, sp_=sp_: e.activation(
                    PT[kt % 2][:, 512 * g:512 * (g + 1)], sp_[g][:, :], AF.Exp,
                    bias=t["negb"][:, h:h + 1], scale=1.0 / (128.0 ** 0.5))),
                     inc=[(cs, 1)] if g == 1 else (), wait=wexp if g == 0 else ())
            exp_done[kt] = cs.n
            cst = cs.n
            for g in range(2):
                p.op("tensor", (lambda e, kt=kt, g=g: e.matmul(
                    [t["p6"], t["p7"]][g][0:1, :], t["ones128_bf"][:, 0:1],
                    PT[kt % 2][:, 512 * g:512 * (g + 1)],
                    start=(kt == 0), stop=(kt == 15))), wait=[(cs, cst)] if g == 0 else ())
            for g in range(2):
                p.op("tensor", (lambda e, kt=kt, g=g: e.matmul(
                    [t["p4"], t["p5"]][g][:, :], vT[:, 128 * kt:128 * (kt + 1)],
                    PT[kt % 2][:, 512 * g:512 * (g + 1)],
                    start=(kt == 0), stop=(kt == 15))),
                     inc=[(ct, 1)] if g == 1 else ())
            p.op("tensor", (lambda e: e.nop()), inc=[(ct, 1)])
            luse_done[kt] = ct.n
        h_exp_last = exp_done[15]
        avl_done = ct.n
        p.op("vector", (lambda e: e.reciprocal(R[0:1, 0:512], t["p6"][0:1, :])),
             inc=[(cv, 1)], wait=[(ct, avl_done)])
        p.op("vector", (lambda e: e.reciprocal(R[0:1, 512:1024], t["p7"][0:1, :])),
             inc=[(cv, 1)], wait=[(cv, cv.n)])
        cvt = flush(C, "vector")
        for g in range(2):
            p.op("tensor", (lambda e, g=g: e.matmul(
                [t["p0"], t["p1"]][g][:, :], t["onesr_f"][:, :], R[0:1, 512 * g:512 * (g + 1)],
                start=True, stop=True)), inc=[(ct, 1)] if g == 1 else (), wait=[(cv, cvt)] if g == 0 else ())
        ctt = ct.n
        for g in range(2):
            p.op("vector", (lambda e, g=g: e.tensor_copy(
                linvB[:, 512 * g:512 * (g + 1)], [t["p0"], t["p1"]][g][:, :])),
                 inc=[(cv, 1)], wait=[(ct, ctt)] if g == 0 else ())
        cvt = cv.n
        for g in range(2):
            p.op("vector", (lambda e, h=h, g=g: e.tensor_mul(
                obf[:, 1024 * h + 512 * g:1024 * h + 512 * (g + 1)],
                [t["p4"], t["p5"]][g][:, :], linvB[:, 512 * g:512 * (g + 1)])),
                 inc=[(cv, 1)], wait=[(cv, cvt)] if g == 0 else ())
        h_prev_obf = cv.n

    if dbg is not None:
        dbg("post_heads")
    barrier(C, f"l1a_heads{tag}")

    if p.cur_scope:
        p.cur_scope = "l1a_wo"
    ev_done = {}
    mm_done = {}
    for o in range(nch):
        wb = [t["wbq0"], t["wbq1"]][o % 2]
        w = [(ct, mm_done[o - 2])] if o >= 2 else []
        ldt = p.dma(["sync", "scalar"][o % 2], wb[:, :], wos_dram[o], wait=w)
        for g in range(2):
            pb = pb4[(2 * o + g) % 4]
            wmm = [ldt]
            if (2 * o + g) >= 4:
                wmm.append((cv, ev_done[2 * o + g - 4]))
            for c in range(nch):
                p.op("tensor", (lambda e, c=c, o=o, g=g, pb=pb, wb=wb: e.matmul(
                    pb[:, :], wb[:, 128 * c:128 * (c + 1)],
                    obf[:, T * c + 512 * g:T * c + 512 * (g + 1)],
                    start=(c == 0), stop=(c == nch - 1))),
                     inc=[(ct, 1)] if c == nch - 1 else (), wait=wmm if c == 0 else ())
            mmt = ct.n
            p.op("vector", (lambda e, o=o, g=g, pb=pb: e.tensor_scalar(
                ybf[:, T * o + 512 * g:T * o + 512 * (g + 1)], pb[:, :],
                t["biaso"][:, o:o + 1], None, op0=ALU.add)),
                 inc=[(cv, 1)], wait=[(ct, mmt)])
            ev_done[2 * o + g] = cv.n
        mm_done[o] = ct.n

    barrier(C, f"l1a_wo{tag}")

    if p.cur_scope:
        p.cur_scope = "l1a_post"
    stats_ready = emit_ln_stats(C, lambda c: chunk(ybf, c, T), nch, t["m2_B"], t["r2_B"])
    for c in range(nch):
        for g in range(T // 512):
            slx = slice(T * c + 512 * g, T * c + 512 * (g + 1))
            slg = slice(512 * g, 512 * (g + 1))
            if c % 2 == 0:
                p.op("vector", (lambda e, slx=slx, slg=slg: e.tensor_sub(
                    t["sqA"][:, 0:512], ybf[:, slx], t["m2_B"][:, slg])))
                p.op("vector", (lambda e, slg=slg: e.tensor_mul(
                    t["tmp32e"][:, :], t["sqA"][:, 0:512], t["r2_B"][:, slg])))
                p.op("vector", (lambda e, slx=slx: e.tensor_add(
                    t["x_res"][:, slx], t["x_res"][:, slx], t["tmp32e"][:, :])),
                     inc=[(cv, 1)])
            else:
                p.op("gpsimd", (lambda e, slx=slx, slg=slg: e.tensor_sub(
                    t["sqB"][:, 0:512], ybf[:, slx], t["m2_B"][:, slg])),
                     wait=[(cv, stats_ready)] if (c == 1 and g == 0) else ())
                p.op("gpsimd", (lambda e, slg=slg: e.tensor_mul(
                    t["e1"][:, :], t["sqB"][:, 0:512], t["r2_B"][:, slg])))
                p.op("gpsimd", (lambda e, slx=slx: e.tensor_add(
                    t["x_res"][:, slx], t["x_res"][:, slx], t["e1"][:, :])),
                     inc=[(cp, 1)])
    barrier(C, f"l1a_done{tag}")


RT128 = 1.0 / (128.0 ** 0.5)


def alloc_l0attn(C, ctx):
    C.sb(ctx, "wsA", [128, 2048], BF16)
    C.sb(ctx, "wsB", [128, 2048], BF16)
    C.sb(ctx, "wsmall", [128, 512], F32)
    C.sb(ctx, "wsmallb", [128, 256], BF16)
    C.sb(ctx, "convw", [128, 160], F32)
    C.sb(ctx, "convext", [128, 1027], BF16)
    C.sb(ctx, "haloloc", [128, 128], BF16)
    C.sb(ctx, "halorem", [128, 128], BF16)
    C.sb(ctx, "ggtmp", [128, 1024], BF16)
    C.sb(ctx, "scanprep", [128, 1792], F32)
    C.sb(ctx, "acol", [128, 8], F32)
    C.sb(ctx, "onw", [128, 1], F32)
    C.sb(ctx, "cumsL", [128, 64], F32)
    C.sb(ctx, "dupB", [64, 128], F32)
    C.sb(ctx, "dupE", [128, 128], F32)
    C.sb(ctx, "dupO", [128, 128], F32)
    C.sb(ctx, "sel63", [64, 1], F32)
    C.sb(ctx, "m1c4", [128, 512], BF16)
    C.sb(ctx, "m2c", [128, 64], BF16)
    C.sb(ctx, "sroleB", [1, 2], mybir.dt.int32)
    C.sb(ctx, "sflagB", [1, 8], mybir.dt.int32)
    C.sb(ctx, "spollB", [1, 8], mybir.dt.int32)


def emit_l0attn_pre(C, Wqs, Wks, Wvs, Wgs, wsmall_d, convw_d, acol_d,
                    x_work, gg_work, sh_halo, sh_flag, nch=16):
    p, t, T = C.p, C.t, C.T
    cv, cs, ct, cp = p.sem("cv"), p.sem("cs"), p.sem("ct"), p.sem("cp")
    xbf, ybf = t["xbf"], t["ybf"]
    KCVC = t["x_res"][:, :].bitcast(BF16)
    qc = lambda o: ybf[:, 1024 * o:1024 * (o + 1)]
    kc = lambda o: KCVC[:, 1024 * o:1024 * (o + 1)]
    vc = lambda o: KCVC[:, 16384 + 1024 * o:16384 + 1024 * (o + 1)]

    emit_ln_stats(C, lambda c: chunk(xbf, c, T), nch, t["m_B"], t["rstd_B"],
                  cast_src=lambda c: chunk(t["x_res"], c, T))
    for c in range(nch):
        p.op("vector", (lambda e, c=c: e.tensor_sub(chunk(xbf, c, T), chunk(xbf, c, T), t["m_B"][:, :])))
        p.op("vector", (lambda e, c=c: e.tensor_mul(chunk(xbf, c, T), chunk(xbf, c, T), t["rstd_B"][:, :])),
             inc=[(cv, 1)] if c == nch - 1 else ())
    xln_done = cv.n
    sp = None
    for c in range(nch):
        sp = p.dma("sync", x_work[128 * c:128 * (c + 1), :], chunk(t["x_res"], c, T))
    spill_done = sp

    barrier(C, "l0a_ln")

    p.op("vector", lambda e: e.memset(t["haloloc"][:, :], 0))
    p.op("vector", lambda e: e.memset(t["halorem"][:, :], 0))

    if p.cur_scope:
        p.cur_scope = "pre_small"
    for c in range(nch):
        p.op("tensor", (lambda e, c=c: e.matmul(
            t["p6"][0:8, 0:1], t["wsmall"][:, 256 + 8 * c:256 + 8 * (c + 1)],
            t["constcol"][:, 1:2], start=(c == 0), stop=(c == nch - 1))),
             inc=[(ct, 1)] if c == nch - 1 else ())
    ctt = ct.n
    p.op("vector", (lambda e: e.tensor_copy(t["acol"][0:8, 3:4], t["p6"][0:8, 0:1])),
         inc=[(cv, 1)], wait=[(ct, ctt)])
    wsum_t = flush(C, "vector")

    sig_done = 0
    aev_done = 0
    for g in range(2):
        wb0 = [(cv, xln_done)]
        if g == 1:
            wb0.append((cs, sig_done))
            wb0.append((cv, aev_done))
        for c in range(nch):
            p.op("tensor", (lambda e, c=c, g=g: e.matmul(
                t["p4"][0:16, :], t["wsmallb"][:, 16 * c:16 * (c + 1)],
                xbf[:, T * c + 512 * g:T * c + 512 * (g + 1)],
                start=(c == 0), stop=(c == nch - 1))), wait=wb0 if c == 0 else ())
        for c in range(nch):
            p.op("tensor", (lambda e, c=c, g=g: e.matmul(
                t["p5"][0:8, :], t["wsmall"][:, 256 + 8 * c:256 + 8 * (c + 1)],
                chunk(t["x_res"], c, T)[:, 512 * g:512 * (g + 1)],
                start=(c == 0), stop=(c == nch - 1))),
                 inc=[(ct, 1)] if c == nch - 1 else ())
        ctt = ct.n
        p.op("scalar", (lambda e, g=g: e.activation(
            t["rows32b"][0:16, 512 * g:512 * (g + 1)], t["p4"][0:16, :], AF.Sigmoid)),
             inc=[(cs, 1)], wait=[(ct, ctt)])
        sig_done = cs.n
        p.op("vector", (lambda e, g=g: e.tensor_scalar(
            t["rows32b"][0:16, 512 * g:512 * (g + 1)], t["rows32b"][0:16, 512 * g:512 * (g + 1)],
            2.0, None, op0=ALU.mult)), wait=[(cs, cs.n)])
        p.op("vector", (lambda e, g=g: e.tensor_scalar(
            t["tmp32e"][0:8, :], t["m_B"][0:8, 512 * g:512 * (g + 1)],
            t["acol"][0:8, 3:4], None, op0=ALU.mult)), inc=[(cv, 1)], wait=[(cv, wsum_t)])
        p.op("vector", (lambda e, g=g: e.scalar_tensor_tensor(
            t["F_B"][0:8, 512 * g:512 * (g + 1)], t["p5"][0:8, :], 1.0,
            t["tmp32e"][0:8, :], op0=ALU.mult, op1=ALU.subtract)), wait=[(cv, cv.n)])
        p.op("vector", (lambda e, g=g: e.tensor_mul(
            t["F_B"][0:8, 512 * g:512 * (g + 1)], t["F_B"][0:8, 512 * g:512 * (g + 1)],
            t["rstd_B"][0:8, 512 * g:512 * (g + 1)])))
        p.op("vector", (lambda e, g=g: e.tensor_scalar(
            t["F_B"][0:8, 512 * g:512 * (g + 1)], t["F_B"][0:8, 512 * g:512 * (g + 1)],
            t["acol"][0:8, 1:2], None, op0=ALU.add)), inc=[(cv, 1)])
        aev_done = cv.n
    ba_done = cv.n
    if getattr(C, "dbg_pre", None):
        C.dbg_pre("a_done")
    A = t["rows32b"]
    p.op("scalar", (lambda e: e.activation(t["r2_B"][0:8, :], t["F_B"][0:8, :], AF.Relu)),
         inc=[(cs, 1)], wait=[(cv, ba_done)])
    relu_t = cs.n
    p.op("vector", (lambda e: e.tensor_scalar(t["AS_B"][0:8, :], t["F_B"][0:8, :], -1.0, None, op0=ALU.mult)))
    p.op("vector", (lambda e: e.tensor_max(t["AS_B"][0:8, :], t["F_B"][0:8, :], t["AS_B"][0:8, :])),
         inc=[(cv, 1)], wait=[(cv, cv.n)])
    cvt = cv.n
    p.op("scalar", (lambda e: e.activation(t["AS_B"][0:8, :], t["AS_B"][0:8, :], AF.Exp, scale=-1.0)),
         inc=[(cs, 1)], wait=[(cv, cvt)])
    cst = cs.n
    p.op("scalar", (lambda e: e.activation(t["AS_B"][0:8, :], t["AS_B"][0:8, :], AF.Ln,
                                           bias=t["constcol"][0:8, 1:2])), inc=[(cs, 1)],
         wait=[(cs, cst)])
    cst = cs.n
    p.op("vector", (lambda e: e.tensor_add(t["F_B"][0:8, :], t["AS_B"][0:8, :], t["r2_B"][0:8, :])),
         wait=[(cs, cst)])
    p.op("scalar", (lambda e: e.activation(t["acol"][0:8, 2:3], t["acol"][0:8, 0:1], AF.Exp)),
         inc=[(cs, 1)])
    cstA = flush(C, "scalar")
    p.op("vector", (lambda e: e.tensor_scalar(t["acol"][0:8, 2:3], t["acol"][0:8, 2:3],
                                              -1.0, None, op0=ALU.mult)),
         inc=[(cv, 1)], wait=[(cs, cstA)])
    cvA = flush(C, "vector")
    p.op("vector", (lambda e: e.tensor_scalar(t["F_B"][0:8, :], t["F_B"][0:8, :], t["acol"][0:8, 2:3],
                                              None, op0=ALU.mult)), inc=[(cv, 1)], wait=[(cv, cv.n)])
    g_done = flush(C, "vector")
    if getattr(C, "dbg_pre", None):
        C.dbg_pre("g_done")

    if p.cur_scope:
        p.cur_scope = "pre_gg"
    mm_done = {}
    for o in range(8):
        wb = [t["wsA"], t["wsB"]][o % 2]
        w = [(ct, mm_done[o - 2])] if o >= 2 else []
        ldt = p.dma("gpsimd", wb[:, :], Wgs[o], wait=w)
        for g in range(2):
            pb = [t["p6"], t["p7"]][g]
            for c in range(nch):
                p.op("tensor", (lambda e, c=c, g=g, pb=pb, wb=wb: e.matmul(
                    pb[:, :], wb[:, 128 * c:128 * (c + 1)],
                    xbf[:, T * c + 512 * g:T * c + 512 * (g + 1)],
                    start=(c == 0), stop=(c == nch - 1))),
                     inc=[(ct, 1)] if c == nch - 1 else (),
                     wait=[ldt] + ([(cs, p.sem("cs").n)] if o > 0 else []) if c == 0 else ())
        mm_done[o] = ct.n
        ctt = ct.n
        for g in range(2):
            pb = [t["p6"], t["p7"]][g]
            wsl = [(ct, ctt)]
            if o >= 1:
                wsl.append(gg_dma_t)
            p.op("scalar", (lambda e, g=g, pb=pb: e.activation(
                t["ggtmp"][:, 512 * g:512 * (g + 1)], pb[:, :], AF.Silu)),
                 inc=[(cs, 1)], wait=wsl if g == 0 else ())
        cst = cs.n
        gg_dma_t = p.dma("gpsimd", gg_work[o], t["ggtmp"][:, :], wait=[(cs, cst)])

    if p.cur_scope:
        p.cur_scope = "pre_qkv"
    PLAN = [("q", 8, Wqs, qc), ("k", 16, Wks, kc), ("v", 16, Wvs, vc)]
    hcol = 0
    halo_done = {}
    pbr = [t["p0"], t["p1"], t["p2"], t["p3"]]
    i = 0
    raw_done = {}
    for nm, nо, Ws, dst in PLAN:
        for o in range(nо):
            wb = [t["wsA"], t["wsB"]][i % 2]
            w = [(ct, raw_done[i - 2])] if i >= 2 else [(ct, mm_done[7])]
            ldt = p.dma("gpsimd", wb[:, :], Ws[o], wait=w)
            for g in range(2):
                pb = pbr[(2 * i + g) % 4]
                wmm = [ldt]
                if i >= 2:
                    wmm.append((cv, halo_done[i - 2]))
                for c in range(nch):
                    p.op("tensor", (lambda e, c=c, g=g, pb=pb, wb=wb: e.matmul(
                        pb[:, :], wb[:, 128 * c:128 * (c + 1)],
                        xbf[:, T * c + 512 * g:T * c + 512 * (g + 1)],
                        start=(c == 0), stop=(c == nch - 1))),
                         inc=[(ct, 1)] if c == nch - 1 else (), wait=wmm if c == 0 else ())
            raw_done[i] = ct.n
            ctt = ct.n
            for g in range(2):
                pb = pbr[(2 * i + g) % 4]
                wv = [(ct, ctt)]
                if i == 0 and g == 0:
                    wv.append(spill_done)
                p.op("vector", (lambda e, g=g, pb=pb, dst=dst, o=o: e.tensor_copy(
                    dst(o)[:, 512 * g:512 * (g + 1)], pb[:, :])), wait=wv)
            pb1 = pbr[(2 * i + 1) % 4]
            p.op("vector", (lambda e, pb1=pb1, hcol=hcol: e.tensor_copy(
                t["haloloc"][:, 3 * hcol:3 * hcol + 3], pb1[:, 509:512])),
                 inc=[(cv, 1)])
            halo_done[i] = cv.n
            hcol += 1
            i += 1
    allraw = flush(C, "vector")

    if p.cur_scope:
        p.cur_scope = "pre_halo"
    exd = p.sem("exdB")
    exs = p.sem("exchB", "hw")

    def exchange(e):
        e.wait_ge(p.real["cv"], allraw)
        with e.register("cntB") as cnt, e.register("rroleB") as rrole, \
             e.register("rflagB") as rflag, e.register("rcondB") as rcond:
            e.reg_mov(cnt, 0)
            e.reg_load(rrole, t["sroleB"][0:1, 0:1])
            with e.If_eq(rrole, 0):
                e.dma_start(out=sh_halo[0], in_=t["haloloc"][:, :]).then_inc(p.real["exchB"], 16)
                e.reg_add(cnt, cnt, 16)
                e.wait_ge(p.real["exchB"], cnt)
                e.dma_start(out=sh_flag[0:1, :], in_=t["sflagB"][:, :]).then_inc(p.real["exchB"], 16)
                e.reg_add(cnt, cnt, 16)
                e.wait_ge(p.real["exchB"], cnt)
            with e.If_eq(rrole, 1):
                def cond():
                    e.dma_start(out=t["spollB"][:, :], in_=sh_flag[0:1, :]).then_inc(p.real["exchB"], 16)
                    e.reg_add(cnt, cnt, 16)
                    e.wait_ge(p.real["exchB"], cnt)
                    e.reg_load(rflag, t["spollB"][0:1, 0:1])
                    e.reg_alu(rcond, rflag, 21, mybir.AluOpType.not_equal)
                    return rcond
                with e.While(cond):
                    e.nop()
                e.dma_start(out=t["halorem"][:, :], in_=sh_halo[0]).then_inc(p.real["exchB"], 16)
                e.reg_add(cnt, cnt, 16)
                e.wait_ge(p.real["exchB"], cnt)
        e.nop().then_inc(p.real["exdB"], 1)

    p.raw("sync", exchange, inc=[(exd, 1)])
    exd_t = exd.n

    barrier(C, "l0a_halo")

    if p.cur_scope:
        p.cur_scope = "pre_conv"
    conv_done = {}
    ci = 0
    for nm, nо, Ws, dst in PLAN:
        base = {"q": 0, "k": 32, "v": 96}[nm]
        for o in range(nо):
            hcolо = {"q": 0, "k": 8, "v": 24}[nm] + o
            p.op("vector", (lambda e, dst=dst, o=o: e.tensor_copy(
                t["convext"][:, 3:1027], dst(o)[:, :])))
            p.op("vector", (lambda e, hcolо=hcolо: e.tensor_copy(
                t["convext"][:, 0:3], t["halorem"][:, 3 * hcolо:3 * hcolо + 3])))
            p.op("vector", (lambda e, base=base, o=o: e.tensor_scalar(
                t["sqA"][:, :], t["convext"][:, 3:1027], t["convw"][:, base + 4 * o:base + 4 * o + 1],
                None, op0=ALU.mult)),
                 wait=[(cs, conv_done[ci - 1])] if ci >= 1 else [(p.sem("exdB"), exd_t)])
            for j in range(1, 4):
                p.op("vector", (lambda e, base=base, o=o, j=j: e.scalar_tensor_tensor(
                    t["sqA"][:, :], t["convext"][:, 3 - j:1027 - j],
                    t["convw"][:, base + 4 * o + j:base + 4 * o + j + 1], t["sqA"][:, :],
                    op0=ALU.mult, op1=ALU.add)))
            p.op("vector", (lambda e: e.engine_nop()), inc=[(cv, 1)])
            cvt = cv.n
            p.op("scalar", (lambda e, dst=dst, o=o: e.activation(
                dst(o)[:, :], t["sqA"][:, :], AF.Silu)), inc=[(cs, 1)], wait=[(cv, cvt)])
            conv_done[ci] = cs.n
            ci += 1
    convs_done = cs.n

    barrier(C, "l0a_conv")

    if p.cur_scope:
        p.cur_scope = "pre_l2n"
    prev_sqrt = 0
    for which, n_, get in (("k", 16, kc), ("q", 8, qc)):
        for o in range(n_):
            p.op("vector", (lambda e, get=get, o=o: e.tensor_mul(
                t["sqA"][:, :], get(o)[:, :], get(o)[:, :])), inc=[(cv, 1)])
            cvt = cv.n
            for g in range(2):
                wmm2 = [(cv, cvt)] + ([(cs, prev_sqrt)] if prev_sqrt else [])
                p.op("tensor", (lambda e, g=g, get=get, o=o: e.matmul(
                    t["p6"][0:1, :] if g == 0 else t["p7"][0:1, :],
                    t["ones128_bf"][:, 0:1], t["sqA"][:, 512 * g:512 * (g + 1)],
                    start=True, stop=True)),
                     inc=[(ct, 1)] if g == 1 else (), wait=wmm2 if g == 0 else ())
            ctt = ct.n
            R = t["rows32a"]
            p.op("scalar", (lambda e: e.activation(R[0:1, 0:512], t["p6"][0:1, :], AF.Sqrt)),
                 wait=[(ct, ctt)])
            p.op("scalar", (lambda e: e.activation(R[0:1, 512:1024], t["p7"][0:1, :], AF.Sqrt)),
                 inc=[(cs, 1)])
            cst = cs.n
            prev_sqrt = cst
            p.op("vector", (lambda e: e.tensor_scalar(R[0:1, :], R[0:1, :], 1e-12, None, op0=ALU.max)),
                 inc=[(cv, 1)], wait=[(cs, cst)])
            p.op("vector", (lambda e: e.reciprocal(R[0:1, :], R[0:1, :])),
                 inc=[(cv, 1)], wait=[(cv, cv.n)])
            if which == "q":
                p.op("vector", (lambda e: e.tensor_scalar(R[0:1, :], R[0:1, :], RT128, None, op0=ALU.mult)),
                     inc=[(cv, 1)], wait=[(cv, cv.n)])
            cvt = flush(C, "vector")
            for g in range(2):
                p.op("tensor", (lambda e, g=g: e.matmul(
                    [t["p6"], t["p7"]][g][:, :], t["onesr_f"][:, :], R[0:1, 512 * g:512 * (g + 1)],
                    start=True, stop=True)), inc=[(ct, 1)] if g == 1 else (),
                     wait=[(cv, cvt)] if g == 0 else ())
            ctt = ct.n
            for g in range(2):
                p.op("vector", (lambda e, g=g, get=get, o=o: e.tensor_mul(
                    get(o)[:, 512 * g:512 * (g + 1)], get(o)[:, 512 * g:512 * (g + 1)],
                    [t["p6"], t["p7"]][g][:, :])),
                     inc=[(cv, 1)] if g == 1 else (), wait=[(ct, ctt)] if g == 0 else ())
    norm_done = cv.n

    if p.cur_scope:
        p.cur_scope = "pre_decay"
    SP = t["scanprep"]
    R = t["rows32a"]
    for jj in range(8):
        pb = [t["p4"], t["p5"]][jj % 2]
        p.op("tensor", (lambda e, jj=jj, pb=pb: e.transpose(
            pb[:, 0:8], t["F_B"][0:8, 128 * jj:128 * (jj + 1)], t["ident_f"][0:8, 0:8])),
             inc=[(ct, 1)], wait=[(cv, p.sem("cv").n), (cv, g_done)])
        ctt = ct.n
        p.op("vector", (lambda e, jj=jj, pb=pb: e.tensor_copy(
            SP[:, 1024 + 8 * jj:1024 + 8 * (jj + 1)], pb[:, 0:8])),
             inc=[(cv, 1)], wait=[(ct, ctt)])
    for jj in range(8):
        pb = [t["p4"], t["p5"]][jj % 2]
        p.op("tensor", (lambda e, jj=jj, pb=pb: e.transpose(
            pb[:, 0:16], A[0:16, 128 * jj:128 * (jj + 1)], t["ident_f"][0:16, 0:16])),
             inc=[(ct, 1)], wait=[(cv, p.sem("cv").n)])
        ctt = ct.n
        p.op("vector", (lambda e, jj=jj, pb=pb: e.tensor_copy(
            SP[:, 1152 + 16 * jj:1152 + 16 * (jj + 1)], pb[:, 0:16])),
             inc=[(cv, 1)], wait=[(ct, ctt)])
    gt_done = flush(C, "vector")
    for par in range(2):
        p.op("tensor", (lambda e, par=par: e.matmul(
            t["p4"][0:64, 0:64], t["cumsL"][64 * par:64 * (par + 1), :],
            SP[64 * par:64 * (par + 1), 1024:1088], start=True, stop=True)),
             inc=[(ct, 1)], wait=[(cv, gt_done)] if par == 0 else [(cv, p.sem("cv").n)])
        ctt = ct.n
        p.op("vector", (lambda e, par=par: e.tensor_copy(
            SP[0:64, 384 + 64 * par:448 + 64 * par], t["p4"][0:64, 0:64])),
             inc=[(cv, 1)], wait=[(ct, ctt)])
    cs_done = flush(C, "vector")
    p.op("tensor", (lambda e: e.matmul(
        t["p4"][:, 0:128], t["dupB"][:, :], SP[0:64, 384:512], start=True, stop=True)),
         inc=[(ct, 1)], wait=[(cv, cs_done)])
    ctt = ct.n
    p.op("vector", (lambda e: e.tensor_copy(SP[:, 0:128], t["p4"][:, 0:128])),
         inc=[(cv, 1)], wait=[(ct, ctt)])
    bT3 = SP[:, 1152:1280].rearrange("p (j r) -> p j r", r=16)
    for par in range(2):
        p.op("tensor", (lambda e, par=par: e.matmul(
            t["p5"][:, 0:64], t["dupE"][64 * par:64 * (par + 1), :], bT3[64 * par:64 * (par + 1), :, 0:8],
            start=True, stop=False)), wait=[(cv, p.sem("cv").n)] if par == 1 else ())
        p.op("tensor", (lambda e, par=par: e.matmul(
            t["p5"][:, 0:64], t["dupO"][64 * par:64 * (par + 1), :], bT3[64 * par:64 * (par + 1), :, 8:16],
            start=False, stop=True)), inc=[(ct, 1)])
        ctt = ct.n
        p.op("vector", (lambda e, par=par: e.tensor_copy(
            SP[:, 128 + 64 * par:192 + 64 * par], t["p5"][:, 0:64])),
             inc=[(cv, 1)], wait=[(ct, ctt)])
    ball_done = flush(C, "vector")
    p.op("tensor", (lambda e: e.matmul(
        t["p6"][0:1, 0:128], t["sel63"][:, :], SP[0:64, 384:512], start=True, stop=True)),
         inc=[(ct, 1)], wait=[(cv, ball_done)])
    ctt = ct.n
    p.op("vector", (lambda e: e.tensor_copy(R[0:1, 0:128], t["p6"][0:1, 0:128])),
         inc=[(cv, 1)], wait=[(ct, ctt)])
    cvt = flush(C, "vector")
    p.op("scalar", (lambda e: e.activation(R[0:1, 128:256], R[0:1, 0:128], AF.Exp)),
         inc=[(cs, 1)], wait=[(cv, cvt)])
    cst = flush(C, "scalar")
    p.op("tensor", (lambda e: e.matmul(
        t["p6"][:, 0:128], t["onesr_f"][:, :], R[0:1, 128:256], start=True, stop=True)),
         wait=[(cs, cst)])
    p.op("tensor", (lambda e: e.matmul(
        t["p6"][:, 128:256], t["onesr_f"][:, :], R[0:1, 0:128], start=True, stop=True)),
         inc=[(ct, 1)])
    ctt = ct.n
    p.op("vector", (lambda e: e.tensor_copy(SP[:, 256:384], t["p6"][:, 0:128])),
         wait=[(ct, ctt)])
    p.op("vector", (lambda e: e.tensor_copy(SP[:, 640:768], t["p6"][:, 128:256])),
         inc=[(cv, 1)])
    p.op("scalar", (lambda e: e.activation(SP[0:64, 512:640], SP[0:64, 384:512], AF.Exp)),
         inc=[(cs, 1)], wait=[(cv, cs_done)])
    barrier(C, "l0a_prep")


def emit_l0attn_scan(C, nch=16):
    p, t, T = C.p, C.t, C.T
    cv, cs, ct, cp = p.sem("cv"), p.sem("cs"), p.sem("ct"), p.sem("cp")
    xbf, ybf = t["xbf"], t["ybf"]
    SCR = xbf
    SCRF = xbf[:, :].bitcast(F32)
    SP = t["scanprep"]
    R = t["rows32a"]
    S_ar = t["F_B"]

    G = [512 * k for k in range(7)]
    H = [3584 + 512 * k for k in range(5)]
    Kq, Vq, KTq, KtTq, Yq, Uq, Sbq, Kngq, EgBq, VTq = (
        6144, 6656, 7168, 7680, 8192, 8704, 9216, 9728, 10240, 11264)
    Poq, Qppq = 10752, 11008
    bf = lambda c0, w=512: SCR[:, c0:c0 + w]
    Ef = SCRF[:, 6144:6656]
    Gtf = SCRF[:, 6656:7168]
    edf = SCRF[:, 7168:7176]

    p0, p1, p2, p3, p4, p5 = t["p0"], t["p1"], t["p2"], t["p3"], t["p4"], t["p5"]
    p6b = t["p6"][:, :].bitcast(BF16)
    p7b = t["p7"][:, :].bitcast(BF16)

    p.op("vector", (lambda e: e.memset(S_ar[:, 0:512], 0.0)), inc=[(cv, 1)])
    s_upd = {h: cv.n for h in range(4)}

    H5R = 11776

    prev_pe_end = 0
    prev_gbuild = 0
    prev_ets = 0
    prev_expg = 0
    prev_acts = 0
    KCV = t["x_res"][:, :].bitcast(BF16)
    for j in range(2 * nch):
        par, jj = j % 2, j // 2
        for q in range(1):
            heads = [0, 1, 2, 3]
            ncol = [768 + 64 * par + 4 * jj + hl for hl in heads]
            w0 = []
            if prev_pe_end:
                w0.append((ct, prev_pe_end))
            first = True
            for idx, h in enumerate(heads):
                for nh in range(2):
                    koff = 2048 * (4 * nh + h) + 64 * j
                    voff = 16384 + 2048 * (4 * nh + h) + 64 * j
                    p.op("vector", (lambda e, idx=idx, nh=nh, koff=koff: e.tensor_copy(
                        bf(Kq)[:, 128 * idx + nh:128 * (idx + 1):2], KCV[:, koff:koff + 64])),
                         wait=w0 if first else ())
                    first = False
                    p.op("vector", (lambda e, idx=idx, nh=nh, voff=voff: e.tensor_copy(
                        bf(Vq)[:, 128 * idx + nh:128 * (idx + 1):2], KCV[:, voff:voff + 64])))
                p.op("vector", (lambda e, idx=idx, h=h: e.tensor_copy(
                    bf(Sbq)[:, 128 * idx:128 * (idx + 1)], S_ar[:, 128 * h:128 * (h + 1)])),
                     inc=[(cv, 1)] if idx == 3 else ())
            t0 = cv.n
            w1 = [(cv, t0)]
            if prev_gbuild:
                w1.append((cv, prev_gbuild))
            if prev_ets:
                w1.append((cv, prev_ets))
            if prev_expg:
                w1.append((cs, prev_expg))
            if prev_acts:
                w1.append((cs, prev_acts))
            for idx in range(4):
                p.op("tensor", (lambda e, idx=idx: e.matmul(
                    p0[:, 128 * idx:128 * (idx + 1)], bf(Kq)[:, 128 * idx:128 * (idx + 1)],
                    bf(Kq)[:, 128 * idx:128 * (idx + 1)], start=True, stop=True)),
                     wait=w1 if idx == 0 else ())
            for idx in range(4):
                p.op("tensor", (lambda e, idx=idx, n=ncol[idx]: e.matmul(
                    p4[0:1, 128 * idx:128 * (idx + 1)], SP[:, n:n + 1], t["ident_f"][:, :],
                    start=True, stop=True)))
            for idx in range(4):
                p.op("tensor", (lambda e, idx=idx: e.transpose(
                    p6b[:, 128 * idx:128 * (idx + 1)], bf(Kq)[:, 128 * idx:128 * (idx + 1)],
                    t["ident_bf"][:, :])))
            for idx in range(4):
                p.op("tensor", (lambda e, idx=idx: e.transpose(
                    p7b[:, 128 * idx:128 * (idx + 1)], bf(Vq)[:, 128 * idx:128 * (idx + 1)],
                    t["ident_bf"][:, :])), inc=[(ct, 1)] if idx == 3 else ())
            t1 = ct.n
            p.op("vector", (lambda e: e.tensor_copy(R[0:1, 256:768], p4[0:1, :])),
                 inc=[(cv, 1)], wait=[(ct, t1)])
            t2v = flush(C, "vector")
            p.op("scalar", (lambda e: e.activation(bf(KTq)[:, :], p6b[:, 0:512], AF.Copy)),
                 wait=[(ct, t1)])
            p.op("scalar", (lambda e: e.activation(bf(VTq)[:, :], p7b[:, 0:512], AF.Copy)),
                 inc=[(cs, 1)])
            t2s = cs.n
            for idx in range(4):
                p.op("tensor", (lambda e, idx=idx: e.matmul(
                    p4[:, 128 * idx:128 * (idx + 1)], t["onesr_f"][:, :],
                    R[0:1, 256 + 128 * idx:256 + 128 * (idx + 1)], start=True, stop=True)),
                     inc=[(ct, 1)] if idx == 3 else (), wait=[(cv, t2v)] if idx == 0 else ())
            t3 = ct.n
            for idx in range(4):
                p.op("vector", (lambda e, idx=idx, n=ncol[idx]: e.tensor_scalar(
                    Ef[:, 128 * idx:128 * (idx + 1)], p4[:, 128 * idx:128 * (idx + 1)],
                    SP[:, n:n + 1], 0.0, op0=ALU.subtract, op1=ALU.min)),
                     inc=[(cv, 1)] if idx == 3 else (), wait=[(ct, t3)] if idx == 0 else ())
            t4v = cv.n
            p.op("scalar", (lambda e: e.activation(Ef[:, :], Ef[:, :], AF.Exp)),
                 wait=[(cv, t4v)])
            p.op("scalar", (lambda e: e.activation(bf(EgBq)[:, :], p4[:, :], AF.Exp)),
                 inc=[(cs, 1)])
            t5s = cs.n
            prev_ets = t4v
            prev_expg = t5s
            p.op("vector", (lambda e: e.tensor_mul(Gtf[:, :], Ef[:, :], p0[:, :])),
                 wait=[(cs, t5s)])
            for idx in range(4):
                p.op("vector", (lambda e, idx=idx, n=ncol[idx]: e.tensor_scalar(
                    Gtf[:, 128 * idx:128 * (idx + 1)], Gtf[:, 128 * idx:128 * (idx + 1)],
                    SP[:, 128 + n:129 + n], None, op0=ALU.mult)))
            p.op("vector", (lambda e: e.tensor_mul(bf(G[0])[:, :], Gtf[:, :], t["m1c4"][:, :])))
            p.op("vector", (lambda e: e.scalar_tensor_tensor(
                bf(Kngq)[:, :], bf(Kq)[:, :], -1.0, bf(EgBq)[:, :], op0=ALU.mult, op1=ALU.mult)),
                 inc=[(cv, 1)])
            t6v = cv.n
            prev_gbuild = t6v
            for idx in range(4):
                p.op("tensor", (lambda e, idx=idx: e.transpose(
                    p6b[:, 128 * idx:128 * (idx + 1)], bf(G[0])[:, 128 * idx:128 * (idx + 1)],
                    t["ident_bf"][:, :])), wait=[(cv, t6v), (cs, t2s)] if idx == 0 else ())
            for idx in range(4):
                p.op("tensor", (lambda e, idx=idx: e.matmul(
                    p1[:, 128 * idx:128 * (idx + 1)], bf(Kngq)[:, 128 * idx:128 * (idx + 1)],
                    bf(Sbq)[:, 128 * idx:128 * (idx + 1)], start=True, stop=True)),
                     inc=[(ct, 1)] if idx == 3 else ())
            t7 = ct.n
            p.op("scalar", (lambda e: e.activation(bf(H[0])[:, :], p6b[:, 0:512], AF.Copy)),
                 inc=[(cs, 1)], wait=[(ct, t7)])
            t8s = cs.n
            prev_acts = t8s
            p.op("vector", (lambda e: e.tensor_add(bf(Yq)[:, :], p1[:, :], bf(VTq)[:, :])),
                 inc=[(cv, 1)], wait=[(ct, t7)])
            t8v = cv.n
            for idx in range(4):
                p.op("vector", (lambda e, idx=idx, n=ncol[idx]: e.tensor_sub(
                    edf[:, idx:idx + 1], SP[:, 640 + n:641 + n], SP[:, n:n + 1])),
                     inc=[(cv, 1)] if idx == 3 else ())
            edt = flush(C, "vector")
            p.op("scalar", (lambda e: e.activation(edf[:, 0:4], edf[:, 0:4], AF.Exp)),
                 inc=[(cs, 1)], wait=[(cv, edt)])
            edts = flush(C, "scalar")
            for idx in range(4):
                p.op("vector", (lambda e, idx=idx: e.tensor_scalar(
                    bf(KtTq)[:, 128 * idx:128 * (idx + 1)], bf(KTq)[:, 128 * idx:128 * (idx + 1)],
                    edf[:, idx:idx + 1], None, op0=ALU.mult)),
                     inc=[(cv, 1)] if idx == 3 else (), wait=[(cs, edts)] if idx == 0 else ())
            ktt_done = cv.n
            seq = [(G[1], H[0], G[0]), (H[1], G[0], H[0]),
                   (G[2], H[1], G[1]), (H[2], G[1], H[1]),
                   (G[3], H[2], G[2]), (H[3], G[2], H[2]),
                   (G[4], H[3], G[3]), (H[4], G[3], H[3]),
                   (G[5], H[4], G[4]), (H5R, G[4], H[4]),
                   (G[6], H5R, G[5])]
            ev = {G[0]: (cv, t6v), H[0]: (cs, t8s)}
            bank_ev = {}
            banks2 = [p1, p5]
            rr = 0
            for dst, lh, rh in seq:
                pb = banks2[rr % 2]
                wv = [ev[lh], ev[rh]]
                if rr == 0:
                    wv.append((cv, t8v))
                if rr >= 2:
                    wv.append(bank_ev[rr - 2])
                for idx in range(4):
                    p.op("tensor", (lambda e, idx=idx, lh=lh, rh=rh, pb=pb: e.matmul(
                        pb[:, 128 * idx:128 * (idx + 1)], bf(lh)[:, 128 * idx:128 * (idx + 1)],
                        bf(rh)[:, 128 * idx:128 * (idx + 1)], start=True, stop=True)),
                         inc=[(ct, 1)] if idx == 3 else (), wait=wv if idx == 0 else ())
                mmt = ct.n
                if rr % 2 == 0:
                    p.op("vector", (lambda e, dst=dst, pb=pb: e.tensor_copy(bf(dst)[:, :], pb[:, :])),
                         inc=[(cv, 1)], wait=[(ct, mmt)])
                    ev[dst] = (cv, cv.n)
                else:
                    p.op("scalar", (lambda e, dst=dst, pb=pb: e.activation(
                        bf(dst)[:, :], pb[:, :], AF.Copy)), inc=[(cs, 1)], wait=[(ct, mmt)])
                    ev[dst] = (cs, cs.n)
                bank_ev[rr] = ev[dst]
                rr += 1
            ylast = (cv, t8v)
            for k in range(7):
                pb = banks2[rr % 2]
                wv = [ev[G[k]], ylast]
                if rr >= 2:
                    wv.append(bank_ev[rr - 2])
                for idx in range(4):
                    p.op("tensor", (lambda e, idx=idx, k=k, pb=pb: e.matmul(
                        pb[:, 128 * idx:128 * (idx + 1)], bf(G[k])[:, 128 * idx:128 * (idx + 1)],
                        bf(Yq)[:, 128 * idx:128 * (idx + 1)], start=True, stop=False)),
                         wait=wv if idx == 0 else ())
                    p.op("tensor", (lambda e, idx=idx, pb=pb: e.matmul(
                        pb[:, 128 * idx:128 * (idx + 1)], t["ident_bf"][:, :],
                        bf(Yq)[:, 128 * idx:128 * (idx + 1)], start=False, stop=True)),
                         inc=[(ct, 1)] if idx == 3 else ())
                mmt = ct.n
                if rr % 2 == 0:
                    p.op("vector", (lambda e, pb=pb: e.tensor_copy(bf(Yq)[:, :], pb[:, :])),
                         inc=[(cv, 1)], wait=[(ct, mmt)])
                    ylast = (cv, cv.n)
                else:
                    p.op("scalar", (lambda e, pb=pb: e.activation(bf(Yq)[:, :], pb[:, :], AF.Copy)),
                         inc=[(cs, 1)], wait=[(ct, mmt)])
                    ylast = (cs, cs.n)
                bank_ev[rr] = ylast
                rr += 1
            for idx in range(4):
                p.op("vector", (lambda e, idx=idx, n=ncol[idx]: e.tensor_scalar(
                    bf(Uq)[:, 128 * idx:128 * (idx + 1)], bf(Yq)[:, 128 * idx:128 * (idx + 1)],
                    SP[:, 128 + n:129 + n], None, op0=ALU.mult)),
                     inc=[(cv, 1)] if idx == 3 else (),
                     wait=[ylast] if (idx == 0 and ylast[0] is cs) else ())
            u_done = cv.n
            wpq = [bank_ev[rr - 2], bank_ev[rr - 1]]
            for idx in range(4):
                p.op("tensor", (lambda e, idx=idx, hh=heads[idx], j=j: e.matmul(
                    p2[:, 64 * idx:64 * (idx + 1)], bf(Kq)[:, 128 * idx:128 * (idx + 1)],
                    ybf[:, 2048 * hh + 64 * j:2048 * hh + 64 * j + 64],
                    start=True, stop=True)))
            for idx in range(4):
                p.op("tensor", (lambda e, idx=idx, n=ncol[idx]: e.matmul(
                    p2[0:1, 256 + 64 * idx:256 + 64 * (idx + 1)],
                    SP[0:64, 512 + n:513 + n], t["ident_f"][0:64, 0:64],
                    start=True, stop=True)), inc=[(ct, 1)] if idx == 3 else ())
            tpq = ct.n
            p.op("vector", (lambda e: e.tensor_copy(R[0:1, 768:1024], p2[0:1, 256:512])),
                 inc=[(cv, 1)], wait=[(ct, tpq)])
            tlrow = flush(C, "vector")
            for idx in range(4):
                p.op("tensor", (lambda e, idx=idx: e.matmul(
                    p2[:, 256 + 64 * idx:256 + 64 * (idx + 1)], t["onesr_f"][:, :],
                    R[0:1, 768 + 64 * idx:768 + 64 * (idx + 1)], start=True, stop=True)),
                     inc=[(ct, 1)] if idx == 3 else (), wait=[(cv, tlrow)] if idx == 0 else ())
            tlb = ct.n
            for idx in range(4):
                p.op("vector", (lambda e, idx=idx: e.tensor_mul(
                    Gtf[:, 128 * idx:128 * idx + 64], p2[:, 64 * idx:64 * (idx + 1)],
                    Ef[:, 128 * idx + 1:128 * (idx + 1):2])), wait=[(ct, tlb)] if idx == 0 else ())
            for idx in range(4):
                p.op("vector", (lambda e, idx=idx: e.tensor_mul(
                    bf(Poq)[:, 64 * idx:64 * (idx + 1)], Gtf[:, 128 * idx:128 * idx + 64],
                    t["m2c"][:, :])))
            for idx in range(4):
                p.op("vector", (lambda e, idx=idx, hh=heads[idx], j=j: e.tensor_mul(
                    bf(Qppq)[:, 64 * idx:64 * (idx + 1)],
                    ybf[:, 2048 * hh + 64 * j:2048 * hh + 64 * j + 64],
                    p2[:, 256 + 64 * idx:256 + 64 * (idx + 1)])),
                     inc=[(cv, 1)] if idx == 3 else ())
            tpo = cv.n
            for idx in range(4):
                p.op("tensor", (lambda e, idx=idx: e.matmul(
                    p3[:, 64 * idx:64 * (idx + 1)], bf(Uq)[:, 128 * idx:128 * (idx + 1)],
                    bf(Poq)[:, 64 * idx:64 * (idx + 1)], start=True, stop=False)),
                     wait=[(cv, tpo), (cv, u_done)] if idx == 0 else ())
                p.op("tensor", (lambda e, idx=idx: e.matmul(
                    p3[:, 64 * idx:64 * (idx + 1)], bf(Sbq)[:, 128 * idx:128 * (idx + 1)],
                    bf(Qppq)[:, 64 * idx:64 * (idx + 1)], start=False, stop=True)))
            for idx in range(4):
                p.op("tensor", (lambda e, idx=idx: e.matmul(
                    p5[:, 128 * idx:128 * (idx + 1)], bf(KtTq)[:, 128 * idx:128 * (idx + 1)],
                    bf(Uq)[:, 128 * idx:128 * (idx + 1)], start=True, stop=True)),
                     inc=[(ct, 1)] if idx == 3 else (),
                     wait=[(cv, ktt_done), bank_ev[rr - 2]] if idx == 0 else ())
            tos = ct.n
            prev_pe_end = tos
            for idx, h in enumerate(heads):
                p.op("vector", (lambda e, idx=idx, h=h, j=j: e.tensor_copy(
                    ybf[:, 8192 + 2048 * h + 64 * j:8192 + 2048 * h + 64 * j + 64],
                    p3[:, 64 * idx:64 * (idx + 1)])), wait=[(ct, tos)] if idx == 0 else ())
            for idx, h in enumerate(heads):
                p.op("vector", (lambda e, idx=idx, h=h, n=ncol[idx]: e.tensor_scalar(
                    S_ar[:, 128 * h:128 * (h + 1)], S_ar[:, 128 * h:128 * (h + 1)],
                    SP[:, 256 + n:257 + n], None, op0=ALU.mult)))
                p.op("vector", (lambda e, idx=idx, h=h: e.tensor_add(
                    S_ar[:, 128 * h:128 * (h + 1)], S_ar[:, 128 * h:128 * (h + 1)],
                    p5[:, 128 * idx:128 * (idx + 1)])), inc=[(cv, 1)])
                s_upd[h] = cv.n
            if getattr(C, "dbg_scan", None):
                C.dbg_scan(j, q)

    sfinal = flush(C, "vector")
    barrier(C, "l0a_scan")


def emit_l0attn_hx(C, sh_img, sh_ptab, sh_flag, nch=16):
    p, t, T = C.p, C.t, C.T
    KCV = t["x_res"][:, :].bitcast(BF16)
    ybf = t["ybf"]
    SP = t["scanprep"]
    exH = p.sem("exchH", "hw")
    exdH = p.sem("exdH")
    cv = p.sem("cv")
    TBL = [(0, 128), (128, 128), (256, 128), (384, 64), (512, 64), (640, 128)]
    stg = t["rows32a"]
    for d in (0, 1):
        for ti, (bt, Rr) in enumerate(TBL):
            for pp in (0, 1):
                for hl in range(4):
                    p.op("vector", (lambda e, d=d, bt=bt, Rr=Rr, ti=ti, pp=pp, hl=hl:
                         e.tensor_copy(
                             stg[0:Rr, 384 * d + 32 * (2 * ti + pp) + hl:
                                 384 * d + 32 * (2 * ti + pp) + hl + 29:4],
                             SP[0:Rr, bt + 64 * pp + 4 * d + hl:
                                bt + 64 * pp + 4 * d + hl + 57:8])))
    p.op("vector", (lambda e: e.nop()), inc=[(cv, 1)])
    prep_t = [(p.sem(nm), p.sem(nm).n) for nm in ("cv", "cs", "ct", "cp")]

    def exchange(e):
        for s_, tgt in prep_t:
            e.wait_ge(p.real[s_.name], tgt)
        with e.register("cntH") as cnt, e.register("rroleH") as rrole, \
             e.register("rflagH") as rflag, e.register("rcondH") as rcond:
            e.reg_mov(cnt, 0)
            e.reg_load(rrole, t["sroleB"][0:1, 0:1])
            for r in (0, 1):
                with e.If_eq(rrole, r):
                    ndma = 0
                    for d in (0, 1):
                        for hl in range(4):
                            h = 4 * d + hl
                            for nh in range(2):
                                e.dma_start(
                                    out=sh_img[d][:, 2048 * (4 * nh + hl) + 1024 * r:
                                                  2048 * (4 * nh + hl) + 1024 * (r + 1)],
                                    in_=KCV[:, 1024 * (8 * nh + h):1024 * (8 * nh + h) + 1024]
                                ).then_inc(p.real["exchH"], 16)
                                e.dma_start(
                                    out=sh_img[d][:, 16384 + 2048 * (4 * nh + hl) + 1024 * r:
                                                  16384 + 2048 * (4 * nh + hl) + 1024 * (r + 1)],
                                    in_=KCV[:, 16384 + 1024 * (8 * nh + h):
                                            16384 + 1024 * (8 * nh + h) + 1024]
                                ).then_inc(p.real["exchH"], 16)
                                ndma += 2
                            e.dma_start(
                                out=sh_img[d][:, 32768 + 2048 * hl + 1024 * r:
                                              32768 + 2048 * hl + 1024 * (r + 1)],
                                in_=ybf[:, 1024 * h:1024 * h + 1024]
                            ).then_inc(p.real["exchH"], 16)
                            ndma += 1
                        for ti, (bt, R) in enumerate(TBL):
                            for pp in (0, 1):
                                e.dma_start(
                                    out=sh_ptab[d][0:R, bt + 64 * pp + 32 * r:
                                                   bt + 64 * pp + 32 * r + 32],
                                    in_=stg[0:R, 384 * d + 32 * (2 * ti + pp):
                                            384 * d + 32 * (2 * ti + pp) + 32]
                                ).then_inc(p.real["exchH"], 16)
                                ndma += 1
                    e.reg_add(cnt, cnt, 16 * ndma)
                    e.wait_ge(p.real["exchH"], cnt)
                    e.dma_start(out=sh_flag[1 + r:2 + r, 0:1],
                                in_=t["sroleB"][0:1, 1:2]).then_inc(p.real["exchH"], 16)
                    e.reg_add(cnt, cnt, 16)
                    e.wait_ge(p.real["exchH"], cnt)

                    def cond():
                        e.dma_start(out=t["spollB"][:, :],
                                    in_=sh_flag[2 - r:3 - r, :]).then_inc(p.real["exchH"], 16)
                        e.reg_add(cnt, cnt, 16)
                        e.wait_ge(p.real["exchH"], cnt)
                        e.reg_load(rflag, t["spollB"][0:1, 0:1])
                        e.reg_alu(rcond, rflag, 33, mybir.AluOpType.not_equal)
                        return rcond
                    with e.While(cond):
                        e.nop()
                    e.dma_start(out=KCV[:, 0:16384],
                                in_=sh_img[r][:, 0:16384]).then_inc(p.real["exchH"], 16)
                    e.dma_start(out=KCV[:, 16384:32768],
                                in_=sh_img[r][:, 16384:32768]).then_inc(p.real["exchH"], 16)
                    e.dma_start(out=ybf[:, 0:8192],
                                in_=sh_img[r][:, 32768:40960]).then_inc(p.real["exchH"], 16)
                    nrb = 3
                    for bt, R in TBL:
                        e.dma_start(out=SP[0:R, 768 + bt:768 + bt + 128],
                                    in_=sh_ptab[r][0:R, bt:bt + 128]).then_inc(p.real["exchH"], 16)
                        nrb += 1
                    e.reg_add(cnt, cnt, 16 * nrb)
                    e.wait_ge(p.real["exchH"], cnt)
        e.nop().then_inc(p.real["exdH"], 1)

    p.raw("sync", exchange, inc=[(exdH, 1)])
    for eng in ("vector", "scalar", "tensor", "gpsimd"):
        p.op(eng, (lambda e: e.nop()), wait=[(exdH, exdH.n)])
    barrier(C, "l0a_hx")


def emit_l0attn_ox(C, sh_o, sh_flag, nch=16):
    p, t, T = C.p, C.t, C.T
    ybf = t["ybf"]
    exO = p.sem("exchO", "hw")
    exdO = p.sem("exdO")
    o_t = [(p.sem(nm), p.sem(nm).n) for nm in ("cv", "cs", "ct", "cp")]

    def exchange(e):
        for s_, tgt in o_t:
            e.wait_ge(p.real[s_.name], tgt)
        with e.register("cntO") as cnt, e.register("rroleO") as rrole, \
             e.register("rflagO") as rflag, e.register("rcondO") as rcond:
            e.reg_mov(cnt, 0)
            e.reg_load(rrole, t["sroleB"][0:1, 0:1])
            for r in (0, 1):
                with e.If_eq(rrole, r):
                    for d in (0, 1):
                        for hl in range(4):
                            h = 4 * r + hl
                            e.dma_start(
                                out=sh_o[d][:, 1024 * h:1024 * (h + 1)],
                                in_=ybf[:, 8192 + 2048 * hl + 1024 * d:
                                        8192 + 2048 * hl + 1024 * (d + 1)]
                            ).then_inc(p.real["exchO"], 16)
                    e.reg_add(cnt, cnt, 16 * 8)
                    e.wait_ge(p.real["exchO"], cnt)
                    e.dma_start(out=sh_flag[3 + r:4 + r, 0:1],
                                in_=t["sroleB"][0:1, 1:2]).then_inc(p.real["exchO"], 16)
                    e.reg_add(cnt, cnt, 16)
                    e.wait_ge(p.real["exchO"], cnt)

                    def cond():
                        e.dma_start(out=t["spollB"][:, :],
                                    in_=sh_flag[4 - r:5 - r, :]).then_inc(p.real["exchO"], 16)
                        e.reg_add(cnt, cnt, 16)
                        e.wait_ge(p.real["exchO"], cnt)
                        e.reg_load(rflag, t["spollB"][0:1, 0:1])
                        e.reg_alu(rcond, rflag, 33, mybir.AluOpType.not_equal)
                        return rcond
                    with e.While(cond):
                        e.nop()
                    e.dma_start(out=ybf[:, 8192:16384],
                                in_=sh_o[r][:, :]).then_inc(p.real["exchO"], 16)
                    e.reg_add(cnt, cnt, 16)
                    e.wait_ge(p.real["exchO"], cnt)
        e.nop().then_inc(p.real["exdO"], 1)

    p.raw("sync", exchange, inc=[(exdO, 1)])
    for eng in ("vector", "scalar", "tensor", "gpsimd"):
        p.op(eng, (lambda e: e.nop()), wait=[(exdO, exdO.n)])
    barrier(C, "l0a_ox")


def emit_l0attn_post(C, Wos, gg_work, x_work, nch=16):
    p, t, T = C.p, C.t, C.T
    cv, cs, ct, cp = p.sem("cv"), p.sem("cs"), p.sem("ct"), p.sem("cp")
    ybf = t["ybf"]
    KCV = t["x_res"][:, :].bitcast(BF16)
    R = t["rows32a"]
    oar = lambda h: ybf[:, 8192 + 1024 * h:8192 + 1024 * (h + 1)]

    prev_gg = 0
    for h in range(8):
        ld = p.dma("gpsimd", t["ggtmp"][:, :], gg_work[h],
                   wait=[(cv, prev_gg)] if h > 0 else ())
        p.op("vector", (lambda e, h=h: e.tensor_mul(t["sqA"][:, :], oar(h)[:, :], oar(h)[:, :])),
             inc=[(cv, 1)])
        cvt = cv.n
        for g in range(2):
            p.op("tensor", (lambda e, g=g: e.matmul(
                [t["p6"], t["p7"]][g][0:1, :], t["ones128_bf"][:, 0:1],
                t["sqA"][:, 512 * g:512 * (g + 1)], start=True, stop=True)),
                 inc=[(ct, 1)] if g == 1 else (), wait=[(cv, cvt)] if g == 0 else ())
        ctt = ct.n
        p.op("scalar", (lambda e: e.activation(R[0:1, 0:512], t["p6"][0:1, :], AF.Sqrt,
                                               scale=1.0 / 128.0, bias=t["constcol"][0:1, 0:1])),
             wait=[(ct, ctt)])
        p.op("scalar", (lambda e: e.activation(R[0:1, 512:1024], t["p7"][0:1, :], AF.Sqrt,
                                               scale=1.0 / 128.0, bias=t["constcol"][0:1, 0:1])),
             inc=[(cs, 1)])
        cst = cs.n
        p.op("vector", (lambda e: e.reciprocal(R[0:1, :], R[0:1, :])),
             inc=[(cv, 1)], wait=[(cs, cst)])
        cvt = flush(C, "vector")
        for g in range(2):
            p.op("tensor", (lambda e, g=g: e.matmul(
                [t["p6"], t["p7"]][g][:, :], t["onesr_f"][:, :], R[0:1, 512 * g:512 * (g + 1)],
                start=True, stop=True)), inc=[(ct, 1)] if g == 1 else (),
                 wait=[(cv, cvt)] if g == 0 else ())
        ctt = ct.n
        for g in range(2):
            p.op("vector", (lambda e, h=h, g=g: e.tensor_mul(
                t["sqB"][:, 0:512], oar(h)[:, 512 * g:512 * (g + 1)],
                [t["p6"], t["p7"]][g][:, :])), wait=[(ct, ctt), ld] if g == 0 else [ld])
            p.op("vector", (lambda e, g=g: e.tensor_scalar(
                t["sqB"][:, 0:512], t["sqB"][:, 0:512], t["onw"][:, 0:1], None, op0=ALU.mult)))
            p.op("vector", (lambda e, h=h, g=g: e.tensor_mul(
                oar(h)[:, 512 * g:512 * (g + 1)], t["sqB"][:, 0:512],
                t["ggtmp"][:, 512 * g:512 * (g + 1)])),
                 inc=[(cv, 1)] if g == 1 else ())
        prev_gg = cv.n
    opost_done = flush(C, "vector")

    barrier(C, "l0a_opost")

    pb4 = [t["p0"], t["p1"], t["p2"], t["p3"]]
    mm_done = {}
    ev_done = {}
    for o in range(nch):
        wb = [t["wsA"], t["wsB"]][o % 2]
        w = [(ct, mm_done[o - 2])] if o >= 2 else []
        ldt = p.dma("gpsimd", wb[:, 0:1024], Wos[o], wait=w)
        for g in range(2):
            pb = pb4[(2 * o + g) % 4]
            wmm = [ldt, (cv, opost_done)]
            if (2 * o + g) >= 4:
                wmm.append((cv, ev_done[2 * o + g - 4]))
            for c in range(8):
                p.op("tensor", (lambda e, c=c, o=o, g=g, pb=pb, wb=wb: e.matmul(
                    pb[:, :], wb[:, 128 * c:128 * (c + 1)],
                    ybf[:, 8192 + 1024 * c + 512 * g:8192 + 1024 * c + 512 * (g + 1)],
                    start=(c == 0), stop=(c == 7))),
                     inc=[(ct, 1)] if c == 7 else (), wait=wmm if c == 0 else ())
            mmt = ct.n
            p.op("vector", (lambda e, o=o, g=g, pb=pb: e.tensor_copy(
                KCV[:, 1024 * o + 512 * g:1024 * o + 512 * (g + 1)], pb[:, :])),
                 inc=[(cv, 1)], wait=[(ct, mmt)])
            ev_done[2 * o + g] = cv.n
        mm_done[o] = ct.n

    barrier(C, "l0a_wo")

    stats_ready = emit_ln_stats(C, lambda c: KCV[:, 1024 * c:1024 * (c + 1)], nch,
                                t["m2_B"], t["r2_B"])
    acc_tgt = {}
    sname = p.sem("dma_gpsimd", "sw")
    for c in range(nch):
        for g in range(2):
            i2 = (2 * c + g) % 2
            stg = t["tmp32e"] if i2 == 0 else t["e1"]
            wv = []
            if (2 * c + g) >= 2:
                wv.append((sname, acc_tgt[(2 * c + g) - 2]))
            sl = slice(1024 * c + 512 * g, 1024 * c + 512 * (g + 1))
            slg = slice(512 * g, 512 * (g + 1))
            p.op("vector", (lambda e, sl=sl, slg=slg: e.tensor_sub(
                t["sqA"][:, 0:512], KCV[:, sl], t["m2_B"][:, slg])), wait=wv)
            p.op("vector", (lambda e, slg=slg, stg=stg: e.tensor_mul(
                stg[:, :], t["sqA"][:, 0:512], t["r2_B"][:, slg])),
                 inc=[(cv, 1)])
            cvt = cv.n
            tgt = p.dma("gpsimd", x_work[128 * c:128 * (c + 1), 512 * g:512 * (g + 1)],
                        stg[:, :], wait=[(cv, cvt)], accum=True)
            acc_tgt[(2 * c + g)] = tgt[1]
    p.op("gpsimd", lambda e: e.nop(), wait=[(sname, sname.n)])
    rl = None
    for c in range(nch):
        rl = p.dma("sync", chunk(t["x_res"], c, T), x_work[128 * c:128 * (c + 1), :],
                   wait=[(sname, sname.n)] if c == 0 else ())
    p.op("sync", (lambda e: e.nop()), wait=[rl])
    barrier(C, "l0a_done")



def build_fused(T=1024):
    I32 = mybir.dt.int32
    nc = bass.Bass("TRN2", target_bir_lowering=False, debug=False,
                   detect_race_conditions=False)
    dram = {}

    def di(nm, sh, dt=F32):
        dram[nm] = nc.dram_tensor(nm, sh, dt, kind="ExternalInput").ap()
        return dram[nm]

    x_in = di("x_in", [2048, T])
    Wqs = di("Wqs", [8, 128, 2048]); Wks = di("Wks", [16, 128, 2048])
    Wvs = di("Wvs", [16, 128, 2048]); Wgs = di("Wgs", [8, 128, 2048])
    Wos = di("Wos", [16, 128, 1024])
    wsmall_d = di("wsmall_d", [128, 512]); convw_d = di("convw_d", [128, 160])
    acol_d = di("acol_d", [128, 8]); di("onw_d", [128, 1])
    di("role_d", [1, 2], I32)
    di("cumsL_d", [128, 64]); di("dupB_d", [64, 128])
    di("dupE_d", [128, 128]); di("dupO_d", [128, 128])
    di("sel63_d", [64, 1])
    di("m1c4_d", [128, 512], BF16); di("m2c_d", [128, 64], BF16)
    W0t = di("W0t", [2048, 2048]); di("d_biasc", [128, 16])
    wqs = di("wqs", [16, 128, 2048]); wos = di("wos", [16, 128, 2048])
    di("wkvp", [128, 4096]); di("d_bq", [128, 16]); di("d_bkv", [128, 2])
    di("d_bo", [128, 16]); di("d_role", [1, 2], I32)
    W1s = di("W1s", [64, 128, 2048]); W2s = di("W2s", [16, 2, 128, 4096])
    di("d_ones_bf", [128, 128], BF16); di("d_ones_f", [128, 128])
    di("d_onesr_f", [1, 128]); di("d_id_bf", [128, 128], BF16)
    di("d_id_f", [128, 128])

    x_out = nc.dram_tensor("x_out", [2048, T], F32, kind="ExternalOutput").ap()
    x_work = nc.dram_tensor("x_work", [2048, T], F32, kind="Internal").ap()
    W1b = nc.dram_tensor("W1b", [64, 128, 2048], BF16, kind="Internal").ap()
    W2b = nc.dram_tensor("W2b", [16, 2, 128, 4096], BF16, kind="Internal").ap()
    wqb = nc.dram_tensor("wqb", [16, 128, 2048], BF16, kind="Internal").ap()
    wob = nc.dram_tensor("wob", [16, 128, 2048], BF16, kind="Internal").ap()
    gg_work = nc.dram_tensor("gg_work", [8, 128, 1024], BF16, kind="Internal").ap()
    sh_halo = nc.dram_tensor("sh_halo", [2, 128, 128], BF16, kind="Internal", addr_space="Shared").ap()
    sh_img = nc.dram_tensor("sh_img", [2, 128, 40960], BF16, kind="Internal", addr_space="Shared").ap()
    sh_ptab = nc.dram_tensor("sh_ptab", [2, 128, 768], F32, kind="Internal", addr_space="Shared").ap()
    sh_oimg = nc.dram_tensor("sh_oimg", [2, 128, 8192], BF16, kind="Internal", addr_space="Shared").ap()
    sh_flagB = nc.dram_tensor("sh_flagB", [8, 8], I32, kind="Internal", addr_space="Shared").ap()
    sh_kv = nc.dram_tensor("sh_kv", [2, 256, 1024], BF16, kind="Internal", addr_space="Shared").ap()
    sh_flag = nc.dram_tensor("sh_flag", [2, 8], I32, kind="Internal", addr_space="Shared").ap()

    with ExitStack() as ctx:
        prog = Prog(nc)
        C = Ctx(nc, prog)
        alloc_common(C, ctx, T)
        C.sb(ctx, "_ovl", [128, 9856], F32)
        C.ovl_cap = 9856 * 4
        C.ovl_skip = {"sroleA", "sflagA", "spollA", "sroleB", "sflagB", "spollB"}
        C.ovl_on = True
        C.ovl_cursor = 0; alloc_l0attn(C, ctx)
        ovl_l0a = C.ovl_cursor
        C.ovl_cursor = 0; alloc_l0ffn(C, ctx)
        ovl_l0f = C.ovl_cursor
        C.ovl_cursor = 0; alloc_l1attn(C, ctx)
        ovl_l1a = C.ovl_cursor
        C.ovl_cursor = 0; alloc_l1ffn(C, ctx)
        ovl_l1f = C.ovl_cursor
        C.ovl_on = False
        assert max(ovl_l0a, ovl_l0f, ovl_l1a, ovl_l1f) <= C.ovl_cap
        t = C.t
        p = prog

        def fence(tag):
            waits = [(p.sem(nm), p.sem(nm).n) for nm in ("cv", "cs", "ct", "cp")]
            p.dma("gpsimd", t["tok8"][0:1, 0:8], x_work[0:1, 0:8], wait=waits)
            p.dma("sync", t["tok8"][0:1, 8:16], x_work[0:1, 8:16], wait=waits)

        def loads_done(tag):
            for q in ("gpsimd", "sync"):
                s = p.sem("dma_" + q, "sw" if q == "gpsimd" else "hw")
                p.op(q if q == "gpsimd" else "sync", lambda e: e.nop(), wait=[(s, s.n)])
            barrier(C, tag)

        for c in range(16):
            p.dma("gpsimd", chunk(t["x_res"], c, T), x_in[128 * c:128 * (c + 1), :])
        p.dma("gpsimd", t["wsmallb"][:, :], dram["wsmall_d"][:, 0:256])
        for nm, tile in [("wsmall_d", "wsmall"), ("convw_d", "convw"), ("acol_d", "acol"),
                         ("onw_d", "onw"), ("role_d", "sroleB"), ("d_ones_bf", "ones128_bf"),
                         ("d_ones_f", "ones128_f"), ("d_onesr_f", "onesr_f"),
                         ("d_id_bf", "ident_bf"), ("d_id_f", "ident_f"),
                         ("cumsL_d", "cumsL"), ("dupB_d", "dupB"), ("dupE_d", "dupE"),
                         ("dupO_d", "dupO"), ("sel63_d", "sel63"), ("m1c4_d", "m1c4"),
                         ("m2c_d", "m2c")]:
            p.dma("gpsimd", t[tile][:, :], dram[nm])
        p.op("vector", lambda e: e.memset(t["xbf"][:, :], 0.0))
        p.op("vector", lambda e: e.memset(t["constcol"][:, 0:1], 1e-5))
        p.op("vector", lambda e: e.memset(t["constcol"][:, 1:2], 1.0))
        p.op("vector", lambda e: e.memset(t["sflagB"][:, :], 21))
        loads_done("loads0")

        p.cur_scope = "l0a_pre"
        emit_l0attn_pre(C, Wqs, Wks, Wvs, Wgs, wsmall_d, convw_d, acol_d,
                        x_work, gg_work, sh_halo, sh_flagB)
        p.cur_scope = "wconv"
        for i in range(64):
            p.dma("gpsimd", W1b[i], W1s[i])
        for oc in range(16):
            for kb in range(2):
                p.dma("gpsimd", W2b[oc, kb], W2s[oc, kb])
        for o in range(16):
            p.dma("gpsimd", wqb[o], wqs[o])
            p.dma("gpsimd", wob[o], wos[o])
        p.cur_scope = "l0a_hx"
        emit_l0attn_hx(C, sh_img, sh_ptab, sh_flagB)
        p.cur_scope = "l0a_scan"
        emit_l0attn_scan(C)
        p.cur_scope = "l0a_ox"
        emit_l0attn_ox(C, sh_oimg, sh_flagB)
        p.cur_scope = "l0a_post"
        emit_l0attn_post(C, Wos, gg_work, x_work)
        p.cur_scope = None

        fence("b")
        p.dma("gpsimd", t["bias_cols"][:, :], dram["d_biasc"])
        loads_done("loadsB")
        p.cur_scope = "l0ffn"
        emit_l0ffn(C, lambda c: chunk(t["x_res"], c, T), W0t)
        p.cur_scope = None

        fence("c")
        p.dma("gpsimd", t["wkv"][:, :], dram["wkvp"])
        p.dma("gpsimd", t["biasq"][:, :], dram["d_bq"])
        p.dma("gpsimd", t["biaskv"][:, :], dram["d_bkv"])
        p.dma("gpsimd", t["biaso"][:, :], dram["d_bo"])
        p.dma("gpsimd", t["sroleA"][:, :], dram["d_role"])
        p.op("vector", lambda e: e.memset(t["sflagA"][:, :], 7))
        loads_done("loadsC")
        p.cur_scope = "l1attn"
        emit_l1attn(C, wqb, wob, dram["d_role"], sh_kv, sh_flag, tag=0, dbg=None)
        p.cur_scope = None

        fence("d")
        loads_done("loadsD")
        emit_l1ffn(C, W1b, W2b, x_out)
        p.cur_scope = None

        sfin = p.sem("dma_gpsimd", "sw")
        p.op("gpsimd", lambda e: e.nop(), wait=[(sfin, sfin.n)])
        p.op("sync", lambda e: e.nop())
        prog.emit(ctx)
    return nc


import ml_dtypes
from concourse.bass_utils import run_bass_kernel_spmd

T = 1024


def kernel(**kw):
    f32 = np.float32
    x = np.asarray(kw["L0_attn_x"], f32)
    cores = list(range(8))

    def slab(W, no):
        return np.ascontiguousarray(
            W.reshape(no, 128, 16, 128).transpose(0, 3, 2, 1).reshape(no, 128, 2048))

    Wb = np.asarray(kw["L0_attn_Wb"], f32); Wa = np.asarray(kw["L0_attn_Wa"], f32)
    wsmall = np.zeros((128, 512), f32)
    wsmall[:, 0:256] = Wb.reshape(16, 16, 128).transpose(2, 1, 0).reshape(128, 256)
    wsmall[:, 256:384] = Wa.reshape(8, 16, 128).transpose(2, 1, 0).reshape(128, 128)
    cq = np.asarray(kw["L0_attn_conv_q"], f32); ck = np.asarray(kw["L0_attn_conv_k"], f32)
    cvv = np.asarray(kw["L0_attn_conv_v"], f32)
    convw = np.zeros((128, 160), f32)
    convw[:, 0:32] = cq[:, ::-1].reshape(8, 128, 4).transpose(1, 0, 2).reshape(128, 32)
    convw[:, 32:96] = ck[:, ::-1].reshape(16, 128, 4).transpose(1, 0, 2).reshape(128, 64)
    convw[:, 96:160] = cvv[:, ::-1].reshape(16, 128, 4).transpose(1, 0, 2).reshape(128, 64)
    acol = np.zeros((128, 8), f32)
    acol[0:8, 0] = np.asarray(kw["L0_attn_A_log"], f32)
    acol[0:8, 1] = np.asarray(kw["L0_attn_dt_bias"], f32)
    sidx = np.arange(64)[:, None]; tidx = np.arange(128)[None, :]
    wk = np.asarray(kw["L1_attn_wk"], f32); wv = np.asarray(kw["L1_attn_wv"], f32)
    wks = wk.reshape(16, 128, 128).transpose(1, 0, 2).reshape(128, 2048)
    wvs = wv.reshape(16, 128, 128).transpose(1, 0, 2).reshape(128, 2048)
    wq1 = np.asarray(kw["L1_attn_wq"], f32); wo1 = np.asarray(kw["L1_attn_wo"], f32)
    W1 = np.asarray(kw["L1_ffn_W1"], f32); W2 = np.asarray(kw["L1_ffn_W2"], f32)
    common = {
        "Wqs": slab(np.asarray(kw["L0_attn_Wq"], f32), 8),
        "Wks": slab(np.asarray(kw["L0_attn_Wk"], f32), 16),
        "Wvs": slab(np.asarray(kw["L0_attn_Wv"], f32), 16),
        "Wgs": slab(np.asarray(kw["L0_attn_Wg"], f32), 8),
        "Wos": np.ascontiguousarray(np.asarray(kw["L0_attn_Wo"], f32)
                                    .reshape(16, 128, 8, 128).transpose(0, 3, 2, 1)
                                    .reshape(16, 128, 1024)),
        "wsmall_d": wsmall, "convw_d": convw, "acol_d": acol,
        "onw_d": np.ascontiguousarray(np.asarray(kw["L0_attn_o_norm_w"], f32).reshape(128, 1)),
        "cumsL_d": np.tile((sidx <= np.arange(64)[None, :]).astype(f32), (2, 1)),
        "dupB_d": (tidx // 2 == sidx).astype(f32),
        "dupE_d": np.tile((tidx == 2 * sidx).astype(f32), (2, 1)),
        "dupO_d": np.tile((tidx == 2 * sidx + 1).astype(f32), (2, 1)),
        "sel63_d": (sidx == 63).astype(f32),
        "m1c4_d": np.tile(-np.triu(np.ones((128, 128), f32), 1), (1, 4)).astype(ml_dtypes.bfloat16),
        "m2c_d": (np.arange(128)[:, None] <= 2 * np.arange(64)[None, :] + 1).astype(ml_dtypes.bfloat16),
        "W0t": np.ascontiguousarray(np.asarray(kw["L0_ffn_weight"], f32).T),
        "d_biasc": np.ascontiguousarray(np.asarray(kw["L0_ffn_bias"], f32).reshape(16, 128).T),
        "wqs": np.ascontiguousarray(wq1.reshape(16, 128, 16, 128).transpose(2, 1, 0, 3).reshape(16, 128, 2048)),
        "wos": np.ascontiguousarray(wo1.reshape(16, 128, 16, 128).transpose(2, 1, 0, 3).reshape(16, 128, 2048)),
        "wkvp": np.ascontiguousarray(np.concatenate([wks, wvs], axis=1)),
        "d_bq": np.ascontiguousarray(np.asarray(kw["L1_attn_bq"], f32).reshape(16, 128).T),
        "d_bkv": np.ascontiguousarray(np.stack([np.asarray(kw["L1_attn_bk"], f32),
                                                np.asarray(kw["L1_attn_bv"], f32)], 1)),
        "d_bo": np.ascontiguousarray(np.asarray(kw["L1_attn_bo"], f32).reshape(16, 128).T),
        "W1s": np.ascontiguousarray(W1.reshape(64, 128, 16, 128).transpose(0, 3, 2, 1).reshape(64, 128, 2048)),
        "W2s": np.ascontiguousarray(W2.reshape(16, 128, 2, 32, 128).transpose(0, 2, 4, 3, 1).reshape(16, 2, 128, 4096)),
        "d_ones_bf": np.ones((128, 128), ml_dtypes.bfloat16),
        "d_ones_f": np.ones((128, 128), f32),
        "d_onesr_f": np.ones((1, 128), f32),
        "d_id_bf": np.eye(128).astype(ml_dtypes.bfloat16),
        "d_id_f": np.eye(128).astype(f32),
    }

    in_maps = []
    for c in cores:
        m = dict(common)
        m["x_in"] = np.ascontiguousarray(x[c // 2, 1024 * (c % 2):1024 * (c % 2) + 1024, :].T)
        m["role_d"] = np.array([[c % 2, 33]], np.int32)
        m["d_role"] = np.array([[c % 2, 0]], np.int32)
        in_maps.append(m)

    nc = build_fused(T)
    res = run_bass_kernel_spmd(nc, in_maps, core_ids=cores)

    out = np.empty((4, 2048, 2048), f32)
    for c in cores:
        out[c // 2, 1024 * (c % 2):1024 * (c % 2) + 1024, :] = np.asarray(res.results[c]["x_out"]).T
    return out
